# version 12
# baseline (speedup 1.0000x reference)
import numpy as np
import ml_dtypes
import concourse.bass as bass
import concourse.mybir as mybir
from concourse.bass_utils import run_bass_kernel_spmd
from contextlib import ExitStack

F32 = mybir.dt.float32
BF16 = mybir.dt.bfloat16
AF = mybir.ActivationFunctionType
ALU = mybir.AluOpType
AX = mybir.AxisListType
BF16NP = ml_dtypes.bfloat16

B, S, D = 4, 4096, 2048
T = 2048
NCORE = 8
EPS = 1e-5


ENGINES = ("tensor", "vector", "scalar", "gpsimd", "sync")

class _Op:
    __slots__ = ("eng", "stream", "fn", "reads", "writes", "idx", "inc", "value",
                 "needed", "short")
    def __init__(self, eng, stream, fn, reads, writes, idx, inc):
        self.eng, self.stream, self.fn = eng, stream, fn
        self.reads, self.writes = reads, writes
        self.idx, self.inc = idx, inc
        self.value = None
        self.needed = False
        self.short = 0

class Sched:
    def __init__(self, nc):
        self.nc = nc
        self.ops = []
        self.deps = {}
        self.writers = {}
        self.readers = {}
        self.last_barrier = []
        self.shapes = {}
        self.scalar_spacer = 0
        self.paranoid = False
        self.pads = {}
        self.dma_streams = {}
        self.dma_free = []
        self.n_dma_streams = 0

    def _track(self, op):
        d = set()
        for t in op.reads:
            for w in self.writers.get(id(t), self.last_barrier):
                d.add(w)
            self.readers.setdefault(id(t), []).append(op)
        for t in op.writes:
            for w in self.writers.get(id(t), self.last_barrier):
                d.add(w)
            for r in self.readers.get(id(t), ()):
                d.add(r)
            self.writers[id(t)] = [op]
            self.readers[id(t)] = []
        d.discard(op)
        self.deps[op] = d

    def op(self, eng, fn, reads=(), writes=(), kind="compute", short=0):
        if kind == "compute":
            stream, inc = eng, 1
        elif kind == "dma":
            key = id(writes[0]) if writes else "anon"
            stream = self.dma_streams.get(key)
            if stream is None:
                stream = (self.dma_free.pop() if self.dma_free
                          else f"dma{self.n_dma_streams}")
                if not self.dma_free and stream == f"dma{self.n_dma_streams}":
                    self.n_dma_streams += 1
                self.dma_streams[key] = stream
            inc = 16
        elif kind == "cc":
            stream, inc = "cc", 1
        else:
            raise ValueError(kind)
        o = _Op(eng, stream, fn, list(reads), list(writes), len(self.ops), inc)
        if short:
            o.short = short
        if kind == "compute" and not o.short:
            if eng == "gpsimd":
                o.short = 800
            elif eng == "scalar" and self.scalar_spacer:
                o.short = self.scalar_spacer
            if self.paranoid and eng in ("vector", "scalar"):
                o.short = 1024
            if not o.short:
                for t in o.writes:
                    shp = self.shapes.get(id(t))
                    if shp is not None and (int(np.prod(shp[1:])) < 128 or shp[0] <= 2):
                        o.short = 1024
                        break
        self.ops.append(o)
        self._track(o)
        return o

    def dma(self, eng, out_ap, in_ap, reads, writes, **kw):
        return self.op(eng, lambda e, o=out_ap, i=in_ap: e.dma_start(out=o, in_=i, **kw),
                       reads=reads, writes=writes, kind="dma")

    def barrier(self):
        used = sorted({o.eng for o in self.ops}) or ["sync"]
        dep_all = set()
        for tid in self.writers:
            dep_all.update(self.writers[tid])
        for tid in self.readers:
            dep_all.update(self.readers[tid])
        barws = []
        for eng in used:
            pad = self.pads.get(eng)
            if pad is not None:
                fn = (lambda e, p=pad: e.mul(p[:, :], p[:, :], 1.0)) if eng == "scalar" \
                    else (lambda e, p=pad: e.tensor_scalar(p[:, :], p[:, :], 1.0,
                                                           None, ALU.mult))
            else:
                fn = lambda e: e.nop()
            o = self.op(eng, fn, reads=[], writes=[])
            self.deps[o] = set(d for d in dep_all if d is not o)
            barws.append(o)
        for tid in list(self.writers.keys()):
            self.writers[tid] = list(barws)
        for tid in list(self.readers.keys()):
            self.readers[tid] = []
        self.last_barrier = list(barws)
        self.dma_free.extend(self.dma_streams.values())
        self.dma_streams = {}

    def emit(self):
        nc = self.nc
        for o, ds in self.deps.items():
            for d in ds:
                d.needed = True
        streams = sorted({o.stream for o in self.ops})
        last_per_stream = {}
        for o in self.ops:
            last_per_stream[o.stream] = o
        for o in last_per_stream.values():
            o.needed = True
        counts = {s: 0 for s in streams}
        for o in self.ops:
            if o.needed:
                counts[o.stream] += o.inc
                o.value = counts[o.stream]
        finals = dict(counts)
        with ExitStack() as ctx:
            sems = {s: ctx.enter_context(nc.semaphore(f"sem_{s.replace('.', '_')}"))
                    for s in streams}
            block = ctx.enter_context(nc.Block())
            by_eng = {}
            for o in self.ops:
                by_eng.setdefault(o.eng, []).append(o)

            def make_body(eng, ops):
                def body(e):
                    hwm = {}
                    for o in ops:
                        for d in sorted(self.deps[o], key=lambda x: x.idx):
                            if d.stream == eng and d.inc == 1:
                                continue
                            assert d.value is not None
                            if hwm.get(d.stream, 0) >= d.value:
                                continue
                            e.wait_ge(sems[d.stream], d.value)
                            hwm[d.stream] = d.value
                        inst = o.fn(e)
                        pad = self.pads.get(eng)
                        if o.short and pad is not None:
                            if eng == "scalar":
                                inst = e.mul(pad[:, :], pad[:, :], 1.0)
                            else:
                                inst = e.tensor_scalar(pad[:, :], pad[:, :], 1.0,
                                                       None, ALU.mult)
                        if o.needed:
                            inst.then_inc(sems[o.stream], o.inc)
                    for s in streams:
                        if finals[s] > hwm.get(s, 0) and not (s == eng):
                            e.wait_ge(sems[s], finals[s])
                return body
            for eng in ENGINES:
                if eng in by_eng:
                    getattr(block, eng)(make_body(eng, by_eng[eng]))
        return nc



class Build:
    def __init__(self, name):
        self.nc = bass.Bass("TRN2", target_bir_lowering=False)
        self.s = Sched(self.nc)
        self.ctx = ExitStack()
        self.n_sb = 0

    def _uniq(self, name):
        self.n_sb += 1
        return f"{name}_{self.n_sb}"

    def sb(self, name, shape, dt):
        t = self.ctx.enter_context(self.nc.sbuf_tensor(self._uniq(name), list(shape), dt))
        self.s.shapes[id(t)] = list(shape)
        return t

    def ps(self, name, shape, dt=F32):
        t = self.ctx.enter_context(self.nc.psum_tensor(self._uniq(name), list(shape), dt))
        self.s.shapes[id(t)] = list(shape)
        return t

    def din(self, name, shape, dt):
        t = self.nc.dram_tensor(name, list(shape), dt, kind="ExternalInput")
        self.s.shapes[id(t)] = list(shape)
        return t

    def dout(self, name, shape, dt):
        t = self.nc.dram_tensor(name, list(shape), dt, kind="ExternalOutput")
        self.s.shapes[id(t)] = list(shape)
        return t

    def dint(self, name, shape, dt):
        t = self.nc.dram_tensor(name, list(shape), dt)
        self.s.shapes[id(t)] = list(shape)
        return t

    def finish(self):
        self.s.emit()
        self.ctx.close()
        return self.nc

    from contextlib import contextmanager as _cm

    @_cm
    def scope(self):
        outer = self.ctx
        inner = ExitStack()
        self.ctx = inner
        try:
            yield
        finally:
            inner.close()
            self.ctx = outer
            self.s.barrier()

    def load_consts(self):
        self.i128_d = self.din("c_i128", [128, 128], BF16)
        self.onesr_b_d = self.din("c_onesr_b", [1, 128], BF16)
        self.onesc_b_d = self.din("c_onesc_b", [128, 1], BF16)
        self.onesr_f_d = self.din("c_onesr_f", [1, 128], F32)
        self.misc1_d = self.din("c_misc1", [1, 8], F32)
        self.misc128_d = self.din("c_misc128", [128, 4], F32)
        self.i128 = self.sb("i128", [128, 128], BF16)
        self.onesr_b = self.sb("onesr_b", [1, 128], BF16)
        self.onesc_b = self.sb("onesc_b", [128, 1], BF16)
        self.onesr_f = self.sb("onesr_f", [1, 128], F32)
        self.misc1 = self.sb("misc1", [1, 8], F32)
        self.misc128 = self.sb("misc128", [128, 4], F32)
        for eng in ("vector", "scalar", "gpsimd"):
            self.s.pads[eng] = self.sb(f"pad_{eng}", [128, 512], F32)
        s = self.s
        for dst, src in [(self.i128, self.i128_d), (self.onesr_b, self.onesr_b_d),
                         (self.onesc_b, self.onesc_b_d), (self.onesr_f, self.onesr_f_d),
                         (self.misc1, self.misc1_d), (self.misc128, self.misc128_d)]:
            s.dma("sync", dst[:, :], src.ap(), reads=[src], writes=[dst])

    def bcast_row_f32(self, row_ap, psum, dst, row_reads):
        s = self.s
        s.op("tensor", lambda e, r=row_ap: e.matmul(psum[:, :], self.onesr_f[:, :], r,
                                                    start=True, stop=True),
             reads=[self.onesr_f] + row_reads, writes=[psum])
        s.op("vector", lambda e: e.tensor_copy(dst[:, :], psum[:, :]),
             reads=[psum], writes=[dst])

def mkconsts():
    return {
        "c_i128": np.eye(128, dtype=BF16NP),
        "c_onesr_b": np.ones((1, 128), BF16NP),
        "c_onesc_b": np.ones((128, 1), BF16NP),
        "c_onesr_f": np.ones((1, 128), np.float32),
        "c_misc1": np.array([[1.0, EPS, EPS * EPS, 0, 0, 0, 0, 0]], np.float32),
        "c_misc128": np.tile(np.array([[1.0, EPS, EPS * EPS, 0]], np.float32), (128, 1)),
    }

def _rearr_cpT(dram_ap, p=128):
    return dram_ap.rearrange("(c p) t -> p c t", p=p)



def emit_layernorm(b: Build, x_dram, x_ln, gamma_beta=None, rho_mode="single",
                   want_r1=False, tag=""):
    s = b.s
    nst = T // 512
    x_stts = [b.sb(f"lnx{tag}", [128, 16 * 512], F32) for _ in range(2)]
    x_bfs = [b.sb(f"lnxb{tag}", [128, 16 * 512], BF16)] * 2
    xsq_bfs = [b.sb(f"lnxs{tag}", [128, 16 * 512], BF16)] * 2
    _rows = b.sb(f"lnrows{tag}", [1, 6 * 512], F32)
    mu_row = _rows[0:1, 0:512]
    var_row = _rows[0:1, 512:1024]
    rho_row = _rows[0:1, 1024:1536]
    nurho_row = _rows[0:1, 1536:2048]
    tmp_row = _rows[0:1, 2048:2560]
    tmp2_row = _rows[0:1, 2560:3072]
    r1_row = b.sb(f"lnr1{tag}", [1, T], F32) if want_r1 else None
    nu1_row = b.sb(f"lnnu1{tag}", [1, T], F32) if want_r1 else None
    ps_sum = b.ps(f"lnps1{tag}", [1, 512])
    ps_sq = b.ps(f"lnps2{tag}", [1, 512])
    ps_bc1 = b.ps(f"lnbc1{tag}", [128, 512])
    ps_bc2 = b.ps(f"lnbc2{tag}", [128, 512])
    rho_bc = b.sb(f"lnrbc{tag}", [128, 512], F32)
    nur_bc = b.sb(f"lnnbc{tag}", [128, 512], F32)
    u1 = b.sb(f"lnu1{tag}", [128, 512], F32)
    xv = _rearr_cpT(x_dram.ap())
    for stt in range(nst):
        x_stt, x_bf, xsq_bf = x_stts[stt % 2], x_bfs[stt % 2], xsq_bfs[stt % 2]
        sl = slice(stt * 512, stt * 512 + 512)
        s.dma("sync", x_stt[:, :].rearrange("p (c t) -> p c t", t=512),
              xv[:, :, sl], reads=[x_dram], writes=[x_stt])
        s.op("vector", lambda e, x_bf=x_bf, x_stt=x_stt: e.tensor_copy(
            x_bf[:, :], x_stt[:, :]),
             reads=[x_stt], writes=[x_bf])
        s.op("scalar", lambda e, xsq_bf=xsq_bf, x_stt=x_stt: e.activation(
            xsq_bf[:, :], x_stt[:, :], AF.Square),
             reads=[x_stt], writes=[xsq_bf])
        for ch in range(16):
            c0 = ch * 512
            s.op("tensor", lambda e, c0=c0, st=(ch == 0), x_bf=x_bf: e.matmul(
                ps_sum[:, :], b.onesc_b[:, :], x_bf[:, c0:c0 + 512],
                start=st, stop=(c0 == 15 * 512)),
                 reads=[b.onesc_b, x_bf], writes=[ps_sum])
        for ch in range(16):
            c0 = ch * 512
            s.op("tensor", lambda e, c0=c0, st=(ch == 0), xsq_bf=xsq_bf: e.matmul(
                ps_sq[:, :], b.onesc_b[:, :], xsq_bf[:, c0:c0 + 512],
                start=st, stop=(c0 == 15 * 512)),
                 reads=[b.onesc_b, xsq_bf], writes=[ps_sq])
        s.op("vector", lambda e, sl=sl: e.tensor_scalar(mu_row[0:1, :], ps_sum[:, :],
                                                        1.0 / D, None, ALU.mult),
             reads=[ps_sum], writes=[mu_row])
        s.op("scalar", lambda e, sl=sl: e.activation(tmp_row[0:1, :], mu_row[0:1, :],
                                                     AF.Square),
             reads=[mu_row], writes=[tmp_row])
        s.op("vector", lambda e, sl=sl: e.scalar_tensor_tensor(
            var_row[0:1, :], ps_sq[:, :], 1.0 / D, tmp_row[0:1, :],
            ALU.mult, ALU.subtract),
             reads=[ps_sq, tmp_row], writes=[var_row])
        if rho_mode == "double":
            s.op("scalar", lambda e, sl=sl: e.activation(
                tmp_row[0:1, :], var_row[0:1, :], AF.Sqrt,
                bias=b.misc1[0:1, 2:3], scale=1.0 + EPS),
                 reads=[var_row, b.misc1], writes=[tmp_row])
        else:
            s.op("scalar", lambda e, sl=sl: e.activation(
                tmp_row[0:1, :], var_row[0:1, :], AF.Sqrt, bias=b.misc1[0:1, 1:2]),
                 reads=[var_row, b.misc1], writes=[tmp_row])
        s.op("vector", lambda e, sl=sl: e.reciprocal(rho_row[0:1, :], tmp_row[0:1, :]),
             reads=[tmp_row], writes=[rho_row])
        s.op("vector", lambda e, sl=sl: e.tensor_mul(nurho_row[0:1, :],
                                                     mu_row[0:1, :], rho_row[0:1, :]),
             reads=[mu_row, rho_row], writes=[nurho_row])
        if want_r1:
            s.op("scalar", lambda e, sl=sl: e.activation(
                tmp2_row[0:1, :], var_row[0:1, :], AF.Sqrt, bias=b.misc1[0:1, 1:2]),
                 reads=[var_row, b.misc1], writes=[tmp2_row])
            s.op("vector", lambda e, sl=sl: e.reciprocal(r1_row[0:1, sl], tmp2_row[0:1, :]),
                 reads=[tmp2_row], writes=[r1_row])
            s.op("vector", lambda e, sl=sl: e.tensor_mul(nu1_row[0:1, sl],
                                                         mu_row[0:1, :], r1_row[0:1, sl]),
                 reads=[mu_row, r1_row], writes=[nu1_row])
        s.op("tensor", lambda e, sl=sl: e.matmul(ps_bc1[:, :], b.onesr_f[:, :],
                                                 rho_row[0:1, :], start=True, stop=True),
             reads=[b.onesr_f, rho_row], writes=[ps_bc1])
        s.op("vector", lambda e: e.tensor_copy(rho_bc[:, :], ps_bc1[:, :]),
             reads=[ps_bc1], writes=[rho_bc])
        s.op("tensor", lambda e, sl=sl: e.matmul(ps_bc2[:, :], b.onesr_f[:, :],
                                                 nurho_row[0:1, :], start=True, stop=True),
             reads=[b.onesr_f, nurho_row], writes=[ps_bc2])
        s.op("vector", lambda e: e.tensor_copy(nur_bc[:, :], ps_bc2[:, :]),
             reads=[ps_bc2], writes=[nur_bc])
        for ch in range(16):
            c0 = ch * 512
            s.op("vector", lambda e, c0=c0, x_stt=x_stt: e.tensor_mul(
                u1[:, :], x_stt[:, c0:c0 + 512], rho_bc[:, :]),
                 reads=[x_stt, rho_bc], writes=[u1])
            if gamma_beta is None:
                s.op("vector", lambda e, c0=c0, sl=sl, ch=ch: e.tensor_sub(
                    x_ln[ch][:, sl], u1[:, :], nur_bc[:, :]),
                     reads=[u1, nur_bc], writes=[x_ln[ch]])
            else:
                gfm, bfm = gamma_beta
                s.op("vector", lambda e, c0=c0: e.tensor_sub(u1[:, :], u1[:, :],
                                                             nur_bc[:, :]),
                     reads=[u1, nur_bc], writes=[u1])
                s.op("vector", lambda e, sl=sl, ch=ch: e.tensor_scalar(
                    x_ln[ch][:, sl], u1[:, :], gfm[:, ch:ch + 1], bfm[:, ch:ch + 1],
                    ALU.mult, ALU.add),
                     reads=[u1, gfm, bfm], writes=[x_ln[ch]])
    return r1_row, nu1_row



def build_l0a_pre(b=None, io=None):
    standalone = b is None
    if standalone:
        b = Build("l0a_pre")
    s = b.s
    if io is None:
        io = {"xin": b.din("xin", [D, T], F32), "expT": b.din("expT", [D, 8704], F32),
              "gam": b.din("gam", [128, 16], F32), "bet": b.din("bet", [128, 16], F32),
              "kv": b.dout("kv", [T, 2304], BF16), "qf": b.dout("qf", [256, T], BF16),
              "gl": b.dout("gl", [D, T], BF16), "murho": b.dout("murho", [2, T], F32)}
    xin, expT, gam, bet = io["xin"], io["expT"], io["gam"], io["bet"]
    kv, qf, gl, murho = io["kv"], io["qf"], io["gl"], io["murho"]
    if standalone:
        b.load_consts()
    gfm = b.sb("gfm", [128, 16], F32)
    bfm = b.sb("bfm", [128, 16], F32)
    s.dma("sync", gfm[:, :], gam.ap(), reads=[gam], writes=[gfm])
    s.dma("sync", bfm[:, :], bet.ap(), reads=[bet], writes=[bfm])
    x_ln = [b.sb(f"xln{ch}", [128, T], BF16) for ch in range(16)]
    with b.scope():
        r1_row, nu1_row = emit_layernorm(b, xin, x_ln, gamma_beta=(gfm, bfm),
                                         rho_mode="double", want_r1=True)
        s.dma("sync", murho.ap()[0:1, :], r1_row[0:1, :], reads=[r1_row], writes=[murho])
        s.dma("sync", murho.ap()[1:2, :], nu1_row[0:1, :], reads=[nu1_row], writes=[murho])

    expTv = expT.ap()
    with b.scope():
        ebufAs = [b.sb("ebA", [128, 16 * 128], F32) for _ in range(2)]
        ebufAbs = [b.sb("ebAb", [128, 16 * 128], BF16) for _ in range(2)]
        ebufBs = [b.sb("ebB", [128, 16 * 128], F32) for _ in range(2)]
        ebufBbs = [b.sb("ebBb", [128, 16 * 128], BF16) for _ in range(2)]
        psA = [b.ps(f"psA{i}", [128, 512]) for i in range(2)]
        psB = [b.ps(f"psB{i}", [128, 512]) for i in range(2)]
        gsc = b.sb("gsc", [128, 512], F32)
        qsc = b.sb("qsc", [128, 512], BF16)
        glsc = b.sb("glsc", [128, 512], BF16)

        def load_ecol(buf, bufb, col0):
            s.dma("sync", buf[:, :].rearrange("p (k o) -> p k o", o=128),
                  expTv.rearrange("(k p) o -> p k o", p=128)[:, :, col0:col0 + 128],
                  reads=[expT], writes=[buf])
            s.op("vector", lambda e, buf=buf, bufb=bufb: e.tensor_copy(bufb[:, :], buf[:, :]),
                 reads=[buf], writes=[bufb])

        for oc in range(2):
            ebufAb = ebufAbs[oc % 2]
            load_ecol(ebufAs[oc % 2], ebufAb, oc * 128)
            for stt in range(4):
                sl = slice(stt * 512, stt * 512 + 512)
                pa = psA[stt % 2]
                for k in range(16):
                    s.op("tensor", lambda e, k=k, sl=sl, pa=pa, ebufAb=ebufAb: e.matmul(
                        pa[:, :], ebufAb[:, k * 128:(k + 1) * 128], x_ln[k][:, sl],
                        start=(k == 0), stop=(k == 15)),
                         reads=[ebufAb, x_ln[k]], writes=[pa])
                s.op("scalar", lambda e, pa=pa: e.mul(qsc[:, :], pa[:, :], 1.0 / 16.0),
                     reads=[pa], writes=[qsc])
                s.dma("scalar", qf.ap()[oc * 128:(oc + 1) * 128, sl], qsc[:, :],
                      reads=[qsc], writes=[qf])
        for j in range(16):
            ebufAb, ebufBb = ebufAbs[j % 2], ebufBbs[j % 2]
            load_ecol(ebufAs[j % 2], ebufAb, 512 + j * 128)
            load_ecol(ebufBs[j % 2], ebufBb, 4608 + j * 128)
            for stt in range(4):
                sl = slice(stt * 512, stt * 512 + 512)
                pa, pb = psA[stt % 2], psB[stt % 2]
                for k in range(16):
                    s.op("tensor", lambda e, k=k, sl=sl, pa=pa, ebufAb=ebufAb: e.matmul(
                        pa[:, :], ebufAb[:, k * 128:(k + 1) * 128], x_ln[k][:, sl],
                        start=(k == 0), stop=(k == 15)),
                         reads=[ebufAb, x_ln[k]], writes=[pa])
                for k in range(16):
                    s.op("tensor", lambda e, k=k, sl=sl, pb=pb, ebufBb=ebufBb: e.matmul(
                        pb[:, :], ebufBb[:, k * 128:(k + 1) * 128], x_ln[k][:, sl],
                        start=(k == 0), stop=(k == 15)),
                         reads=[ebufBb, x_ln[k]], writes=[pb])
                s.op("scalar", lambda e, pb=pb: e.activation(gsc[:, :], pb[:, :], AF.Gelu),
                     reads=[pb], writes=[gsc], short=500)
                s.op("vector", lambda e, pa=pa: e.tensor_mul(glsc[:, :], pa[:, :], gsc[:, :]),
                     reads=[pa, gsc], writes=[glsc])
                s.dma("scalar", gl.ap()[j * 128:(j + 1) * 128, sl], glsc[:, :],
                      reads=[glsc], writes=[gl])

    with b.scope():
        ebigFs = [b.sb("egF", [128, 16 * 512], F32) for _ in range(2)]
        ebigAbs = [b.sb("egAb", [128, 16 * 512], BF16) for _ in range(2)]
        ebigBbs = [b.sb("egBb", [128, 16 * 512], BF16) for _ in range(2)]
        psA = [b.ps(f"psA{i}", [128, 512]) for i in range(2)]
        psB = [b.ps(f"psB{i}", [128, 512]) for i in range(2)]
        gsc = b.sb("gsc", [128, 512], F32)
        ksc = b.sb("ksc", [128, 256], BF16)
        vsc = b.sb("vsc", [128, 512], BF16)
        psK = b.ps("psK", [128, 256])

        def load_ebig(bufb, col0, width, par):
            ebigF = ebigFs[par]
            s.dma("sync", ebigF[:, 0:16 * width].rearrange("p (k o) -> p k o", o=width),
                  expTv.rearrange("(k p) o -> p k o", p=128)[:, :, col0:col0 + width],
                  reads=[expT], writes=[ebigF])
            s.op("vector", lambda e, w=width, bufb=bufb, ebigF=ebigF: e.tensor_copy(
                bufb[:, 0:16 * w], ebigF[:, 0:16 * w]),
                 reads=[ebigF], writes=[bufb])

        ebigAb = ebigAbs[0]
        load_ebig(ebigAb, 256, 256, 0)
        for tt in range(16):
            t0 = tt * 128
            for k in range(16):
                s.op("tensor", lambda e, k=k, t0=t0, ebigAb=ebigAb: e.matmul(
                    psK[:, :], x_ln[k][:, t0:t0 + 128], ebigAb[:, k * 256:(k + 1) * 256],
                    start=(k == 0), stop=(k == 15)),
                     reads=[x_ln[k], ebigAb], writes=[psK])
            s.op("vector", lambda e: e.tensor_copy(ksc[:, :], psK[:, :]),
                 reads=[psK], writes=[ksc])
            s.dma("scalar", kv.ap()[t0:t0 + 128, 0:256], ksc[:, :], reads=[ksc], writes=[kv])
        for j in range(4):
            ebigAb, ebigBb = ebigAbs[j % 2], ebigBbs[j % 2]
            load_ebig(ebigAb, 2560 + j * 512, 512, j % 2)
            load_ebig(ebigBb, 6656 + j * 512, 512, (j + 1) % 2)
            for tt in range(16):
                t0 = tt * 128
                pa, pb = psA[tt % 2], psB[tt % 2]
                for k in range(16):
                    s.op("tensor", lambda e, k=k, t0=t0, pa=pa, ebigAb=ebigAb: e.matmul(
                        pa[:, :], x_ln[k][:, t0:t0 + 128], ebigAb[:, k * 512:(k + 1) * 512],
                        start=(k == 0), stop=(k == 15)),
                         reads=[x_ln[k], ebigAb], writes=[pa])
                for k in range(16):
                    s.op("tensor", lambda e, k=k, t0=t0, pb=pb, ebigBb=ebigBb: e.matmul(
                        pb[:, :], x_ln[k][:, t0:t0 + 128], ebigBb[:, k * 512:(k + 1) * 512],
                        start=(k == 0), stop=(k == 15)),
                         reads=[x_ln[k], ebigBb], writes=[pb])
                s.op("scalar", lambda e, pb=pb: e.activation(gsc[:, :], pb[:, :], AF.Gelu),
                     reads=[pb], writes=[gsc], short=500)
                s.op("vector", lambda e, pa=pa: e.tensor_mul(vsc[:, :], pa[:, :], gsc[:, :]),
                     reads=[pa, gsc], writes=[vsc])
                s.dma("scalar", kv.ap()[t0:t0 + 128, 256 + j * 512:256 + (j + 1) * 512],
                      vsc[:, :], reads=[vsc], writes=[kv])
    return b.finish() if standalone else None



BANDW = 5120
OFFC = 1024

def band_view(band, qg, kt):
    colbase = (8 * qg - kt) * 128 + OFFC
    return band[:, colbase:colbase + 1024].rearrange(
        "p (j t u) -> p j t u", t=2, u=128)[:, :, 0:1, :]

def ps_view(pa):
    return pa[:, 0:512].rearrange("p (j t u) -> p j t u", t=1, u=128)

def mk_band(half, kind):
    jj = np.arange(BANDW)[None, :]
    ii = np.arange(128)[:, None]
    rel = (ii + OFFC - jj - 128 * half).astype(np.float32)
    M = np.where(rel <= 0, 0.0, -60000.0).astype(np.float32)
    if kind == "alibi":
        R = np.minimum(rel, 0.0).astype(np.float32)
        return R, M
    return M

def nkt_of(qg):
    return 8 * qg + 8


def build_l0a_attn(b=None, io=None):
    standalone = b is None
    if standalone:
        b = Build("l0a_attn")
    s = b.s
    if io is None:
        io = {"qf": b.din("qf", [256, T], BF16), "kvf": b.din("kvf", [S, 2304], BF16),
              "gl": b.din("gl", [D, T], BF16), "projT": b.din("projT", [4096, D], F32),
              "xin": b.din("xin", [D, T], F32), "murho": b.din("murho", [2, T], F32),
              "bandR": b.din("bandR", [128, BANDW], F32),
              "bandM": b.din("bandM", [128, BANDW], F32),
              "posm": b.din("posm", [1, 128], F32), "x1": b.dout("x1", [D, T], F32)}
    qf, kvf, gl, projT, xin = io["qf"], io["kvf"], io["gl"], io["projT"], io["xin"]
    murho, bandR, bandM, posm, x1 = (io["murho"], io["bandR"], io["bandM"],
                                     io["posm"], io["x1"])
    kvrow = io.get("kvrow", lambda kt: kt * 128)
    if standalone:
        b.load_consts()

    band = b.sb("band", [128, BANDW], F32)
    with b.scope():
        mstage = b.sb("mstage", [128, BANDW], F32)
        pm = b.sb("pm", [1, 128], F32)
        sp_col = b.sb("spcol", [128, 1], F32)
        t1 = b.sb("t1", [128, 1], F32)
        t2 = b.sb("t2", [128, 1], F32)
        psb = b.ps("psb", [128, 128])
        s.dma("sync", band[:, :], bandR.ap(), reads=[bandR], writes=[band])
        s.dma("sync", mstage[:, :], bandM.ap(), reads=[bandM], writes=[mstage])
        s.dma("sync", pm[:, :], posm.ap(), reads=[posm], writes=[pm])
        s.op("tensor", lambda e: e.matmul(psb[:, :], b.onesr_f[:, :], pm[0:1, :],
                                          start=True, stop=True),
             reads=[b.onesr_f, pm], writes=[psb])
        s.op("vector", lambda e: e.tensor_copy(t1[:, :], psb[:, 0:1]),
             reads=[psb], writes=[t1])
        s.op("scalar", lambda e: e.activation(t2[:, :], t1[:, :], AF.Abs),
             reads=[t1], writes=[t2])
        s.op("scalar", lambda e: e.activation(t2[:, :], t2[:, :], AF.Exp, scale=-1.0),
             reads=[t2], writes=[t2])
        s.op("scalar", lambda e: e.activation(t2[:, :], t2[:, :], AF.Ln,
                                              bias=b.misc128[:, 0:1]),
             reads=[t2, b.misc128], writes=[t2])
        s.op("vector", lambda e: e.tensor_scalar(sp_col[:, :], t1[:, :], 0.0, None,
                                                 ALU.max),
             reads=[t1], writes=[sp_col])
        s.op("vector", lambda e: e.tensor_add(sp_col[:, :], sp_col[:, :], t2[:, :]),
             reads=[sp_col, t2], writes=[sp_col])
        s.op("vector", lambda e: e.tensor_scalar(band[:, :], band[:, :], sp_col[:, :],
                                                 None, ALU.mult),
             reads=[band, sp_col], writes=[band])
        s.op("vector", lambda e: e.tensor_add(band[:, :], band[:, :], mstage[:, :]),
             reads=[band, mstage], writes=[band])

    k_fm = [b.sb(f"kfm{c}", [128, S], BF16) for c in range(2)]
    q_fm = [b.sb(f"qfm{c}", [128, T], BF16) for c in range(2)]
    r1p1_row = b.sb("r1p1", [1, T], F32)
    nu1_row = b.sb("nu1", [1, T], F32)
    with b.scope():
        kseg = [b.sb(f"kseg{i}", [128, 256], BF16) for i in range(2)]
        pst = b.ps("pst", [128, 128], BF16)
        for c in range(2):
            s.dma("sync", q_fm[c][:, :], qf.ap()[c * 128:(c + 1) * 128, :],
                  reads=[qf], writes=[q_fm[c]])
        s.dma("sync", r1p1_row[0:1, :], murho.ap()[0:1, :], reads=[murho], writes=[r1p1_row])
        s.dma("sync", nu1_row[0:1, :], murho.ap()[1:2, :], reads=[murho], writes=[nu1_row])
        s.op("vector", lambda e: e.tensor_scalar(r1p1_row[0:1, :], r1p1_row[0:1, :],
                                                 1.0, None, ALU.add),
             reads=[r1p1_row], writes=[r1p1_row])
        for kt in range(32):
            sg = kseg[kt % 2]
            s.dma("sync", sg[:, :], kvf.ap()[kvrow(kt):kvrow(kt) + 128, 0:256],
                  reads=[kvf], writes=[sg])
            for c in range(2):
                s.op("tensor", lambda e, sg=sg, c=c: e.transpose(
                    pst[:, :], sg[:, c * 128:(c + 1) * 128], b.i128[:, :]),
                     reads=[sg, b.i128], writes=[pst])
                s.op("vector", lambda e, kt=kt, c=c: e.tensor_copy(
                    k_fm[c][:, kt * 128:(kt + 1) * 128], pst[:, :]),
                     reads=[pst], writes=[k_fm[c]])

    with b.scope():
        probsT = [b.sb(f"pT{i}", [128, 512], BF16) for i in range(32)]
        ps_sc = [b.ps(f"pssc{i}", [128, 512]) for i in range(2)]
        ps_z = b.ps("psz", [1, 512])
        po4 = [b.ps(f"po{i}", [128, 512]) for i in range(4)]
        ps_t = b.ps("pst", [128, 512], BF16)
        sc_f = [b.sb(f"scf{i}", [128, 512], F32) for i in range(2)]
        vbuf = [b.sb(f"vb{i}", [128, 512], BF16) for i in range(2)]
        zrow = b.sb("zrow", [1, 512], F32)
        rz = b.sb("rz", [1, 512], F32)
        rzc4 = [b.sb(f"rzc{i}", [128, 1], F32) for i in range(4)]
        atm4 = [b.sb(f"attm{i}", [128, 2048], BF16) for i in range(4)]
        afm = b.sb("afm", [128, 16 * 512], BF16)
        glbuf = b.sb("glbuf", [128, 16 * 512], BF16)
        pbufF = b.sb("pbufF", [128, 32 * 128], F32)
        pbufB = b.sb("pbufB", [128, 32 * 128], BF16)
        xbuf = [b.sb(f"xb{i}", [128, 512], F32) for i in range(2)]
        x1t = b.sb("x1t", [128, 512], F32)
        rb_bc = b.sb("rbbc", [128, 512], F32)
        nb_bc = b.sb("nbbc", [128, 512], F32)
        for qg in range(4):
            nkt = nkt_of(qg)
            qsl = slice(qg * 512, qg * 512 + 512)
            for kt in range(nkt):
                pa = ps_sc[kt % 2]
                sf = sc_f[kt % 2]
                for c in range(2):
                    s.op("tensor", lambda e, kt=kt, c=c, pa=pa, qsl=qsl: e.matmul(
                        pa[:, :], k_fm[c][:, kt * 128:(kt + 1) * 128], q_fm[c][:, qsl],
                        start=(c == 0), stop=(c == 1)),
                         reads=[k_fm[c], q_fm[c]], writes=[pa])
                s.op("vector", lambda e, pa=pa, sf=sf, qg=qg, kt=kt: e.scalar_tensor_tensor(
                    ps_view(sf), ps_view(pa), 1.0, band_view(band, qg, kt),
                    ALU.mult, ALU.add),
                     reads=[pa, band], writes=[sf])
                s.op("scalar", lambda e, sf=sf, kt=kt: e.activation(
                    probsT[kt][:, :], sf[:, :], AF.Exp),
                     reads=[sf], writes=[probsT[kt]])
            for kt in range(nkt):
                s.op("tensor", lambda e, kt=kt: e.matmul(
                    ps_z[:, :], b.onesc_b[:, :], probsT[kt][:, :],
                    start=(kt == 0), stop=(kt == nkt - 1)),
                     reads=[b.onesc_b, probsT[kt]], writes=[ps_z])
            s.op("vector", lambda e: e.tensor_copy(zrow[0:1, :], ps_z[:, :]),
                 reads=[ps_z], writes=[zrow])
            s.op("vector", lambda e: e.reciprocal(rz[0:1, :], zrow[0:1, :]),
                 reads=[zrow], writes=[rz])
            for qsub in range(4):
                s.op("tensor", lambda e, qsub=qsub: e.matmul(
                    ps_sc[0][:, 0:128], rz[0:1, qsub * 128:(qsub + 1) * 128],
                    b.onesr_f[0:1, :], start=True, stop=True),
                     reads=[rz, b.onesr_f], writes=[ps_sc[0]])
                s.op("vector", lambda e, qsub=qsub: e.tensor_copy(
                    rzc4[qsub][:, :], ps_sc[0][:, 0:1]),
                     reads=[ps_sc[0]], writes=[rzc4[qsub]])
            for vh in range(2):
                for vd in range(2):
                    col0 = 256 + vh * 1024 + vd * 512
                    for kt in range(nkt):
                        vb = vbuf[kt % 2]
                        s.dma("sync", vb[:, 0:256],
                              kvf.ap()[kvrow(kt):kvrow(kt) + 128, col0:col0 + 256],
                              reads=[kvf], writes=[vb])
                        s.dma("scalar", vb[:, 256:512],
                              kvf.ap()[kvrow(kt):kvrow(kt) + 128,
                                       col0 + 256:col0 + 512],
                              reads=[kvf], writes=[vb])
                        for qsub in range(4):
                            po = po4[qsub]
                            s.op("tensor", lambda e, kt=kt, vb=vb, qsub=qsub, po=po,
                                 nkt=nkt: e.matmul(
                                     po[:, :],
                                     probsT[kt][:, qsub * 128:(qsub + 1) * 128],
                                     vb[:, 0:512],
                                     start=(kt == 0), stop=(kt == nkt - 1)),
                                 reads=[probsT[kt], vb], writes=[po])
                    for qsub in range(4):
                        s.op("vector", lambda e, qsub=qsub, vh=vh, vd=vd: e.tensor_scalar(
                            atm4[qsub][:, vh * 1024 + vd * 512:
                                       vh * 1024 + (vd + 1) * 512],
                            po4[qsub][:, :], rzc4[qsub][:, :], None, ALU.mult),
                             reads=[po4[qsub], rzc4[qsub]], writes=[atm4[qsub]])
            for qsub in range(4):
                at = atm4[qsub]
                for vg in range(4):
                    for i in range(4):
                        vc = vg * 4 + i
                        s.op("tensor", lambda e, vc=vc, i=i, at=at: e.transpose(
                            ps_t[:, i * 128:(i + 1) * 128],
                            at[:, vc * 128:(vc + 1) * 128], b.i128[:, :]),
                             reads=[at, b.i128], writes=[ps_t])
                    s.op("vector", lambda e, vg=vg, qsub=qsub: e.tensor_copy(
                        afm[:, :].rearrange("p (c t) -> p c t", t=512)
                        [:, vg * 4:(vg + 1) * 4, qsub * 128:(qsub + 1) * 128],
                        ps_t[:, :].rearrange("p (c t) -> p c t", t=128)),
                         reads=[ps_t], writes=[afm])
            s.dma("sync", glbuf[:, :].rearrange("p (c t) -> p c t", t=512),
                  _rearr_cpT(gl.ap())[:, :, qsl], reads=[gl], writes=[glbuf])
            s.op("tensor", lambda e, qsl=qsl: e.matmul(
                ps_sc[0][:, :], b.onesr_f[:, :], r1p1_row[0:1, qsl], start=True, stop=True),
                 reads=[b.onesr_f, r1p1_row], writes=[ps_sc[0]])
            s.op("vector", lambda e: e.tensor_copy(rb_bc[:, :], ps_sc[0][:, :]),
                 reads=[ps_sc[0]], writes=[rb_bc])
            s.op("tensor", lambda e, qsl=qsl: e.matmul(
                ps_sc[1][:, :], b.onesr_f[:, :], nu1_row[0:1, qsl], start=True, stop=True),
                 reads=[b.onesr_f, nu1_row], writes=[ps_sc[1]])
            s.op("vector", lambda e: e.tensor_copy(nb_bc[:, :], ps_sc[1][:, :]),
                 reads=[ps_sc[1]], writes=[nb_bc])
            for dm in range(16):
                projv = projT.ap().rearrange("(c p) o -> p c o", p=128)
                s.dma("sync", pbufF[:, 0:16 * 128].rearrange("p (c o) -> p c o", o=128),
                      projv[:, 0:16, dm * 128:(dm + 1) * 128],
                      reads=[projT], writes=[pbufF])
                s.dma("gpsimd", pbufF[:, 16 * 128:32 * 128].rearrange("p (c o) -> p c o", o=128),
                      projv[:, 16:32, dm * 128:(dm + 1) * 128],
                      reads=[projT], writes=[pbufF])
                s.op("vector", lambda e: e.tensor_copy(pbufB[:, :], pbufF[:, :]),
                     reads=[pbufF], writes=[pbufB])
                xb = xbuf[dm % 2]
                s.dma("scalar", xb[:, :], xin.ap()[dm * 128:(dm + 1) * 128, qsl],
                      reads=[xin], writes=[xb])
                pp = ps_sc[dm % 2]
                for ec in range(32):
                    rhs_t = glbuf if ec < 16 else afm
                    rhs_ap = (glbuf[:, (ec % 16) * 512:(ec % 16 + 1) * 512] if ec < 16
                              else afm[:, (ec - 16) * 512:(ec - 15) * 512])
                    s.op("tensor", lambda e, ec=ec, pp=pp, rhs_ap=rhs_ap: e.matmul(
                        pp[:, :], pbufB[:, ec * 128:(ec + 1) * 128], rhs_ap,
                        start=(ec == 0), stop=(ec == 31)),
                         reads=[pbufB, rhs_t], writes=[pp])
                s.op("vector", lambda e, xb=xb: e.tensor_mul(x1t[:, :], xb[:, :],
                                                             rb_bc[:, :]),
                     reads=[xb, rb_bc], writes=[x1t])
                s.op("vector", lambda e, pp=pp: e.tensor_add(x1t[:, :], x1t[:, :],
                                                             pp[:, :]),
                     reads=[x1t, pp], writes=[x1t])
                s.op("vector", lambda e: e.tensor_sub(x1t[:, :], x1t[:, :], nb_bc[:, :]),
                     reads=[x1t, nb_bc], writes=[x1t])
                s.dma("scalar", x1.ap()[dm * 128:(dm + 1) * 128, qsl], x1t[:, :],
                      reads=[x1t], writes=[x1])
    return b.finish() if standalone else None



def build_l0f_pre(b=None, io=None):
    standalone = b is None
    if standalone:
        b = Build("l0f_pre")
    s = b.s
    if io is None:
        io = {"xin": b.din("xin", [D, T], F32),
              "gwT": b.din("gwT", [128, 16 * 128], F32),
              "sgw": b.din("sgw", [128, 16], F32),
              "xln2": b.dout("xln2", [D, T], BF16),
              "comb": b.dout("comb", [8, T], BF16),
              "grow": b.dout("grow", [1, T], F32)}
    xin, gwT, sgw = io["xin"], io["gwT"], io["sgw"]
    xln2, comb, grow = io.get("xln2"), io["comb"], io["grow"]
    if standalone:
        b.load_consts()
    x_ln = io.get("xln_sb") or [b.sb(f"xln{ch}", [128, T], BF16) for ch in range(16)]
    with b.scope():
        emit_layernorm(b, xin, x_ln, gamma_beta=None, rho_mode="single")
    if xln2 is not None:
        for ch in range(16):
            s.dma("scalar", xln2.ap()[ch * 128:(ch + 1) * 128, :], x_ln[ch][:, :],
                  reads=[x_ln[ch]], writes=[xln2])
    with b.scope():
        gwf = b.sb("gwf", [128, 16 * 128], F32)
        gwb = b.sb("gwb", [128, 16 * 128], BF16)
        sgf = b.sb("sgf", [128, 16], F32)
        sgb = b.sb("sgb", [128, 16], BF16)
        lg_all = b.sb("lgall", [128, 128], F32)
        e_all = b.sb("eall", [128, 128], F32)
        eq1 = b.sb("eq1", [128, 128], F32)
        neq = b.sb("neq", [128, 128], F32)
        me_all = b.sb("meall", [128, 128], F32)
        m1a = b.sb("m1a", [128, 16], F32)
        e2a = b.sb("e2a", [128, 16], F32)
        dena = b.sb("dena", [128, 16], F32)
        rden = b.sb("rden", [128, 16], F32)
        cmb_all = b.sb("cmba", [128, 128], F32)
        cmb_b = b.sb("cmbb", [128, 128], BF16)
        cfm = b.sb("cfm", [8, 16 * 128], BF16)
        grow_row = b.sb("growr", [1, T], F32)
        ps_g = b.ps("psg", [128, 128])
        ps_t8 = b.ps("pst8", [8, 128], BF16)
        ps_sg = b.ps("pssg", [1, 512])
        s.dma("sync", gwf[:, :], gwT.ap(), reads=[gwT], writes=[gwf])
        s.op("vector", lambda e: e.tensor_copy(gwb[:, :], gwf[:, :]),
             reads=[gwf], writes=[gwb])
        s.dma("sync", sgf[:, :], sgw.ap(), reads=[sgw], writes=[sgf])
        s.op("vector", lambda e: e.tensor_copy(sgb[:, :], sgf[:, :]),
             reads=[sgf], writes=[sgb])
        v3 = lambda t, o=8: t[:, :].rearrange("p (g o) -> p g o", o=o)
        for tt in range(16):
            t0 = tt * 128
            for k in range(16):
                s.op("tensor", lambda e, k=k, t0=t0: e.matmul(
                    ps_g[:, :], x_ln[k][:, t0:t0 + 128], gwb[:, k * 128:(k + 1) * 128],
                    start=(k == 0), stop=(k == 15)),
                     reads=[x_ln[k], gwb], writes=[ps_g])
            s.op("vector", lambda e, tt=tt: e.tensor_copy(lg_all[:, tt * 8:(tt + 1) * 8],
                                                          ps_g[:, 0:8]),
                 reads=[ps_g], writes=[lg_all])
        s.op("scalar", lambda e: e.activation(e_all[:, :], lg_all[:, :], AF.Exp),
             reads=[lg_all], writes=[e_all], short=1200)
        s.op("vector", lambda e: e.reduce_max(v3(m1a, 1), v3(e_all), AX.X),
             reads=[e_all], writes=[m1a])
        s.op("vector", lambda e: e.tensor_tensor(v3(eq1), v3(e_all),
                                                 v3(m1a, 1).broadcast_to([128, 16, 8]),
                                                 ALU.is_equal),
             reads=[e_all, m1a], writes=[eq1], short=700)
        s.op("vector", lambda e: e.tensor_scalar(neq[:, :], eq1[:, :], -1.0, 1.0,
                                                 ALU.mult, ALU.add),
             reads=[eq1], writes=[neq])
        s.op("vector", lambda e: e.tensor_mul(me_all[:, :], e_all[:, :], neq[:, :]),
             reads=[e_all, neq], writes=[me_all], short=700)
        s.op("vector", lambda e: e.reduce_max(v3(e2a, 1), v3(me_all), AX.X),
             reads=[me_all], writes=[e2a])
        s.op("vector", lambda e: e.tensor_add(dena[:, :], m1a[:, :], e2a[:, :]),
             reads=[m1a, e2a], writes=[dena])
        s.op("vector", lambda e: e.reciprocal(rden[:, :], dena[:, :]),
             reads=[dena], writes=[rden])
        s.op("vector", lambda e: e.tensor_tensor(v3(neq), v3(me_all),
                                                 v3(e2a, 1).broadcast_to([128, 16, 8]),
                                                 ALU.is_equal),
             reads=[me_all, e2a], writes=[neq], short=700)
        s.op("vector", lambda e: e.tensor_add(eq1[:, :], eq1[:, :], neq[:, :]),
             reads=[eq1, neq], writes=[eq1], short=700)
        s.op("vector", lambda e: e.tensor_mul(cmb_all[:, :], e_all[:, :], eq1[:, :]),
             reads=[e_all, eq1], writes=[cmb_all], short=700)
        s.op("vector", lambda e: e.tensor_tensor(v3(cmb_all), v3(cmb_all),
                                                 v3(rden, 1).broadcast_to([128, 16, 8]),
                                                 ALU.mult),
             reads=[cmb_all, rden], writes=[cmb_all], short=700)
        s.op("vector", lambda e: e.tensor_copy(cmb_b[:, :], cmb_all[:, :]),
             reads=[cmb_all], writes=[cmb_b])
        for tt in range(16):
            t0 = tt * 128
            s.op("tensor", lambda e, tt=tt: e.transpose(
                ps_t8[:, :], cmb_b[:, tt * 8:(tt + 1) * 8], b.i128[:, :]),
                 reads=[cmb_b, b.i128], writes=[ps_t8])
            s.op("vector", lambda e, t0=t0: e.tensor_copy(cfm[:, t0:t0 + 128],
                                                          ps_t8[:, :]),
                 reads=[ps_t8], writes=[cfm])
        s.dma("scalar", comb.ap(), cfm[:, :], reads=[cfm], writes=[comb])
        for stt in range(4):
            sl = slice(stt * 512, stt * 512 + 512)
            for k in range(16):
                s.op("tensor", lambda e, k=k, sl=sl: e.matmul(
                    ps_sg[:, :], sgb[:, k:k + 1], x_ln[k][:, sl],
                    start=(k == 0), stop=(k == 15)),
                     reads=[sgb, x_ln[k]], writes=[ps_sg])
            s.op("scalar", lambda e, sl=sl: e.activation(grow_row[0:1, sl], ps_sg[:, :],
                                                         AF.Sigmoid),
                 reads=[ps_sg], writes=[grow_row])
        s.dma("scalar", grow.ap(), grow_row[0:1, :], reads=[grow_row], writes=[grow])
    return b.finish() if standalone else None



I_MOE = 1408
NIC = I_MOE // 128

def build_l0f_hh(b=None, io=None):
    standalone = b is None
    if standalone:
        b = Build("l0f_hh")
    s = b.s
    if io is None:
        io = {"xln2": b.din("xln2", [D, T], BF16),
              "wg": b.din("wg", [8 * D, I_MOE], F32),
              "wu": b.din("wu", [8 * D, I_MOE], F32),
              "comb": b.din("comb", [8, T], BF16),
              "hh": b.dout("hh", [8 * I_MOE, T], BF16)}
    xln2, wg, wu, comb, hh = io.get("xln2"), io["wg"], io["wu"], io["comb"], io["hh"]
    if standalone:
        b.load_consts()
    x_ln = io.get("xln_sb")
    if x_ln is None:
        x_ln = [b.sb(f"xln{ch}", [128, T], BF16) for ch in range(16)]
        for ch in range(16):
            s.dma("sync", x_ln[ch][:, :], xln2.ap()[ch * 128:(ch + 1) * 128, :],
                  reads=[xln2], writes=[x_ln[ch]])
    with b.scope():
        crow = b.sb("crow", [1, T], BF16)
        cbc = b.sb("cbc", [128, T], F32)
        wgF = [b.sb(f"wgF{i}", [128, 16 * 128], F32) for i in range(2)]
        wgB = [b.sb(f"wgB{i}", [128, 16 * 128], BF16) for i in range(2)]
        wuF = [b.sb(f"wuF{i}", [128, 16 * 128], F32) for i in range(2)]
        wuB = [b.sb(f"wuB{i}", [128, 16 * 128], BF16) for i in range(2)]
        sg = b.sb("sg", [128, 512], F32)
        ue = b.sb("ue", [128, 512], F32)
        hhT = b.sb("hhT", [128, 4 * 512], BF16)
        psG = [b.ps(f"psG{i}", [128, 512]) for i in range(2)]
        psU = [b.ps(f"psU{i}", [128, 512]) for i in range(2)]
        psC = b.ps("psC", [128, 512])
        for ei in range(8):
            s.dma("sync", crow[0:1, :], comb.ap()[ei:ei + 1, :], reads=[comb],
                  writes=[crow])
            for stt in range(4):
                sl = slice(stt * 512, stt * 512 + 512)
                s.op("tensor", lambda e, sl=sl: e.matmul(
                    psC[:, :], b.onesr_b[:, :], crow[0:1, sl], start=True, stop=True),
                     reads=[b.onesr_b, crow], writes=[psC])
                s.op("vector", lambda e, sl=sl: e.tensor_copy(cbc[:, sl], psC[:, :]),
                     reads=[psC], writes=[cbc])
            for ic in range(NIC):
                wgf, wgb = wgF[ic % 2], wgB[ic % 2]
                wuf, wub = wuF[ic % 2], wuB[ic % 2]
                col = slice(ic * 128, ic * 128 + 128)
                s.dma("sync", wgf[:, :].rearrange("p (k o) -> p k o", o=128),
                      wg.ap()[ei * D:(ei + 1) * D, col].rearrange("(k p) o -> p k o", p=128),
                      reads=[wg], writes=[wgf])
                s.op("vector", lambda e, wgf=wgf, wgb=wgb: e.tensor_copy(wgb[:, :], wgf[:, :]),
                     reads=[wgf], writes=[wgb])
                s.dma("sync", wuf[:, :].rearrange("p (k o) -> p k o", o=128),
                      wu.ap()[ei * D:(ei + 1) * D, col].rearrange("(k p) o -> p k o", p=128),
                      reads=[wu], writes=[wuf])
                s.op("vector", lambda e, wuf=wuf, wub=wub: e.tensor_copy(wub[:, :], wuf[:, :]),
                     reads=[wuf], writes=[wub])
                for stt in range(4):
                    sl = slice(stt * 512, stt * 512 + 512)
                    pg, pu = psG[stt % 2], psU[stt % 2]
                    for k in range(16):
                        s.op("tensor", lambda e, k=k, sl=sl, pg=pg, wgb=wgb: e.matmul(
                            pg[:, :], wgb[:, k * 128:(k + 1) * 128], x_ln[k][:, sl],
                            start=(k == 0), stop=(k == 15)),
                             reads=[wgb, x_ln[k]], writes=[pg])
                    for k in range(16):
                        s.op("tensor", lambda e, k=k, sl=sl, pu=pu, wub=wub: e.matmul(
                            pu[:, :], wub[:, k * 128:(k + 1) * 128], x_ln[k][:, sl],
                            start=(k == 0), stop=(k == 15)),
                             reads=[wub, x_ln[k]], writes=[pu])
                    s.op("scalar", lambda e, pg=pg: e.activation(sg[:, :], pg[:, :],
                                                                 AF.Silu),
                         reads=[pg], writes=[sg], short=500)
                    s.op("vector", lambda e, pu=pu, sl=sl: e.tensor_mul(
                        ue[:, :], pu[:, :], cbc[:, sl]),
                         reads=[pu, cbc], writes=[ue])
                    s.op("vector", lambda e, sl=sl, hhT=hhT: e.tensor_mul(
                        hhT[:, sl], sg[:, :], ue[:, :]),
                         reads=[sg, ue], writes=[hhT])
                r0 = ei * I_MOE + ic * 128
                s.dma("scalar", hh.ap()[r0:r0 + 128, :], hhT[:, :],
                      reads=[hhT], writes=[hh])
    return b.finish() if standalone else None



IS = 5632
NICS = IS // 128

def build_l0f_sh(b=None, io=None):
    standalone = b is None
    if standalone:
        b = Build("l0f_sh")
    s = b.s
    if io is None:
        io = {"xln2": b.din("xln2", [D, T], BF16), "swg": b.din("swg", [D, IS], F32),
              "swu": b.din("swu", [D, IS], F32), "grow": b.din("grow", [1, T], F32),
              "wdf": b.din("wdf", [8 * I_MOE, D], F32),
              "swdf": b.din("swdf", [IS, D], F32),
              "sh": b.dout("sh", [IS, T], BF16),
              "wdb": b.dout("wdb", [8 * I_MOE, D], BF16),
              "swdb": b.dout("swdb", [IS, D], BF16)}
    xln2, swg, swu, grow = io.get("xln2"), io["swg"], io["swu"], io["grow"]
    wdf, swdf, sh, wdb, swdb = io["wdf"], io["swdf"], io["sh"], io["wdb"], io["swdb"]
    if standalone:
        b.load_consts()
    x_ln = io.get("xln_sb")
    if x_ln is None:
        x_ln = [b.sb(f"xln{ch}", [128, T], BF16) for ch in range(16)]
        for ch in range(16):
            s.dma("sync", x_ln[ch][:, :], xln2.ap()[ch * 128:(ch + 1) * 128, :],
                  reads=[xln2], writes=[x_ln[ch]])
    with b.scope():
        grow_row = b.sb("growr", [1, T], F32)
        gbc = b.sb("gbc", [128, T], F32)
        wgF = [b.sb(f"wgF{i}", [128, 16 * 128], F32) for i in range(2)]
        wgB = [b.sb(f"wgB{i}", [128, 16 * 128], BF16) for i in range(2)]
        wuF = [b.sb(f"wuF{i}", [128, 16 * 128], F32) for i in range(2)]
        wuB = [b.sb(f"wuB{i}", [128, 16 * 128], BF16) for i in range(2)]
        cstF = [b.sb(f"cstF{i}", [128, 2048], F32) for i in range(2)]
        cstB = [b.sb(f"cstB{i}", [128, 2048], BF16) for i in range(2)]
        sg = b.sb("sg", [128, 512], F32)
        ue = b.sb("ue", [128, 512], F32)
        shT = b.sb("shT", [128, 4 * 512], BF16)
        psG = [b.ps(f"psG{i}", [128, 512]) for i in range(2)]
        psU = [b.ps(f"psU{i}", [128, 512]) for i in range(2)]
        psC = b.ps("psC", [128, 512])
        s.dma("sync", grow_row[0:1, :], grow.ap(), reads=[grow], writes=[grow_row])
        for stt in range(4):
            sl = slice(stt * 512, stt * 512 + 512)
            s.op("tensor", lambda e, sl=sl: e.matmul(
                psC[:, :], b.onesr_f[:, :], grow_row[0:1, sl], start=True, stop=True),
                 reads=[b.onesr_f, grow_row], writes=[psC])
            s.op("vector", lambda e, sl=sl: e.tensor_copy(gbc[:, sl], psC[:, :]),
                 reads=[psC], writes=[gbc])
        cast_jobs = [(wdf, wdb, r) for r in range(8 * I_MOE // 128)]
        cast_jobs += [(swdf, swdb, r) for r in range(IS // 128)]
        cast_i = [0]

        def emit_casts(n):
            for _ in range(n):
                if cast_i[0] >= len(cast_jobs):
                    return
                src, dst, r = cast_jobs[cast_i[0]]
                cast_i[0] += 1
                cf, cb = cstF[r % 2], cstB[r % 2]
                s.dma("sync", cf[:, :], src.ap()[r * 128:(r + 1) * 128, :],
                      reads=[src], writes=[cf])
                s.op("vector", lambda e, cf=cf, cb=cb: e.tensor_copy(cb[:, :], cf[:, :]),
                     reads=[cf], writes=[cb])
                s.dma("scalar", dst.ap()[r * 128:(r + 1) * 128, :], cb[:, :],
                      reads=[cb], writes=[dst])

        for ic in range(NICS):
            emit_casts(3)
            wgf, wgb = wgF[ic % 2], wgB[ic % 2]
            wuf, wub = wuF[ic % 2], wuB[ic % 2]
            col = slice(ic * 128, ic * 128 + 128)
            s.dma("sync", wgf[:, :].rearrange("p (k o) -> p k o", o=128),
                  swg.ap()[:, col].rearrange("(k p) o -> p k o", p=128),
                  reads=[swg], writes=[wgf])
            s.op("vector", lambda e, wgf=wgf, wgb=wgb: e.tensor_copy(wgb[:, :], wgf[:, :]),
                 reads=[wgf], writes=[wgb])
            s.dma("sync", wuf[:, :].rearrange("p (k o) -> p k o", o=128),
                  swu.ap()[:, col].rearrange("(k p) o -> p k o", p=128),
                  reads=[swu], writes=[wuf])
            s.op("vector", lambda e, wuf=wuf, wub=wub: e.tensor_copy(wub[:, :], wuf[:, :]),
                 reads=[wuf], writes=[wub])
            for stt in range(4):
                sl = slice(stt * 512, stt * 512 + 512)
                pg, pu = psG[stt % 2], psU[stt % 2]
                for k in range(16):
                    s.op("tensor", lambda e, k=k, sl=sl, pg=pg, wgb=wgb: e.matmul(
                        pg[:, :], wgb[:, k * 128:(k + 1) * 128], x_ln[k][:, sl],
                        start=(k == 0), stop=(k == 15)),
                         reads=[wgb, x_ln[k]], writes=[pg])
                for k in range(16):
                    s.op("tensor", lambda e, k=k, sl=sl, pu=pu, wub=wub: e.matmul(
                        pu[:, :], wub[:, k * 128:(k + 1) * 128], x_ln[k][:, sl],
                        start=(k == 0), stop=(k == 15)),
                         reads=[wub, x_ln[k]], writes=[pu])
                s.op("scalar", lambda e, pg=pg: e.activation(sg[:, :], pg[:, :], AF.Silu),
                     reads=[pg], writes=[sg], short=500)
                s.op("vector", lambda e, pu=pu, sl=sl: e.tensor_mul(
                    ue[:, :], pu[:, :], gbc[:, sl]),
                     reads=[pu, gbc], writes=[ue])
                s.op("vector", lambda e, sl=sl, shT=shT: e.tensor_mul(
                    shT[:, sl], sg[:, :], ue[:, :]),
                     reads=[sg, ue], writes=[shT])
            r0 = ic * 128
            s.dma("scalar", sh.ap()[r0:r0 + 128, :], shT[:, :],
                  reads=[shT], writes=[sh])
    return b.finish() if standalone else None



def build_l0f_down(b=None, io=None):
    standalone = b is None
    if standalone:
        b = Build("l0f_down")
    s = b.s
    if io is None:
        io = {"hh": b.din("hh", [8 * I_MOE, T], BF16), "sh": b.din("sh", [IS, T], BF16),
              "wdb": b.din("wdb", [8 * I_MOE, D], BF16),
              "swdb": b.din("swdb", [IS, D], BF16),
              "x1": b.din("x1", [D, T], F32), "x2": b.dout("x2", [D, T], F32)}
    hh, sh, wdb, swdb, x1, x2 = (io["hh"], io["sh"], io["wdb"], io["swdb"],
                                 io["x1"], io["x2"])
    if standalone:
        b.load_consts()
    with b.scope():
        big = b.sb("big", [128, 88 * 512], BF16)
        tsb = b.sb("tsb", [128, 16 * 512], F32)
        wcols = [b.sb(f"wcol{i}", [128, 88 * 128], BF16) for i in range(2)]
        swcols = [b.sb(f"swcol{i}", [128, 44 * 128], BF16) for i in range(2)]
        x1t = [b.sb(f"x1t{i}", [128, 512], F32) for i in range(2)]
        ut = b.sb("ut", [128, 512], F32)
        psD = [b.ps(f"psD{i}", [128, 512]) for i in range(2)]
        hhv = hh.ap().rearrange("(c p) t -> p c t", p=128)
        shv = sh.ap().rearrange("(c p) t -> p c t", p=128)
        for stt in range(4):
            sl = slice(stt * 512, stt * 512 + 512)
            s.dma("sync", big[:, 0:44 * 512].rearrange("p (c t) -> p c t", t=512),
                  hhv[:, 0:44, sl], reads=[hh], writes=[big])
            s.dma("gpsimd", big[:, 44 * 512:88 * 512].rearrange("p (c t) -> p c t", t=512),
                  hhv[:, 44:88, sl], reads=[hh], writes=[big])
            for dm in range(16):
                dcol = slice(dm * 128, dm * 128 + 128)
                wcol = wcols[dm % 2]
                s.dma("sync", wcol[:, :].rearrange("p (c o) -> p c o", o=128),
                      wdb.ap().rearrange("(c p) o -> p c o", p=128)[:, :, dcol],
                      reads=[wdb], writes=[wcol])
                pd = psD[dm % 2]
                for ic in range(88):
                    s.op("tensor", lambda e, ic=ic, pd=pd, wcol=wcol: e.matmul(
                        pd[:, :], wcol[:, ic * 128:(ic + 1) * 128],
                        big[:, ic * 512:(ic + 1) * 512],
                        start=(ic == 0), stop=(ic == 87)),
                         reads=[wcol, big], writes=[pd])
                s.op("vector", lambda e, pd=pd, dm=dm: e.tensor_copy(
                    tsb[:, dm * 512:(dm + 1) * 512], pd[:, :]),
                     reads=[pd], writes=[tsb])
            s.dma("sync", big[:, 0:22 * 512].rearrange("p (c t) -> p c t", t=512),
                  shv[:, 0:22, sl], reads=[sh], writes=[big])
            s.dma("gpsimd", big[:, 22 * 512:44 * 512].rearrange("p (c t) -> p c t", t=512),
                  shv[:, 22:44, sl], reads=[sh], writes=[big])
            for dm in range(16):
                dcol = slice(dm * 128, dm * 128 + 128)
                swcol = swcols[dm % 2]
                s.dma("sync", swcol[:, :].rearrange("p (c o) -> p c o", o=128),
                      swdb.ap().rearrange("(c p) o -> p c o", p=128)[:, :, dcol],
                      reads=[swdb], writes=[swcol])
                pd = psD[dm % 2]
                for ic in range(44):
                    s.op("tensor", lambda e, ic=ic, pd=pd, swcol=swcol: e.matmul(
                        pd[:, :], swcol[:, ic * 128:(ic + 1) * 128],
                        big[:, ic * 512:(ic + 1) * 512],
                        start=(ic == 0), stop=(ic == 43)),
                         reads=[swcol, big], writes=[pd])
                xb = x1t[dm % 2]
                s.dma("scalar", xb[:, :], x1.ap()[dm * 128:(dm + 1) * 128, sl],
                      reads=[x1], writes=[xb])
                s.op("vector", lambda e, pd=pd, dm=dm: e.tensor_add(
                    ut[:, :], pd[:, :], tsb[:, dm * 512:(dm + 1) * 512]),
                     reads=[pd, tsb], writes=[ut])
                s.op("vector", lambda e, xb=xb: e.tensor_add(ut[:, :], ut[:, :], xb[:, :]),
                     reads=[ut, xb], writes=[ut])
                s.dma("scalar", x2.ap()[dm * 128:(dm + 1) * 128, sl], ut[:, :],
                      reads=[ut], writes=[x2])
    return b.finish() if standalone else None


def host_inputs_l0f(inputs, x1_arrs, resC=None, resD=None, resD2=None, stage="pre"):
    consts = mkconsts()
    maps = []
    if stage == "pre":
        gwT_small = inputs["L0_ffn_gate_w"].T.reshape(16, 128, 8).transpose(1, 0, 2)
        gwT = np.zeros((128, 16, 128), np.float32)
        gwT[:, :, 0:8] = gwT_small
        gwT = np.ascontiguousarray(gwT.reshape(128, 16 * 128))
        sgw = np.ascontiguousarray(inputs["L0_ffn_shared_gate_w"].reshape(16, 128).T)
        for c in range(NCORE):
            m = {"xin": x1_arrs[c], "gwT": gwT, "sgw": sgw}
            m.update(consts)
            maps.append(m)
    elif stage == "hh":
        wg = np.ascontiguousarray(inputs["L0_ffn_w_gate"].reshape(8 * D, I_MOE))
        wu = np.ascontiguousarray(inputs["L0_ffn_w_up"].reshape(8 * D, I_MOE))
        for c in range(NCORE):
            m = {"xln2": resC[c]["xln2"], "wg": wg, "wu": wu, "comb": resC[c]["comb"]}
            m.update(consts)
            maps.append(m)
    elif stage == "sh":
        swg = np.ascontiguousarray(inputs["L0_ffn_sw_gate"])
        swu = np.ascontiguousarray(inputs["L0_ffn_sw_up"])
        wdf = np.ascontiguousarray(inputs["L0_ffn_w_down"].reshape(8 * I_MOE, D))
        swdf = np.ascontiguousarray(inputs["L0_ffn_sw_down"])
        for c in range(NCORE):
            m = {"xln2": resC[c]["xln2"], "swg": swg, "swu": swu,
                 "grow": resC[c]["grow"], "wdf": wdf, "swdf": swdf}
            m.update(consts)
            maps.append(m)
    elif stage == "down":
        for c in range(NCORE):
            m = {"hh": resD[c]["hh"], "sh": resD2[c]["sh"], "wdb": resD2[c]["wdb"],
                 "swdb": resD2[c]["swdb"], "x1": x1_arrs[c]}
            m.update(consts)
            maps.append(m)
    return maps



def build_l1a_pre(b=None, io=None):
    standalone = b is None
    if standalone:
        b = Build("l1a_pre")
    s = b.s
    if io is None:
        io = {"xin": b.din("xin", [D, T], F32), "wq": b.din("wq", [D, D], F32),
              "wk": b.din("wk", [D, 512], F32), "wv": b.din("wv", [D, 512], F32),
              "ctab": b.din("ctab", [128, 16 * 512], BF16),
              "stab": b.din("stab", [128, 16 * 512], BF16),
              "q_tm": b.dout("q_tm", [T, D], BF16),
              "kv1": b.dout("kv1", [T, 1024], BF16)}
    xin, wq, wk, wv = io["xin"], io["wq"], io["wk"], io["wv"]
    ctab, stab, q_tm, kv1 = io["ctab"], io["stab"], io["q_tm"], io["kv1"]
    if standalone:
        b.load_consts()
    x_ln = [b.sb(f"xln{ch}", [128, T], BF16) for ch in range(16)]
    with b.scope():
        emit_layernorm(b, xin, x_ln, gamma_beta=None, rho_mode="single")
    ct_sb = b.sb("ctsb", [128, 16 * 512], BF16)
    st_sb = b.sb("stsb", [128, 16 * 512], BF16)
    s.dma("sync", ct_sb[:, :], ctab.ap(), reads=[ctab], writes=[ct_sb])
    s.dma("sync", st_sb[:, :], stab.ap(), reads=[stab], writes=[st_sb])
    with b.scope():
        wf = b.sb("wf", [128, 16 * 512], F32)
        wb = [b.sb(f"wb{i}", [128, 16 * 512], BF16) for i in range(2)]
        ps = [b.ps(f"ps{i}", [128, 512]) for i in range(2)]
        uu = b.sb("uu", [128, 512], F32)
        vv = b.sb("vv", [128, 512], F32)
        ot = b.sb("ot", [128, 512], BF16)
        r4 = lambda ap: ap.rearrange("p (h a j) -> p h a j", a=2, j=32)
        parts = [("k", wk, 0, 1.0, True, kv1, 0), ("v", wv, 0, 1.0, False, kv1, 512)]
        parts += [("q", wq, oc, 0.125, True, q_tm, oc * 512) for oc in range(4)]
        for pi, (nm, src, oc, scale, rope, dst, dcol) in enumerate(parts):
            if pi == 2 and io.get("after_kv"):
                io["after_kv"]()
            wbb = wb[pi % 2]
            s.dma("sync", wf[:, :].rearrange("p (k o) -> p k o", o=512),
                  src.ap().rearrange("(k p) o -> p k o", p=128)[:, :, oc * 512:(oc + 1) * 512],
                  reads=[src], writes=[wf])
            s.op("vector", lambda e, wbb=wbb: e.tensor_copy(wbb[:, :], wf[:, :]),
                 reads=[wf], writes=[wbb])
            for tt in range(16):
                t0 = tt * 128
                pp = ps[tt % 2]
                for k in range(16):
                    s.op("tensor", lambda e, k=k, t0=t0, pp=pp, wbb=wbb: e.matmul(
                        pp[:, :], x_ln[k][:, t0:t0 + 128], wbb[:, k * 512:(k + 1) * 512],
                        start=(k == 0), stop=(k == 15)),
                         reads=[x_ln[k], wbb], writes=[pp])
                tsl = slice(tt * 512, tt * 512 + 512)
                if rope:
                    s.op("vector", lambda e, pp=pp, tsl=tsl, scale=scale:
                         e.scalar_tensor_tensor(uu[:, :], pp[:, :], scale,
                                                ct_sb[:, tsl], ALU.mult, ALU.mult),
                         reads=[pp, ct_sb], writes=[uu])
                    s.op("vector", lambda e, pp=pp, tsl=tsl, scale=scale:
                         e.scalar_tensor_tensor(
                             r4(vv[:, :])[:, :, 0:1, :], r4(pp[:, :])[:, :, 1:2, :],
                             scale, r4(st_sb[:, tsl])[:, :, 0:1, :], ALU.mult, ALU.mult),
                         reads=[pp, st_sb], writes=[vv], short=700)
                    s.op("vector", lambda e, pp=pp, tsl=tsl, scale=scale:
                         e.scalar_tensor_tensor(
                             r4(vv[:, :])[:, :, 1:2, :], r4(pp[:, :])[:, :, 0:1, :],
                             scale, r4(st_sb[:, tsl])[:, :, 1:2, :], ALU.mult, ALU.mult),
                         reads=[pp, st_sb], writes=[vv], short=700)
                    s.op("vector", lambda e: e.tensor_add(ot[:, :], uu[:, :], vv[:, :]),
                         reads=[uu, vv], writes=[ot])
                else:
                    s.op("vector", lambda e, pp=pp: e.tensor_copy(ot[:, :], pp[:, :]),
                         reads=[pp], writes=[ot])
                if nm == "q":
                    dv = dst.ap()[t0:t0 + 128, dcol:dcol + 512].rearrange(
                        "t (c a j) -> t c a j", c=4, a=2, j=64)
                    for a in range(2):
                        s.dma("scalar", dv[:, :, a:a + 1, :],
                              ot[:, a * 256:(a + 1) * 256].rearrange(
                                  "t (c x j) -> t c x j", c=4, x=1, j=64),
                              reads=[ot], writes=[dst])
                else:
                    s.dma("scalar", dst.ap()[t0:t0 + 128, dcol:dcol + 512], ot[:, :],
                          reads=[ot], writes=[dst])
    return b.finish() if standalone else None



def build_l1a_attn(b=None, io=None):
    standalone = b is None
    if standalone:
        b = Build("l1a_attn")
    s = b.s
    if io is None:
        io = {"q_tm": b.din("q_tm", [T, D], BF16), "kvf": b.din("kvf", [S, 1024], BF16),
              "wo": b.din("wo", [D, D], F32), "xin": b.din("xin", [D, T], F32),
              "bandM": b.din("bandM", [128, 2048], F32),
              "x3": b.dout("x3", [D, T], F32)}
    q_tm, kvf, wo, xin, bandM, x3 = (io["q_tm"], io["kvf"], io["wo"], io["xin"],
                                     io["bandM"], io["x3"])
    kvrow = io.get("kvrow", lambda kt: kt * 128)
    if standalone:
        b.load_consts()
    band = b.sb("band", [128, 2048], F32)
    s.dma("sync", band[:, :], bandM.ap(), reads=[bandM], writes=[band])
    k_fm = [b.sb(f"kfm{c}", [128, S], BF16) for c in range(4)]
    vs_sb = b.sb("vs", [128, 32 * 520], BF16)
    with b.scope():
        kseg = [b.sb(f"kseg{i}", [128, 512], BF16) for i in range(2)]
        pst = b.ps("pst", [128, 128], BF16)
        for kt in range(32):
            sg = kseg[kt % 2]
            s.dma("sync", sg[:, :], kvf.ap()[kvrow(kt):kvrow(kt) + 128, 0:512],
                  reads=[kvf], writes=[sg])
            for c in range(4):
                s.op("tensor", lambda e, sg=sg, c=c: e.transpose(
                    pst[:, :], sg[:, c * 128:(c + 1) * 128], b.i128[:, :]),
                     reads=[sg, b.i128], writes=[pst])
                s.op("vector", lambda e, kt=kt, c=c: e.tensor_copy(
                    k_fm[c][:, kt * 128:(kt + 1) * 128], pst[:, :]),
                     reads=[pst], writes=[k_fm[c]])
            s.dma("scalar",
                  vs_sb[:, kt * 520:(kt + 1) * 520].rearrange("p (h j) -> p h j", j=65)[:, :, 0:64],
                  kvf.ap()[kvrow(kt):kvrow(kt) + 128, 512:1024].rearrange("p (h j) -> p h j", j=64),
                  reads=[kvf], writes=[vs_sb])
            s.op("vector", lambda e, kt=kt: e.memset(
                vs_sb[:, kt * 520:(kt + 1) * 520].rearrange("p (h j) -> p h j", j=65)[:, :, 64:65],
                1.0),
                 reads=[], writes=[vs_sb])
    with b.scope():
        q_fm = [b.sb(f"qfm{c}", [128, 512], BF16) for c in range(16)]
        qtl = [b.sb(f"qtl{i}", [128, 2048], BF16) for i in range(2)]
        probsT = [[b.sb(f"pT{i}_{j}", [128, 512], BF16) for j in range(2)]
                  for i in range(4)]
        sc_f = [b.sb(f"scf{i}", [128, 512], F32) for i in range(2)]
        zrow = b.sb("zrow", [1, 512], F32)
        rz = b.sb("rz", [1, 512], F32)
        zb_sb = b.sb("zbsb", [64, 512], F32)
        y_sb = [b.sb(f"ysb{c}", [128, 512], BF16) for c in range(16)]
        xb = [b.sb(f"xb{i}", [128, 512], F32) for i in range(2)]
        x3t = b.sb("x3t", [128, 512], F32)
        pst = b.ps("pstq", [128, 128], BF16)
        ps_sc = [b.ps(f"pssc{i}", [128, 512]) for i in range(2)]
        ps_pv = [b.ps(f"pspv{i}", [65, 512]) for i in range(4)]
        ps_bc = b.ps("psbc", [64, 512])
        wocf = b.sb("wocf", [128, 16 * 128], F32)
        wocb = [b.sb(f"wocb{i}", [128, 16 * 128], BF16) for i in range(2)]
        for qg in range(4):
            nkt = nkt_of(qg)
            for tts in range(4):
                qt = qtl[tts % 2]
                s.dma("sync", qt[:, :],
                      q_tm.ap()[qg * 512 + tts * 128:qg * 512 + (tts + 1) * 128, :],
                      reads=[q_tm], writes=[qt])
                for ch in range(16):
                    s.op("tensor", lambda e, qt=qt, ch=ch: e.transpose(
                        pst[:, :], qt[:, ch * 128:(ch + 1) * 128], b.i128[:, :]),
                         reads=[qt, b.i128], writes=[pst])
                    s.op("vector", lambda e, ch=ch, tts=tts: e.tensor_copy(
                        q_fm[ch][:, tts * 128:(tts + 1) * 128], pst[:, :]),
                         reads=[pst], writes=[q_fm[ch]])
            for kvh in range(8):
                kch, krow = kvh // 2, (kvh % 2) * 64

                def emit_pv(kt, nkt=None, kvh=kvh):
                    for h4 in range(4):
                        pT = probsT[h4][kt % 2]
                        s.op("tensor", lambda e, h4=h4, kt=kt, kvh=kvh, pT=pT,
                             nkt=nkt: e.matmul(
                                 ps_pv[h4][:, :],
                                 vs_sb[:, kt * 520 + kvh * 65:kt * 520 + (kvh + 1) * 65],
                                 pT[:, :],
                                 start=(kt == 0), stop=(kt == nkt - 1)),
                             reads=[vs_sb, pT], writes=[ps_pv[h4]])

                for kt in range(nkt):
                    for h4 in range(4):
                        h = kvh * 4 + h4
                        slot = (h // 8) * 8 + (h % 4) * 2 + (h // 4) % 2
                        qch, qrow = slot // 2, (slot % 2) * 64
                        pa = ps_sc[(kt * 4 + h4) % 2]
                        sf = sc_f[(kt * 4 + h4) % 2]
                        s.op("tensor", lambda e, pa=pa, kt=kt, kch=kch, krow=krow,
                             qch=qch, qrow=qrow: e.matmul(
                                 pa[:, :],
                                 k_fm[kch][krow:krow + 64, kt * 128:(kt + 1) * 128],
                                 q_fm[qch][qrow:qrow + 64, :],
                                 start=True, stop=True),
                             reads=[k_fm[kch], q_fm[qch]], writes=[pa])
                        if kt >= 8 * qg:
                            s.op("vector", lambda e, pa=pa, sf=sf, qg=qg, kt=kt:
                                 e.scalar_tensor_tensor(ps_view(sf), ps_view(pa), 1.0,
                                                        band_view(band, qg, kt),
                                                        ALU.mult, ALU.add),
                                 reads=[pa, band], writes=[sf])
                            esrc = sf
                        else:
                            esrc = pa
                        s.op("scalar", lambda e, esrc=esrc, h4=h4, kt=kt: e.activation(
                            probsT[h4][kt % 2][:, :], esrc[:, :], AF.Exp),
                             reads=[esrc], writes=[probsT[h4][kt % 2]])
                    if kt > 0:
                        emit_pv(kt - 1, nkt=nkt)
                emit_pv(nkt - 1, nkt=nkt)
                for h4 in range(4):
                    h = kvh * 4 + h4
                    ych, yrow = h // 2, (h % 2) * 64
                    s.op("vector", lambda e, h4=h4: e.reciprocal(
                        rz[0:1, :], ps_pv[h4][64:65, :]),
                         reads=[ps_pv[h4]], writes=[rz])
                    s.op("tensor", lambda e: e.matmul(
                        ps_bc[:, :], b.onesr_f[0:1, 0:64], rz[0:1, :],
                        start=True, stop=True),
                         reads=[b.onesr_f, rz], writes=[ps_bc])
                    s.op("vector", lambda e: e.tensor_copy(zb_sb[:, :], ps_bc[:, :]),
                         reads=[ps_bc], writes=[zb_sb])
                    s.op("vector", lambda e, h4=h4, ych=ych, yrow=yrow: e.tensor_mul(
                        y_sb[ych][yrow:yrow + 64, :], ps_pv[h4][0:64, :], zb_sb[:, :]),
                         reads=[ps_pv[h4], zb_sb], writes=[y_sb[ych]])
            qsl = slice(qg * 512, qg * 512 + 512)
            for dm in range(16):
                pp = ps_sc[dm % 2]
                xbb = xb[dm % 2]
                wob = wocb[dm % 2]
                s.dma("sync", wocf[:, :].rearrange("p (k o) -> p k o", o=128),
                      wo.ap().rearrange("(k p) o -> p k o", p=128)[:, :, dm * 128:(dm + 1) * 128],
                      reads=[wo], writes=[wocf])
                s.op("vector", lambda e, wob=wob: e.tensor_copy(wob[:, :], wocf[:, :]),
                     reads=[wocf], writes=[wob])
                s.dma("scalar", xbb[:, :], xin.ap()[dm * 128:(dm + 1) * 128, qsl],
                      reads=[xin], writes=[xbb])
                for k in range(16):
                    s.op("tensor", lambda e, k=k, pp=pp, wob=wob: e.matmul(
                        pp[:, :], wob[:, k * 128:(k + 1) * 128],
                        y_sb[k][:, :], start=(k == 0), stop=(k == 15)),
                         reads=[wob, y_sb[k]], writes=[pp])
                s.op("vector", lambda e, pp=pp, xbb=xbb: e.tensor_add(
                    x3t[:, :], pp[:, :], xbb[:, :]),
                     reads=[pp, xbb], writes=[x3t])
                s.dma("scalar", x3.ap()[dm * 128:(dm + 1) * 128, qsl], x3t[:, :],
                      reads=[x3t], writes=[x3])
    return b.finish() if standalone else None



def build_l1f_pre(b=None, io=None):
    standalone = b is None
    if standalone:
        b = Build("l1f_pre")
    s = b.s
    if io is None:
        io = {"xin": b.din("xin", [D, T], F32),
              "wqkvT": b.din("wqkvT", [D, 6144], F32),
              "wgT": b.din("wgT", [128, 16 * 128], F32),
              "q_fm": b.dout("q_fm", [D, T], BF16),
              "ksvs": b.dout("ksvs", [T, 257], BF16)}
    xin, wqkvT, wgT = io["xin"], io["wqkvT"], io["wgT"]
    q_fm_o, ksvs = io["q_fm"], io["ksvs"]
    if standalone:
        b.load_consts()
    x_ln = [b.sb(f"xln{ch}", [128, T], BF16) for ch in range(16)]
    with b.scope():
        emit_layernorm(b, xin, x_ln, gamma_beta=None, rho_mode="single")
    ind_all = b.sb("indall", [128, 16 * 16], F32)
    wv_all = b.sb("wvall", [128, 16 * 16], F32)
    with b.scope():
        gwf = b.sb("gwf", [128, 16 * 128], F32)
        gwb = b.sb("gwb", [128, 16 * 128], BF16)
        lg_all = b.sb("lgall", [128, 16 * 16], F32)
        e_all = b.sb("eall", [128, 16 * 16], F32)
        m1a = b.sb("m1a", [128, 16], F32)
        za = b.sb("za", [128, 16], F32)
        rza = b.sb("rza", [128, 16], F32)
        s1a = b.sb("s1a", [128, 16], F32)
        ps_g = b.ps("psg", [128, 128])
        v3 = lambda t, o=16: t[:, :].rearrange("p (g o) -> p g o", o=o)
        s.dma("sync", gwf[:, :], wgT.ap(), reads=[wgT], writes=[gwf])
        s.op("vector", lambda e: e.tensor_copy(gwb[:, :], gwf[:, :]),
             reads=[gwf], writes=[gwb])
        for tt in range(16):
            t0 = tt * 128
            for k in range(16):
                s.op("tensor", lambda e, k=k, t0=t0: e.matmul(
                    ps_g[:, :], x_ln[k][:, t0:t0 + 128], gwb[:, k * 128:(k + 1) * 128],
                    start=(k == 0), stop=(k == 15)),
                     reads=[x_ln[k], gwb], writes=[ps_g])
            s.op("vector", lambda e, tt=tt: e.tensor_copy(
                lg_all[:, tt * 16:(tt + 1) * 16], ps_g[:, 0:16]),
                 reads=[ps_g], writes=[lg_all])
        s.op("scalar", lambda e: e.activation(e_all[:, :], lg_all[:, :], AF.Exp),
             reads=[lg_all], writes=[e_all], short=1200)
        s.op("vector", lambda e: e.reduce_max(v3(m1a, 1), v3(e_all), AX.X),
             reads=[e_all], writes=[m1a])
        s.op("vector", lambda e: e.reduce_sum(v3(za, 1), v3(e_all), AX.X),
             reads=[e_all], writes=[za])
        s.op("vector", lambda e: e.reciprocal(rza[:, :], za[:, :]),
             reads=[za], writes=[rza])
        s.op("vector", lambda e: e.tensor_mul(s1a[:, :], m1a[:, :], rza[:, :]),
             reads=[m1a, rza], writes=[s1a])
        s.op("vector", lambda e: e.tensor_tensor(v3(ind_all), v3(e_all),
                                                 v3(m1a, 1).broadcast_to([128, 16, 16]),
                                                 ALU.is_equal),
             reads=[e_all, m1a], writes=[ind_all], short=700)
        s.op("vector", lambda e: e.tensor_tensor(v3(wv_all), v3(ind_all),
                                                 v3(s1a, 1).broadcast_to([128, 16, 16]),
                                                 ALU.mult),
             reads=[ind_all, s1a], writes=[wv_all], short=700)
    with b.scope():
        ebuf = b.sb("ebuf", [128, 16 * 128], F32)
        ebufb = b.sb("ebufb", [128, 16 * 128], BF16)
        qsc = b.sb("qsc", [128, 512], BF16)
        psq = [b.ps(f"psq{i}", [128, 512]) for i in range(2)]
        for och in range(16):
            s.dma("sync", ebuf[:, :].rearrange("p (k o) -> p k o", o=128),
                  wqkvT.ap().rearrange("(k p) o -> p k o", p=128)[:, :, och * 128:(och + 1) * 128],
                  reads=[wqkvT], writes=[ebuf])
            s.op("vector", lambda e: e.tensor_copy(ebufb[:, :], ebuf[:, :]),
                 reads=[ebuf], writes=[ebufb])
            for stt in range(4):
                sl = slice(stt * 512, stt * 512 + 512)
                pp = psq[stt % 2]
                for k in range(16):
                    s.op("tensor", lambda e, k=k, sl=sl, pp=pp: e.matmul(
                        pp[:, :], ebufb[:, k * 128:(k + 1) * 128], x_ln[k][:, sl],
                        start=(k == 0), stop=(k == 15)),
                         reads=[ebufb, x_ln[k]], writes=[pp])
                s.op("scalar", lambda e, pp=pp: e.mul(qsc[:, :], pp[:, :],
                                                      1.0 / math_sqrt128()),
                     reads=[pp], writes=[qsc])
                s.dma("scalar", q_fm_o.ap()[och * 128:(och + 1) * 128, sl], qsc[:, :],
                      reads=[qsc], writes=[q_fm_o])
    with b.scope():
        wbuf = b.sb("wbuf", [128, 16 * 512], F32)
        wbufb = [b.sb(f"wbufb{i}", [128, 16 * 512], BF16) for i in range(2)]
        ktf = b.sb("ktf", [128, 512], F32)
        ksacc = b.sb("ksacc", [128, 16 * 128], F32)
        vsacc = b.sb("vsacc", [128, 16 * 128], F32)
        kv257 = b.sb("kv257", [128, 257], BF16)
        psk = [b.ps(f"psk{i}", [128, 512]) for i in range(2)]
        for part, wsel in [(0, ind_all), (1, wv_all)]:
            acc = ksacc if part == 0 else vsacc
            for ob in range(4):
                wbb = wbufb[ob % 2]
                col0 = 2048 + part * 2048 + ob * 512
                s.dma("sync", wbuf[:, :].rearrange("p (k o) -> p k o", o=512),
                      wqkvT.ap().rearrange("(k p) o -> p k o", p=128)[:, :, col0:col0 + 512],
                      reads=[wqkvT], writes=[wbuf])
                s.op("vector", lambda e, wbb=wbb: e.tensor_copy(wbb[:, :], wbuf[:, :]),
                     reads=[wbuf], writes=[wbb])
                for tt in range(16):
                    t0 = tt * 128
                    pp = psk[tt % 2]
                    for k in range(16):
                        s.op("tensor", lambda e, k=k, t0=t0, pp=pp, wbb=wbb: e.matmul(
                            pp[:, :], x_ln[k][:, t0:t0 + 128],
                            wbb[:, k * 512:(k + 1) * 512],
                            start=(k == 0), stop=(k == 15)),
                             reads=[x_ln[k], wbb], writes=[pp])
                    s.op("vector", lambda e, pp=pp: e.tensor_copy(ktf[:, :], pp[:, :]),
                         reads=[pp], writes=[ktf])
                    for h4 in range(4):
                        h = ob * 4 + h4
                        scal = wsel[:, tt * 16 + h:tt * 16 + h + 1]
                        if ob == 0 and h4 == 0:
                            s.op("vector", lambda e, h4=h4, tt=tt, scal=scal, acc=acc:
                                 e.tensor_scalar(acc[:, tt * 128:(tt + 1) * 128],
                                                 ktf[:, h4 * 128:(h4 + 1) * 128],
                                                 scal, None, ALU.mult),
                                 reads=[ktf, wsel], writes=[acc])
                        else:
                            s.op("vector", lambda e, h4=h4, tt=tt, scal=scal, acc=acc:
                                 e.scalar_tensor_tensor(
                                     acc[:, tt * 128:(tt + 1) * 128],
                                     ktf[:, h4 * 128:(h4 + 1) * 128], scal,
                                     acc[:, tt * 128:(tt + 1) * 128],
                                     ALU.mult, ALU.add),
                                 reads=[ktf, wsel, acc], writes=[acc])
        for tt in range(16):
            t0 = tt * 128
            s.op("vector", lambda e, tt=tt: e.tensor_copy(
                kv257[:, 0:128], ksacc[:, tt * 128:(tt + 1) * 128]),
                 reads=[ksacc], writes=[kv257])
            s.op("vector", lambda e, tt=tt: e.tensor_copy(
                kv257[:, 128:256], vsacc[:, tt * 128:(tt + 1) * 128]),
                 reads=[vsacc], writes=[kv257])
            s.op("vector", lambda e: e.memset(kv257[:, 256:257], 1.0),
                 reads=[], writes=[kv257])
            s.dma("scalar", ksvs.ap()[t0:t0 + 128, :], kv257[:, :],
                  reads=[kv257], writes=[ksvs])
    return b.finish() if standalone else None


def math_sqrt128():
    return float(np.sqrt(128.0))



def build_l1f_attn(b=None, io=None):
    standalone = b is None
    if standalone:
        b = Build("l1f_attn")
    s = b.s
    if io is None:
        io = {"q_fm": b.din("q_fm", [D, T], BF16),
              "ksvsf": b.din("ksvsf", [S, 257], BF16),
              "xin": b.din("xin", [D, T], F32),
              "bandM": b.din("bandM", [128, 2048], F32),
              "x4": b.dout("x4", [D, T], F32)}
    q_fm_i, ksvsf, xin, bandM, x4 = (io["q_fm"], io["ksvsf"], io["xin"],
                                     io["bandM"], io["x4"])
    kvrow = io.get("kvrow", lambda kt: kt * 128)
    if standalone:
        b.load_consts()
    band = b.sb("band", [128, 2048], F32)
    s.dma("sync", band[:, :], bandM.ap(), reads=[bandM], writes=[band])
    ks_fm = b.sb("ksfm", [128, S], BF16)
    vs_sb = b.sb("vs", [128, 32 * 129], BF16)
    with b.scope():
        kseg = [b.sb(f"kseg{i}", [128, 128], BF16) for i in range(2)]
        pst = b.ps("pst", [128, 128], BF16)
        for kt in range(32):
            sg = kseg[kt % 2]
            s.dma("sync", sg[:, :], ksvsf.ap()[kvrow(kt):kvrow(kt) + 128, 0:128],
                  reads=[ksvsf], writes=[sg])
            s.op("tensor", lambda e, sg=sg: e.transpose(pst[:, :], sg[:, :], b.i128[:, :]),
                 reads=[sg, b.i128], writes=[pst])
            s.op("vector", lambda e, kt=kt: e.tensor_copy(
                ks_fm[:, kt * 128:(kt + 1) * 128], pst[:, :]),
                 reads=[pst], writes=[ks_fm])
            s.dma("scalar", vs_sb[:, kt * 129:(kt + 1) * 129],
                  ksvsf.ap()[kvrow(kt):kvrow(kt) + 128, 128:257],
                  reads=[ksvsf], writes=[vs_sb])
    with b.scope():
        q_sb = b.sb("qsb", [128, 16 * 512], BF16)
        probsT = [b.sb(f"pT{i}", [128, 512], BF16) for i in range(2)]
        sc_f = [b.sb(f"scf{i}", [128, 512], F32) for i in range(2)]
        rz = b.sb("rz", [1, 512], F32)
        zb_sb = b.sb("zbsb", [128, 512], F32)
        y_sb = b.sb("ysb", [128, 16 * 512], F32)
        xb = [b.sb(f"xb{i}", [128, 512], F32) for i in range(2)]
        x4t = b.sb("x4t", [128, 512], F32)
        ps_sc = [b.ps(f"pssc{i}", [128, 512]) for i in range(2)]
        ps_pv = b.ps("pspv", [128, 512])
        ps_z = b.ps("psz", [1, 512])
        ps_bc = b.ps("psbc", [128, 512])
        for qg in range(4):
            nkt = nkt_of(qg)
            qsl = slice(qg * 512, qg * 512 + 512)
            s.dma("sync", q_sb[:, :].rearrange("p (c t) -> p c t", t=512),
                  _rearr_cpT(q_fm_i.ap())[:, :, qsl], reads=[q_fm_i], writes=[q_sb])
            for h in range(16):
                def emit_pv2(kt, nkt=None):
                    pT = probsT[kt % 2]
                    s.op("tensor", lambda e, kt=kt, pT=pT, nkt=nkt: e.matmul(
                        ps_pv[:, :], vs_sb[:, kt * 129:kt * 129 + 128], pT[:, :],
                        start=(kt == 0), stop=(kt == nkt - 1)),
                         reads=[vs_sb, pT], writes=[ps_pv])
                    s.op("tensor", lambda e, kt=kt, pT=pT, nkt=nkt: e.matmul(
                        ps_z[:, :], vs_sb[:, kt * 129 + 128:kt * 129 + 129], pT[:, :],
                        start=(kt == 0), stop=(kt == nkt - 1)),
                         reads=[vs_sb, pT], writes=[ps_z])
                for kt in range(nkt):
                    pa = ps_sc[kt % 2]
                    sf = sc_f[kt % 2]
                    s.op("tensor", lambda e, pa=pa, kt=kt, h=h: e.matmul(
                        pa[:, :], ks_fm[:, kt * 128:(kt + 1) * 128],
                        q_sb[:, h * 512:(h + 1) * 512], start=True, stop=True),
                         reads=[ks_fm, q_sb], writes=[pa])
                    if kt >= 8 * qg:
                        s.op("vector", lambda e, pa=pa, sf=sf, qg=qg, kt=kt:
                             e.scalar_tensor_tensor(ps_view(sf), ps_view(pa), 1.0,
                                                    band_view(band, qg, kt),
                                                    ALU.mult, ALU.add),
                             reads=[pa, band], writes=[sf])
                        esrc = sf
                    else:
                        esrc = pa
                    pT = probsT[kt % 2]
                    s.op("scalar", lambda e, esrc=esrc, pT=pT: e.activation(
                        pT[:, :], esrc[:, :], AF.Exp),
                         reads=[esrc], writes=[pT])
                    if kt > 0:
                        emit_pv2(kt - 1, nkt=nkt)
                emit_pv2(nkt - 1, nkt=nkt)
                s.op("vector", lambda e: e.reciprocal(rz[0:1, :], ps_z[:, :]),
                     reads=[ps_z], writes=[rz])
                s.op("tensor", lambda e: e.matmul(ps_bc[:, :], b.onesr_f[:, :],
                                                  rz[0:1, :], start=True, stop=True),
                     reads=[b.onesr_f, rz], writes=[ps_bc])
                s.op("vector", lambda e: e.tensor_copy(zb_sb[:, :], ps_bc[:, :]),
                     reads=[ps_bc], writes=[zb_sb])
                s.op("vector", lambda e, h=h: e.tensor_mul(
                    y_sb[:, h * 512:(h + 1) * 512], ps_pv[:, :], zb_sb[:, :]),
                     reads=[ps_pv, zb_sb], writes=[y_sb])
            for ch in range(16):
                xbb = xb[ch % 2]
                s.dma("scalar", xbb[:, :], xin.ap()[ch * 128:(ch + 1) * 128, qsl],
                      reads=[xin], writes=[xbb])
                s.op("vector", lambda e, ch=ch, xbb=xbb: e.tensor_add(
                    x4t[:, :], y_sb[:, ch * 512:(ch + 1) * 512], xbb[:, :]),
                     reads=[y_sb, xbb], writes=[x4t])
                s.dma("scalar", x4.ap()[ch * 128:(ch + 1) * 128, qsl], x4t[:, :],
                      reads=[x4t], writes=[x4])
    return b.finish() if standalone else None


def host_rope_tables(c):
    half = c % 2
    pos = np.empty((128, 16), np.int64)
    for tt in range(16):
        pos[:, tt] = (2 * tt + half) * 128 + np.arange(128)
    inv = 1.0 / (10000.0 ** (np.arange(32) / 32.0))
    ang = pos[:, :, None].astype(np.float64) * inv[None, None, :]
    cosv = np.cos(ang); sinv = np.sin(ang)
    ct = np.empty((128, 16, 8, 64), np.float32)
    st = np.empty((128, 16, 8, 64), np.float32)
    ct[:, :, :, 0:32] = cosv[:, :, None, :]
    ct[:, :, :, 32:64] = cosv[:, :, None, :]
    st[:, :, :, 0:32] = -sinv[:, :, None, :]
    st[:, :, :, 32:64] = sinv[:, :, None, :]
    return (ct.reshape(128, 16 * 512).astype(BF16NP),
            st.reshape(128, 16 * 512).astype(BF16NP))


def host_inputs_l1(inputs, x2_arrs=None, resF=None, x3_arrs=None, resH1=None, stage="pre"):
    consts = mkconsts()
    maps = []
    if stage == "pre":
        for c in range(NCORE):
            ct, st = host_rope_tables(c)
            m = {"xin": x2_arrs[c], "wq": inputs["L1_attn_Wq"],
                 "wk": inputs["L1_attn_Wk"], "wv": inputs["L1_attn_Wv"],
                 "ctab": ct, "stab": st}
            m.update(consts)
            maps.append(m)
    elif stage == "attn":
        for (bb, half) in cores_bh():
            c = 2 * bb + half
            kvf = host_kv_global([resF[2 * bb]["kv1"], resF[2 * bb + 1]["kv1"]])
            m = {"q_tm": resF[c]["q_tm"], "kvf": kvf, "wo": inputs["L1_attn_Wo"],
                 "xin": x2_arrs[c], "bandM": np.ascontiguousarray(mk_band(half, "mask")[:, 0:2048])}
            m.update(consts)
            maps.append(m)
    elif stage == "fpre":
        wqkvT = np.ascontiguousarray(inputs["L1_ffn_W_qkv"].T)
        wgT_small = inputs["L1_ffn_W_gate"].T.reshape(16, 128, 16).transpose(1, 0, 2)
        wgT = np.zeros((128, 16, 128), np.float32)
        wgT[:, :, 0:16] = wgT_small
        wgT = np.ascontiguousarray(wgT.reshape(128, 16 * 128))
        for c in range(NCORE):
            m = {"xin": x3_arrs[c], "wqkvT": wqkvT, "wgT": wgT}
            m.update(consts)
            maps.append(m)
    elif stage == "fattn":
        for (bb, half) in cores_bh():
            c = 2 * bb + half
            ksvsf = host_kv_global([resH1[2 * bb]["ksvs"], resH1[2 * bb + 1]["ksvs"]])
            m = {"q_fm": resH1[c]["q_fm"], "ksvsf": ksvsf,
                 "xin": x3_arrs[c],
                 "bandM": np.ascontiguousarray(mk_band(half, "mask")[:, 0:2048])}
            m.update(consts)
            maps.append(m)
    return maps


def core_perm(c):
    half = c % 2
    return np.concatenate([np.arange((2 * j + half) * 128, (2 * j + half) * 128 + 128)
                           for j in range(16)])

def cores_bh():
    return [(c // 2, c % 2) for c in range(NCORE)]


def host_inputs_l0a_pre(inputs):
    x = inputs["L0_attn_x"]
    expT = np.ascontiguousarray(inputs["L0_attn_expand"].T)
    gam = np.ascontiguousarray(inputs["L0_attn_ln_gamma"].reshape(16, 128).T)
    bet = np.ascontiguousarray(inputs["L0_attn_ln_beta"].reshape(16, 128).T)
    consts = mkconsts()
    maps = []
    for (bb, half) in cores_bh():
        m = {"xin": np.ascontiguousarray(x[bb, core_perm(2 * bb + half), :].T),
             "expT": expT, "gam": gam, "bet": bet}
        m.update(consts)
        maps.append(m)
    return maps


def host_kv_global(kv_pair):
    ev, od = kv_pair
    W = ev.shape[1]
    out = np.empty((S, W), ev.dtype)
    out.reshape(32, 128, W)[0::2] = ev.reshape(16, 128, W)
    out.reshape(32, 128, W)[1::2] = od.reshape(16, 128, W)
    return out


def host_inputs_l0a_attn(inputs, resA):
    projT = np.ascontiguousarray(inputs["L0_attn_project"].T)
    x = inputs["L0_attn_x"]
    posm = np.full((1, 128), float(np.asarray(inputs["L0_attn_pos_mult"])), np.float32)
    consts = mkconsts()
    maps = []
    for (bb, half) in cores_bh():
        c = 2 * bb + half
        kvf = host_kv_global([resA[2 * bb]["kv"], resA[2 * bb + 1]["kv"]])
        Rb, Mb = mk_band(half, "alibi")
        m = {"qf": resA[c]["qf"], "kvf": kvf, "gl": resA[c]["gl"],
             "projT": projT,
             "xin": np.ascontiguousarray(x[bb, core_perm(c), :].T),
             "murho": resA[c]["murho"], "bandR": Rb, "bandM": Mb, "posm": posm}
        m.update(consts)
        maps.append(m)
    return maps



def _kvrow_ag(kt):
    return (kt % 2) * T + (kt // 2) * 128


def _cc_pair_gather(b, src, dst):
    b.s.op("gpsimd", lambda e: e.collective_compute(
        "AllGather", ALU.bypass,
        replica_groups=[[0, 1], [2, 3], [4, 5], [6, 7]],
        ins=[src.ap().opt()], outs=[dst.ap().opt()]),
        reads=[src], writes=[dst], kind="cc")


def build_fused():
    b = Build("fused")
    din, dint, dout = b.din, b.dint, b.dout
    xin = din("xin", [D, T], F32)
    expT = din("expT", [D, 8704], F32)
    gam = din("gam", [128, 16], F32)
    bet = din("bet", [128, 16], F32)
    projT = din("projT", [4096, D], F32)
    bandR = din("bandR", [128, BANDW], F32)
    bandM = din("bandM", [128, BANDW], F32)
    posm = din("posm", [1, 128], F32)
    gwT = din("gwT", [128, 16 * 128], F32)
    sgw = din("sgw", [128, 16], F32)
    wg = din("wg", [8 * D, I_MOE], F32)
    wu = din("wu", [8 * D, I_MOE], F32)
    swg = din("swg", [D, IS], F32)
    swu = din("swu", [D, IS], F32)
    wdf = din("wdf", [8 * I_MOE, D], F32)
    swdf = din("swdf", [IS, D], F32)
    wq = din("wq", [D, D], F32)
    wk = din("wk", [D, 512], F32)
    wv = din("wv", [D, 512], F32)
    ctab = din("ctab", [128, 16 * 512], BF16)
    stab = din("stab", [128, 16 * 512], BF16)
    wo = din("wo", [D, D], F32)
    bandM2 = din("bandM2", [128, 2048], F32)
    wqkvT = din("wqkvT", [D, 6144], F32)
    wgT2 = din("wgT2", [128, 16 * 128], F32)
    b.load_consts()
    kv = dint("i_kv", [T, 2304], BF16)
    kvg = dint("i_kvg", [S, 2304], BF16)
    qf = dint("i_qf", [256, T], BF16)
    gl = dint("i_gl", [D, T], BF16)
    murho = dint("i_murho", [2, T], F32)
    x1 = dint("i_x1", [D, T], F32)
    comb = dint("i_comb", [8, T], BF16)
    grow = dint("i_grow", [1, T], F32)
    hh = dint("i_hh", [8 * I_MOE, T], BF16)
    sh = dint("i_sh", [IS, T], BF16)
    wdb = dint("i_wdb", [8 * I_MOE, D], BF16)
    swdb = dint("i_swdb", [IS, D], BF16)
    x2 = dint("i_x2", [D, T], F32)
    q_tm = dint("i_qtm", [T, D], BF16)
    kv1 = dint("i_kv1", [T, 1024], BF16)
    kv1g = dint("i_kv1g", [S, 1024], BF16)
    x3 = dint("i_x3", [D, T], F32)
    q2 = dint("i_q2", [D, T], BF16)
    ksvs = dint("i_ksvs", [T, 257], BF16)
    ksvsg = dint("i_ksvsg", [S, 257], BF16)
    x4 = dout("x4", [D, T], F32)

    with b.scope():
        build_l0a_pre(b, io={"xin": xin, "expT": expT, "gam": gam, "bet": bet,
                             "kv": kv, "qf": qf, "gl": gl, "murho": murho})
    _cc_pair_gather(b, kv, kvg)
    with b.scope():
        build_l0a_attn(b, io={"qf": qf, "kvf": kvg, "gl": gl, "projT": projT,
                              "xin": xin, "murho": murho, "bandR": bandR,
                              "bandM": bandM, "posm": posm, "x1": x1,
                              "kvrow": _kvrow_ag})
    with b.scope():
        xln_sb = [b.sb(f"fxln{ch}", [128, T], BF16) for ch in range(16)]
        with b.scope():
            build_l0f_pre(b, io={"xin": x1, "gwT": gwT, "sgw": sgw,
                                 "comb": comb, "grow": grow, "xln_sb": xln_sb})
        with b.scope():
            build_l0f_hh(b, io={"wg": wg, "wu": wu, "comb": comb, "hh": hh,
                                "xln_sb": xln_sb})
        with b.scope():
            build_l0f_sh(b, io={"swg": swg, "swu": swu, "grow": grow,
                                "wdf": wdf, "swdf": swdf, "sh": sh, "wdb": wdb,
                                "swdb": swdb, "xln_sb": xln_sb})
    with b.scope():
        build_l0f_down(b, io={"hh": hh, "sh": sh, "wdb": wdb, "swdb": swdb,
                              "x1": x1, "x2": x2})
    with b.scope():
        build_l1a_pre(b, io={"xin": x2, "wq": wq, "wk": wk, "wv": wv,
                             "ctab": ctab, "stab": stab, "q_tm": q_tm, "kv1": kv1,
                             "after_kv": lambda: _cc_pair_gather(b, kv1, kv1g)})
    with b.scope():
        build_l1a_attn(b, io={"q_tm": q_tm, "kvf": kv1g, "wo": wo, "xin": x2,
                              "bandM": bandM2, "x3": x3, "kvrow": _kvrow_ag})
    with b.scope():
        build_l1f_pre(b, io={"xin": x3, "wqkvT": wqkvT, "wgT": wgT2,
                             "q_fm": q2, "ksvs": ksvs})
    _cc_pair_gather(b, ksvs, ksvsg)
    with b.scope():
        build_l1f_attn(b, io={"q_fm": q2, "ksvsf": ksvsg, "xin": x3,
                              "bandM": bandM2, "x4": x4, "kvrow": _kvrow_ag})
    return b.finish()


def host_inputs_fused(inputs):
    x = inputs["L0_attn_x"]
    consts = mkconsts()
    expT = np.ascontiguousarray(inputs["L0_attn_expand"].T)
    gam = np.ascontiguousarray(inputs["L0_attn_ln_gamma"].reshape(16, 128).T)
    bet = np.ascontiguousarray(inputs["L0_attn_ln_beta"].reshape(16, 128).T)
    projT = np.ascontiguousarray(inputs["L0_attn_project"].T)
    posm = np.full((1, 128), float(np.asarray(inputs["L0_attn_pos_mult"])), np.float32)
    gwT_small = inputs["L0_ffn_gate_w"].T.reshape(16, 128, 8).transpose(1, 0, 2)
    gwT = np.zeros((128, 16, 128), np.float32)
    gwT[:, :, 0:8] = gwT_small
    gwT = np.ascontiguousarray(gwT.reshape(128, 16 * 128))
    sgw = np.ascontiguousarray(inputs["L0_ffn_shared_gate_w"].reshape(16, 128).T)
    wg_ = np.ascontiguousarray(inputs["L0_ffn_w_gate"].reshape(8 * D, I_MOE))
    wu_ = np.ascontiguousarray(inputs["L0_ffn_w_up"].reshape(8 * D, I_MOE))
    wdf = np.ascontiguousarray(inputs["L0_ffn_w_down"].reshape(8 * I_MOE, D))
    wqkvT = np.ascontiguousarray(inputs["L1_ffn_W_qkv"].T)
    wgT2_small = inputs["L1_ffn_W_gate"].T.reshape(16, 128, 16).transpose(1, 0, 2)
    wgT2 = np.zeros((128, 16, 128), np.float32)
    wgT2[:, :, 0:16] = wgT2_small
    wgT2 = np.ascontiguousarray(wgT2.reshape(128, 16 * 128))
    maps = []
    for (bb, half) in cores_bh():
        c = 2 * bb + half
        Rb, Mb = mk_band(half, "alibi")
        ct, st = host_rope_tables(c)
        m = {"xin": np.ascontiguousarray(x[bb, core_perm(c), :].T),
             "expT": expT, "gam": gam, "bet": bet, "projT": projT,
             "bandR": Rb, "bandM": Mb, "posm": posm,
             "gwT": gwT, "sgw": sgw, "wg": wg_, "wu": wu_,
             "swg": inputs["L0_ffn_sw_gate"], "swu": inputs["L0_ffn_sw_up"],
             "wdf": wdf, "swdf": inputs["L0_ffn_sw_down"],
             "wq": inputs["L1_attn_Wq"], "wk": inputs["L1_attn_Wk"],
             "wv": inputs["L1_attn_Wv"], "ctab": ct, "stab": st,
             "wo": inputs["L1_attn_Wo"],
             "bandM2": np.ascontiguousarray(mk_band(half, "mask")[:, 0:2048]),
             "wqkvT": wqkvT, "wgT2": wgT2}
        m.update(consts)
        maps.append(m)
    return maps


_NC_CACHE = {}

def _get_nc(name, builder):
    if name not in _NC_CACHE:
        _NC_CACHE[name] = builder()
    return _NC_CACHE[name]


def kernel(**inputs):
    nc = _get_nc("fused", build_fused)
    res = run_bass_kernel_spmd(nc, host_inputs_fused(inputs),
                               core_ids=list(range(NCORE))).results
    out = np.empty((B, S, D), np.float32)
    for c in range(NCORE):
        out[c // 2, core_perm(c), :] = np.asarray(res[c]["x4"]).T
    return out


def kernel_unfused(**inputs):
    cores = list(range(NCORE))
    resA = run_bass_kernel_spmd(_get_nc("A", build_l0a_pre),
                                host_inputs_l0a_pre(inputs), core_ids=cores).results
    resB = run_bass_kernel_spmd(_get_nc("B", build_l0a_attn),
                                host_inputs_l0a_attn(inputs, resA), core_ids=cores).results
    x1 = [np.asarray(resB[c]["x1"]) for c in cores]
    resC = run_bass_kernel_spmd(_get_nc("C", build_l0f_pre),
                                host_inputs_l0f(inputs, x1, stage="pre"),
                                core_ids=cores).results
    resD = run_bass_kernel_spmd(_get_nc("D", build_l0f_hh),
                                host_inputs_l0f(inputs, x1, resC=resC, stage="hh"),
                                core_ids=cores).results
    resD2 = run_bass_kernel_spmd(_get_nc("D2", build_l0f_sh),
                                 host_inputs_l0f(inputs, x1, resC=resC, stage="sh"),
                                 core_ids=cores).results
    resE = run_bass_kernel_spmd(_get_nc("E", build_l0f_down),
                                host_inputs_l0f(inputs, x1, resD=resD, resD2=resD2,
                                                stage="down"),
                                core_ids=cores).results
    x2 = [np.asarray(resE[c]["x2"]) for c in cores]
    resF = run_bass_kernel_spmd(_get_nc("F", build_l1a_pre),
                                host_inputs_l1(inputs, x2_arrs=x2, stage="pre"),
                                core_ids=cores).results
    resG = run_bass_kernel_spmd(_get_nc("G", build_l1a_attn),
                                host_inputs_l1(inputs, x2_arrs=x2, resF=resF, stage="attn"),
                                core_ids=cores).results
    x3 = [np.asarray(resG[c]["x3"]) for c in cores]
    resH1 = run_bass_kernel_spmd(_get_nc("H1", build_l1f_pre),
                                 host_inputs_l1(inputs, x3_arrs=x3, stage="fpre"),
                                 core_ids=cores).results
    resH2 = run_bass_kernel_spmd(_get_nc("H2", build_l1f_attn),
                                 host_inputs_l1(inputs, x3_arrs=x3, resH1=resH1,
                                                stage="fattn"),
                                 core_ids=cores).results
    out = np.empty((B, S, D), np.float32)
    for c in cores:
        bb = c // 2
        out[bb, core_perm(c), :] = np.asarray(resH2[c]["x4"]).T
    return out


# version 13
# speedup vs baseline: 1.0003x; 1.0003x over previous
import numpy as np
import ml_dtypes
import concourse.bass as bass
import concourse.mybir as mybir
from concourse.bass_utils import run_bass_kernel_spmd
from contextlib import ExitStack

F32 = mybir.dt.float32
BF16 = mybir.dt.bfloat16
AF = mybir.ActivationFunctionType
ALU = mybir.AluOpType
AX = mybir.AxisListType
BF16NP = ml_dtypes.bfloat16

B, S, D = 4, 4096, 2048
T = 2048
NCORE = 8
EPS = 1e-5


ENGINES = ("tensor", "vector", "scalar", "gpsimd", "sync")

class _Op:
    __slots__ = ("eng", "stream", "fn", "reads", "writes", "idx", "inc", "value",
                 "needed", "short")
    def __init__(self, eng, stream, fn, reads, writes, idx, inc):
        self.eng, self.stream, self.fn = eng, stream, fn
        self.reads, self.writes = reads, writes
        self.idx, self.inc = idx, inc
        self.value = None
        self.needed = False
        self.short = 0

class Sched:
    def __init__(self, nc):
        self.nc = nc
        self.ops = []
        self.deps = {}
        self.writers = {}
        self.readers = {}
        self.last_barrier = []
        self.shapes = {}
        self.scalar_spacer = 0
        self.paranoid = False
        self.pads = {}
        self.dma_streams = {}
        self.dma_free = []
        self.n_dma_streams = 0

    def _track(self, op):
        d = set()
        for t in op.reads:
            for w in self.writers.get(id(t), self.last_barrier):
                d.add(w)
            self.readers.setdefault(id(t), []).append(op)
        for t in op.writes:
            for w in self.writers.get(id(t), self.last_barrier):
                d.add(w)
            for r in self.readers.get(id(t), ()):
                d.add(r)
            self.writers[id(t)] = [op]
            self.readers[id(t)] = []
        d.discard(op)
        self.deps[op] = d

    def op(self, eng, fn, reads=(), writes=(), kind="compute", short=0):
        if kind == "compute":
            stream, inc = eng, 1
        elif kind == "dma":
            key = id(writes[0]) if writes else "anon"
            stream = self.dma_streams.get(key)
            if stream is None:
                stream = (self.dma_free.pop() if self.dma_free
                          else f"dma{self.n_dma_streams}")
                if not self.dma_free and stream == f"dma{self.n_dma_streams}":
                    self.n_dma_streams += 1
                self.dma_streams[key] = stream
            inc = 16
        elif kind == "cc":
            stream, inc = "cc", 1
        else:
            raise ValueError(kind)
        o = _Op(eng, stream, fn, list(reads), list(writes), len(self.ops), inc)
        if short:
            o.short = short
        if kind == "compute" and not o.short:
            if eng == "gpsimd":
                o.short = 800
            elif eng == "scalar" and self.scalar_spacer:
                o.short = self.scalar_spacer
            if self.paranoid and eng in ("vector", "scalar"):
                o.short = 1024
            if not o.short:
                for t in o.writes:
                    shp = self.shapes.get(id(t))
                    if shp is not None and (int(np.prod(shp[1:])) < 128 or shp[0] <= 2):
                        o.short = 1024
                        break
        self.ops.append(o)
        self._track(o)
        return o

    def dma(self, eng, out_ap, in_ap, reads, writes, **kw):
        return self.op(eng, lambda e, o=out_ap, i=in_ap: e.dma_start(out=o, in_=i, **kw),
                       reads=reads, writes=writes, kind="dma")

    def barrier(self):
        used = sorted({o.eng for o in self.ops}) or ["sync"]
        dep_all = set()
        for tid in self.writers:
            dep_all.update(self.writers[tid])
        for tid in self.readers:
            dep_all.update(self.readers[tid])
        barws = []
        for eng in used:
            pad = self.pads.get(eng)
            if pad is not None:
                fn = (lambda e, p=pad: e.mul(p[:, :], p[:, :], 1.0)) if eng == "scalar" \
                    else (lambda e, p=pad: e.tensor_scalar(p[:, :], p[:, :], 1.0,
                                                           None, ALU.mult))
            else:
                fn = lambda e: e.nop()
            o = self.op(eng, fn, reads=[], writes=[])
            self.deps[o] = set(d for d in dep_all if d is not o)
            barws.append(o)
        for tid in list(self.writers.keys()):
            self.writers[tid] = list(barws)
        for tid in list(self.readers.keys()):
            self.readers[tid] = []
        self.last_barrier = list(barws)
        self.dma_free.extend(self.dma_streams.values())
        self.dma_streams = {}

    def emit(self):
        nc = self.nc
        for o, ds in self.deps.items():
            for d in ds:
                d.needed = True
        streams = sorted({o.stream for o in self.ops})
        last_per_stream = {}
        for o in self.ops:
            last_per_stream[o.stream] = o
        for o in last_per_stream.values():
            o.needed = True
        counts = {s: 0 for s in streams}
        for o in self.ops:
            if o.needed:
                counts[o.stream] += o.inc
                o.value = counts[o.stream]
        finals = dict(counts)
        with ExitStack() as ctx:
            sems = {s: ctx.enter_context(nc.semaphore(f"sem_{s.replace('.', '_')}"))
                    for s in streams}
            block = ctx.enter_context(nc.Block())
            by_eng = {}
            for o in self.ops:
                by_eng.setdefault(o.eng, []).append(o)

            def make_body(eng, ops):
                def body(e):
                    hwm = {}
                    for o in ops:
                        for d in sorted(self.deps[o], key=lambda x: x.idx):
                            if d.stream == eng and d.inc == 1:
                                continue
                            assert d.value is not None
                            if hwm.get(d.stream, 0) >= d.value:
                                continue
                            e.wait_ge(sems[d.stream], d.value)
                            hwm[d.stream] = d.value
                        inst = o.fn(e)
                        pad = self.pads.get(eng)
                        if o.short and pad is not None:
                            if eng == "scalar":
                                inst = e.mul(pad[:, :], pad[:, :], 1.0)
                            else:
                                inst = e.tensor_scalar(pad[:, :], pad[:, :], 1.0,
                                                       None, ALU.mult)
                        if o.needed:
                            inst.then_inc(sems[o.stream], o.inc)
                    for s in streams:
                        if finals[s] > hwm.get(s, 0) and not (s == eng):
                            e.wait_ge(sems[s], finals[s])
                return body
            for eng in ENGINES:
                if eng in by_eng:
                    getattr(block, eng)(make_body(eng, by_eng[eng]))
        return nc



class Build:
    def __init__(self, name):
        self.nc = bass.Bass("TRN2", target_bir_lowering=False)
        self.s = Sched(self.nc)
        self.ctx = ExitStack()
        self.n_sb = 0

    def _uniq(self, name):
        self.n_sb += 1
        return f"{name}_{self.n_sb}"

    def sb(self, name, shape, dt):
        t = self.ctx.enter_context(self.nc.sbuf_tensor(self._uniq(name), list(shape), dt))
        self.s.shapes[id(t)] = list(shape)
        return t

    def ps(self, name, shape, dt=F32):
        t = self.ctx.enter_context(self.nc.psum_tensor(self._uniq(name), list(shape), dt))
        self.s.shapes[id(t)] = list(shape)
        return t

    def din(self, name, shape, dt):
        t = self.nc.dram_tensor(name, list(shape), dt, kind="ExternalInput")
        self.s.shapes[id(t)] = list(shape)
        return t

    def dout(self, name, shape, dt):
        t = self.nc.dram_tensor(name, list(shape), dt, kind="ExternalOutput")
        self.s.shapes[id(t)] = list(shape)
        return t

    def dint(self, name, shape, dt):
        t = self.nc.dram_tensor(name, list(shape), dt)
        self.s.shapes[id(t)] = list(shape)
        return t

    def finish(self):
        self.s.emit()
        self.ctx.close()
        return self.nc

    from contextlib import contextmanager as _cm

    @_cm
    def scope(self):
        outer = self.ctx
        inner = ExitStack()
        self.ctx = inner
        try:
            yield
        finally:
            inner.close()
            self.ctx = outer
            self.s.barrier()

    def load_consts(self):
        self.i128_d = self.din("c_i128", [128, 128], BF16)
        self.onesr_b_d = self.din("c_onesr_b", [1, 128], BF16)
        self.onesc_b_d = self.din("c_onesc_b", [128, 1], BF16)
        self.onesr_f_d = self.din("c_onesr_f", [1, 128], F32)
        self.misc1_d = self.din("c_misc1", [1, 8], F32)
        self.misc128_d = self.din("c_misc128", [128, 4], F32)
        self.i128 = self.sb("i128", [128, 128], BF16)
        self.onesr_b = self.sb("onesr_b", [1, 128], BF16)
        self.onesc_b = self.sb("onesc_b", [128, 1], BF16)
        self.onesr_f = self.sb("onesr_f", [1, 128], F32)
        self.misc1 = self.sb("misc1", [1, 8], F32)
        self.misc128 = self.sb("misc128", [128, 4], F32)
        for eng in ("vector", "scalar", "gpsimd"):
            self.s.pads[eng] = self.sb(f"pad_{eng}", [128, 512], F32)
        s = self.s
        for dst, src in [(self.i128, self.i128_d), (self.onesr_b, self.onesr_b_d),
                         (self.onesc_b, self.onesc_b_d), (self.onesr_f, self.onesr_f_d),
                         (self.misc1, self.misc1_d), (self.misc128, self.misc128_d)]:
            s.dma("sync", dst[:, :], src.ap(), reads=[src], writes=[dst])

    def bcast_row_f32(self, row_ap, psum, dst, row_reads):
        s = self.s
        s.op("tensor", lambda e, r=row_ap: e.matmul(psum[:, :], self.onesr_f[:, :], r,
                                                    start=True, stop=True),
             reads=[self.onesr_f] + row_reads, writes=[psum])
        s.op("vector", lambda e: e.tensor_copy(dst[:, :], psum[:, :]),
             reads=[psum], writes=[dst])

def mkconsts():
    return {
        "c_i128": np.eye(128, dtype=BF16NP),
        "c_onesr_b": np.ones((1, 128), BF16NP),
        "c_onesc_b": np.ones((128, 1), BF16NP),
        "c_onesr_f": np.ones((1, 128), np.float32),
        "c_misc1": np.array([[1.0, EPS, EPS * EPS, 0, 0, 0, 0, 0]], np.float32),
        "c_misc128": np.tile(np.array([[1.0, EPS, EPS * EPS, 0]], np.float32), (128, 1)),
    }

def _rearr_cpT(dram_ap, p=128):
    return dram_ap.rearrange("(c p) t -> p c t", p=p)



def emit_layernorm(b: Build, x_dram, x_ln, gamma_beta=None, rho_mode="single",
                   want_r1=False, tag=""):
    s = b.s
    nst = T // 512
    x_stts = [b.sb(f"lnx{tag}", [128, 16 * 512], F32) for _ in range(2)]
    x_bfs = [b.sb(f"lnxb{tag}", [128, 16 * 512], BF16)] * 2
    xsq_bfs = [b.sb(f"lnxs{tag}", [128, 16 * 512], BF16)] * 2
    _rows = b.sb(f"lnrows{tag}", [1, 6 * 512], F32)
    mu_row = _rows[0:1, 0:512]
    var_row = _rows[0:1, 512:1024]
    rho_row = _rows[0:1, 1024:1536]
    nurho_row = _rows[0:1, 1536:2048]
    tmp_row = _rows[0:1, 2048:2560]
    tmp2_row = _rows[0:1, 2560:3072]
    r1_row = b.sb(f"lnr1{tag}", [1, T], F32) if want_r1 else None
    nu1_row = b.sb(f"lnnu1{tag}", [1, T], F32) if want_r1 else None
    ps_sum = b.ps(f"lnps1{tag}", [1, 512])
    ps_sq = b.ps(f"lnps2{tag}", [1, 512])
    ps_bc1 = b.ps(f"lnbc1{tag}", [128, 512])
    ps_bc2 = b.ps(f"lnbc2{tag}", [128, 512])
    rho_bc = b.sb(f"lnrbc{tag}", [128, 512], F32)
    nur_bc = b.sb(f"lnnbc{tag}", [128, 512], F32)
    u1 = b.sb(f"lnu1{tag}", [128, 512], F32)
    xv = _rearr_cpT(x_dram.ap())
    for stt in range(nst):
        x_stt, x_bf, xsq_bf = x_stts[stt % 2], x_bfs[stt % 2], xsq_bfs[stt % 2]
        sl = slice(stt * 512, stt * 512 + 512)
        s.dma("sync", x_stt[:, :].rearrange("p (c t) -> p c t", t=512),
              xv[:, :, sl], reads=[x_dram], writes=[x_stt])
        s.op("vector", lambda e, x_bf=x_bf, x_stt=x_stt: e.tensor_copy(
            x_bf[:, :], x_stt[:, :]),
             reads=[x_stt], writes=[x_bf])
        s.op("scalar", lambda e, xsq_bf=xsq_bf, x_stt=x_stt: e.activation(
            xsq_bf[:, :], x_stt[:, :], AF.Square),
             reads=[x_stt], writes=[xsq_bf])
        for ch in range(16):
            c0 = ch * 512
            s.op("tensor", lambda e, c0=c0, st=(ch == 0), x_bf=x_bf: e.matmul(
                ps_sum[:, :], b.onesc_b[:, :], x_bf[:, c0:c0 + 512],
                start=st, stop=(c0 == 15 * 512)),
                 reads=[b.onesc_b, x_bf], writes=[ps_sum])
        for ch in range(16):
            c0 = ch * 512
            s.op("tensor", lambda e, c0=c0, st=(ch == 0), xsq_bf=xsq_bf: e.matmul(
                ps_sq[:, :], b.onesc_b[:, :], xsq_bf[:, c0:c0 + 512],
                start=st, stop=(c0 == 15 * 512)),
                 reads=[b.onesc_b, xsq_bf], writes=[ps_sq])
        s.op("vector", lambda e, sl=sl: e.tensor_scalar(mu_row[0:1, :], ps_sum[:, :],
                                                        1.0 / D, None, ALU.mult),
             reads=[ps_sum], writes=[mu_row])
        s.op("scalar", lambda e, sl=sl: e.activation(tmp_row[0:1, :], mu_row[0:1, :],
                                                     AF.Square),
             reads=[mu_row], writes=[tmp_row])
        s.op("vector", lambda e, sl=sl: e.scalar_tensor_tensor(
            var_row[0:1, :], ps_sq[:, :], 1.0 / D, tmp_row[0:1, :],
            ALU.mult, ALU.subtract),
             reads=[ps_sq, tmp_row], writes=[var_row])
        if rho_mode == "double":
            s.op("scalar", lambda e, sl=sl: e.activation(
                tmp_row[0:1, :], var_row[0:1, :], AF.Sqrt,
                bias=b.misc1[0:1, 2:3], scale=1.0 + EPS),
                 reads=[var_row, b.misc1], writes=[tmp_row])
        else:
            s.op("scalar", lambda e, sl=sl: e.activation(
                tmp_row[0:1, :], var_row[0:1, :], AF.Sqrt, bias=b.misc1[0:1, 1:2]),
                 reads=[var_row, b.misc1], writes=[tmp_row])
        s.op("vector", lambda e, sl=sl: e.reciprocal(rho_row[0:1, :], tmp_row[0:1, :]),
             reads=[tmp_row], writes=[rho_row])
        s.op("vector", lambda e, sl=sl: e.tensor_mul(nurho_row[0:1, :],
                                                     mu_row[0:1, :], rho_row[0:1, :]),
             reads=[mu_row, rho_row], writes=[nurho_row])
        if want_r1:
            s.op("scalar", lambda e, sl=sl: e.activation(
                tmp2_row[0:1, :], var_row[0:1, :], AF.Sqrt, bias=b.misc1[0:1, 1:2]),
                 reads=[var_row, b.misc1], writes=[tmp2_row])
            s.op("vector", lambda e, sl=sl: e.reciprocal(r1_row[0:1, sl], tmp2_row[0:1, :]),
                 reads=[tmp2_row], writes=[r1_row])
            s.op("vector", lambda e, sl=sl: e.tensor_mul(nu1_row[0:1, sl],
                                                         mu_row[0:1, :], r1_row[0:1, sl]),
                 reads=[mu_row, r1_row], writes=[nu1_row])
        s.op("tensor", lambda e, sl=sl: e.matmul(ps_bc1[:, :], b.onesr_f[:, :],
                                                 rho_row[0:1, :], start=True, stop=True),
             reads=[b.onesr_f, rho_row], writes=[ps_bc1])
        s.op("vector", lambda e: e.tensor_copy(rho_bc[:, :], ps_bc1[:, :]),
             reads=[ps_bc1], writes=[rho_bc])
        s.op("tensor", lambda e, sl=sl: e.matmul(ps_bc2[:, :], b.onesr_f[:, :],
                                                 nurho_row[0:1, :], start=True, stop=True),
             reads=[b.onesr_f, nurho_row], writes=[ps_bc2])
        s.op("vector", lambda e: e.tensor_copy(nur_bc[:, :], ps_bc2[:, :]),
             reads=[ps_bc2], writes=[nur_bc])
        for ch in range(16):
            c0 = ch * 512
            s.op("vector", lambda e, c0=c0, x_stt=x_stt: e.tensor_mul(
                u1[:, :], x_stt[:, c0:c0 + 512], rho_bc[:, :]),
                 reads=[x_stt, rho_bc], writes=[u1])
            if gamma_beta is None:
                s.op("vector", lambda e, c0=c0, sl=sl, ch=ch: e.tensor_sub(
                    x_ln[ch][:, sl], u1[:, :], nur_bc[:, :]),
                     reads=[u1, nur_bc], writes=[x_ln[ch]])
            else:
                gfm, bfm = gamma_beta
                s.op("vector", lambda e, c0=c0: e.tensor_sub(u1[:, :], u1[:, :],
                                                             nur_bc[:, :]),
                     reads=[u1, nur_bc], writes=[u1])
                s.op("vector", lambda e, sl=sl, ch=ch: e.tensor_scalar(
                    x_ln[ch][:, sl], u1[:, :], gfm[:, ch:ch + 1], bfm[:, ch:ch + 1],
                    ALU.mult, ALU.add),
                     reads=[u1, gfm, bfm], writes=[x_ln[ch]])
    return r1_row, nu1_row



def build_l0a_pre(b=None, io=None):
    standalone = b is None
    if standalone:
        b = Build("l0a_pre")
    s = b.s
    if io is None:
        io = {"xin": b.din("xin", [D, T], F32), "expT": b.din("expT", [D, 8704], F32),
              "gam": b.din("gam", [128, 16], F32), "bet": b.din("bet", [128, 16], F32),
              "kv": b.dout("kv", [T, 2304], BF16), "qf": b.dout("qf", [256, T], BF16),
              "gl": b.dout("gl", [D, T], BF16), "murho": b.dout("murho", [2, T], F32)}
    xin, expT, gam, bet = io["xin"], io["expT"], io["gam"], io["bet"]
    kv, qf, gl, murho = io["kv"], io["qf"], io["gl"], io["murho"]
    if standalone:
        b.load_consts()
    gfm = b.sb("gfm", [128, 16], F32)
    bfm = b.sb("bfm", [128, 16], F32)
    s.dma("sync", gfm[:, :], gam.ap(), reads=[gam], writes=[gfm])
    s.dma("sync", bfm[:, :], bet.ap(), reads=[bet], writes=[bfm])
    x_ln = [b.sb(f"xln{ch}", [128, T], BF16) for ch in range(16)]
    with b.scope():
        r1_row, nu1_row = emit_layernorm(b, xin, x_ln, gamma_beta=(gfm, bfm),
                                         rho_mode="double", want_r1=True)
        s.dma("sync", murho.ap()[0:1, :], r1_row[0:1, :], reads=[r1_row], writes=[murho])
        s.dma("sync", murho.ap()[1:2, :], nu1_row[0:1, :], reads=[nu1_row], writes=[murho])

    expTv = expT.ap()
    with b.scope():
        ebufAs = [b.sb("ebA", [128, 16 * 128], F32) for _ in range(2)]
        ebufAbs = [b.sb("ebAb", [128, 16 * 128], BF16) for _ in range(2)]
        ebufBs = [b.sb("ebB", [128, 16 * 128], F32) for _ in range(2)]
        ebufBbs = [b.sb("ebBb", [128, 16 * 128], BF16) for _ in range(2)]
        psA = [b.ps(f"psA{i}", [128, 512]) for i in range(2)]
        psB = [b.ps(f"psB{i}", [128, 512]) for i in range(2)]
        gsc = b.sb("gsc", [128, 512], F32)
        qsc = b.sb("qsc", [128, 512], BF16)
        glsc = b.sb("glsc", [128, 512], BF16)

        def load_ecol(buf, bufb, col0):
            s.dma("sync", buf[:, :].rearrange("p (k o) -> p k o", o=128),
                  expTv.rearrange("(k p) o -> p k o", p=128)[:, :, col0:col0 + 128],
                  reads=[expT], writes=[buf])
            s.op("vector", lambda e, buf=buf, bufb=bufb: e.tensor_copy(bufb[:, :], buf[:, :]),
                 reads=[buf], writes=[bufb])

        for oc in range(2):
            ebufAb = ebufAbs[oc % 2]
            load_ecol(ebufAs[oc % 2], ebufAb, oc * 128)
            for stt in range(4):
                sl = slice(stt * 512, stt * 512 + 512)
                pa = psA[stt % 2]
                for k in range(16):
                    s.op("tensor", lambda e, k=k, sl=sl, pa=pa, ebufAb=ebufAb: e.matmul(
                        pa[:, :], ebufAb[:, k * 128:(k + 1) * 128], x_ln[k][:, sl],
                        start=(k == 0), stop=(k == 15)),
                         reads=[ebufAb, x_ln[k]], writes=[pa])
                s.op("scalar", lambda e, pa=pa: e.mul(qsc[:, :], pa[:, :], 1.0 / 16.0),
                     reads=[pa], writes=[qsc])
                s.dma("scalar", qf.ap()[oc * 128:(oc + 1) * 128, sl], qsc[:, :],
                      reads=[qsc], writes=[qf])
        for j in range(16):
            ebufAb, ebufBb = ebufAbs[j % 2], ebufBbs[j % 2]
            load_ecol(ebufAs[j % 2], ebufAb, 512 + j * 128)
            load_ecol(ebufBs[j % 2], ebufBb, 4608 + j * 128)
            for stt in range(4):
                sl = slice(stt * 512, stt * 512 + 512)
                pa, pb = psA[stt % 2], psB[stt % 2]
                for k in range(16):
                    s.op("tensor", lambda e, k=k, sl=sl, pa=pa, ebufAb=ebufAb: e.matmul(
                        pa[:, :], ebufAb[:, k * 128:(k + 1) * 128], x_ln[k][:, sl],
                        start=(k == 0), stop=(k == 15)),
                         reads=[ebufAb, x_ln[k]], writes=[pa])
                for k in range(16):
                    s.op("tensor", lambda e, k=k, sl=sl, pb=pb, ebufBb=ebufBb: e.matmul(
                        pb[:, :], ebufBb[:, k * 128:(k + 1) * 128], x_ln[k][:, sl],
                        start=(k == 0), stop=(k == 15)),
                         reads=[ebufBb, x_ln[k]], writes=[pb])
                s.op("scalar", lambda e, pb=pb: e.activation(gsc[:, :], pb[:, :], AF.Gelu),
                     reads=[pb], writes=[gsc], short=500)
                s.op("vector", lambda e, pa=pa: e.tensor_mul(glsc[:, :], pa[:, :], gsc[:, :]),
                     reads=[pa, gsc], writes=[glsc])
                s.dma("scalar", gl.ap()[j * 128:(j + 1) * 128, sl], glsc[:, :],
                      reads=[glsc], writes=[gl])

    with b.scope():
        ebigFs = [b.sb("egF", [128, 16 * 512], F32) for _ in range(2)]
        ebigAbs = [b.sb("egAb", [128, 16 * 512], BF16) for _ in range(2)]
        ebigBbs = [b.sb("egBb", [128, 16 * 512], BF16) for _ in range(2)]
        psA = [b.ps(f"psA{i}", [128, 512]) for i in range(2)]
        psB = [b.ps(f"psB{i}", [128, 512]) for i in range(2)]
        gsc = b.sb("gsc", [128, 512], F32)
        ksc = b.sb("ksc", [128, 256], BF16)
        vsc = b.sb("vsc", [128, 512], BF16)
        psK = b.ps("psK", [128, 256])

        def load_ebig(bufb, col0, width, par):
            ebigF = ebigFs[par]
            s.dma("sync", ebigF[:, 0:16 * width].rearrange("p (k o) -> p k o", o=width),
                  expTv.rearrange("(k p) o -> p k o", p=128)[:, :, col0:col0 + width],
                  reads=[expT], writes=[ebigF])
            s.op("vector", lambda e, w=width, bufb=bufb, ebigF=ebigF: e.tensor_copy(
                bufb[:, 0:16 * w], ebigF[:, 0:16 * w]),
                 reads=[ebigF], writes=[bufb])

        ebigAb = ebigAbs[0]
        load_ebig(ebigAb, 256, 256, 0)
        for tt in range(16):
            t0 = tt * 128
            for k in range(16):
                s.op("tensor", lambda e, k=k, t0=t0, ebigAb=ebigAb: e.matmul(
                    psK[:, :], x_ln[k][:, t0:t0 + 128], ebigAb[:, k * 256:(k + 1) * 256],
                    start=(k == 0), stop=(k == 15)),
                     reads=[x_ln[k], ebigAb], writes=[psK])
            s.op("vector", lambda e: e.tensor_copy(ksc[:, :], psK[:, :]),
                 reads=[psK], writes=[ksc])
            s.dma("scalar", kv.ap()[t0:t0 + 128, 0:256], ksc[:, :], reads=[ksc], writes=[kv])
        for j in range(4):
            ebigAb, ebigBb = ebigAbs[j % 2], ebigBbs[j % 2]
            load_ebig(ebigAb, 2560 + j * 512, 512, j % 2)
            load_ebig(ebigBb, 6656 + j * 512, 512, (j + 1) % 2)
            for tt in range(16):
                t0 = tt * 128
                pa, pb = psA[tt % 2], psB[tt % 2]
                for k in range(16):
                    s.op("tensor", lambda e, k=k, t0=t0, pa=pa, ebigAb=ebigAb: e.matmul(
                        pa[:, :], x_ln[k][:, t0:t0 + 128], ebigAb[:, k * 512:(k + 1) * 512],
                        start=(k == 0), stop=(k == 15)),
                         reads=[x_ln[k], ebigAb], writes=[pa])
                for k in range(16):
                    s.op("tensor", lambda e, k=k, t0=t0, pb=pb, ebigBb=ebigBb: e.matmul(
                        pb[:, :], x_ln[k][:, t0:t0 + 128], ebigBb[:, k * 512:(k + 1) * 512],
                        start=(k == 0), stop=(k == 15)),
                         reads=[x_ln[k], ebigBb], writes=[pb])
                s.op("scalar", lambda e, pb=pb: e.activation(gsc[:, :], pb[:, :], AF.Gelu),
                     reads=[pb], writes=[gsc], short=500)
                s.op("vector", lambda e, pa=pa: e.tensor_mul(vsc[:, :], pa[:, :], gsc[:, :]),
                     reads=[pa, gsc], writes=[vsc])
                s.dma("scalar", kv.ap()[t0:t0 + 128, 256 + j * 512:256 + (j + 1) * 512],
                      vsc[:, :], reads=[vsc], writes=[kv])
    return b.finish() if standalone else None



BANDW = 5120
OFFC = 1024

def band_view(band, qg, kt):
    colbase = (8 * qg - kt) * 128 + OFFC
    return band[:, colbase:colbase + 1024].rearrange(
        "p (j t u) -> p j t u", t=2, u=128)[:, :, 0:1, :]

def ps_view(pa):
    return pa[:, 0:512].rearrange("p (j t u) -> p j t u", t=1, u=128)

def mk_band(half, kind):
    jj = np.arange(BANDW)[None, :]
    ii = np.arange(128)[:, None]
    rel = (ii + OFFC - jj - 128 * half).astype(np.float32)
    M = np.where(rel <= 0, 0.0, -60000.0).astype(np.float32)
    if kind == "alibi":
        R = np.minimum(rel, 0.0).astype(np.float32)
        return R, M
    return M

def nkt_of(qg):
    return 8 * qg + 8


def build_l0a_attn(b=None, io=None):
    standalone = b is None
    if standalone:
        b = Build("l0a_attn")
    s = b.s
    if io is None:
        io = {"qf": b.din("qf", [256, T], BF16), "kvf": b.din("kvf", [S, 2304], BF16),
              "gl": b.din("gl", [D, T], BF16), "projT": b.din("projT", [4096, D], F32),
              "xin": b.din("xin", [D, T], F32), "murho": b.din("murho", [2, T], F32),
              "bandR": b.din("bandR", [128, BANDW], F32),
              "bandM": b.din("bandM", [128, BANDW], F32),
              "posm": b.din("posm", [1, 128], F32), "x1": b.dout("x1", [D, T], F32)}
    qf, kvf, gl, projT, xin = io["qf"], io["kvf"], io["gl"], io["projT"], io["xin"]
    murho, bandR, bandM, posm, x1 = (io["murho"], io["bandR"], io["bandM"],
                                     io["posm"], io["x1"])
    kvrow = io.get("kvrow", lambda kt: kt * 128)
    if standalone:
        b.load_consts()

    band = b.sb("band", [128, BANDW], F32)
    with b.scope():
        mstage = b.sb("mstage", [128, BANDW], F32)
        pm = b.sb("pm", [1, 128], F32)
        sp_col = b.sb("spcol", [128, 1], F32)
        t1 = b.sb("t1", [128, 1], F32)
        t2 = b.sb("t2", [128, 1], F32)
        psb = b.ps("psb", [128, 128])
        s.dma("sync", band[:, :], bandR.ap(), reads=[bandR], writes=[band])
        s.dma("sync", mstage[:, :], bandM.ap(), reads=[bandM], writes=[mstage])
        s.dma("sync", pm[:, :], posm.ap(), reads=[posm], writes=[pm])
        s.op("tensor", lambda e: e.matmul(psb[:, :], b.onesr_f[:, :], pm[0:1, :],
                                          start=True, stop=True),
             reads=[b.onesr_f, pm], writes=[psb])
        s.op("vector", lambda e: e.tensor_copy(t1[:, :], psb[:, 0:1]),
             reads=[psb], writes=[t1])
        s.op("scalar", lambda e: e.activation(t2[:, :], t1[:, :], AF.Abs),
             reads=[t1], writes=[t2])
        s.op("scalar", lambda e: e.activation(t2[:, :], t2[:, :], AF.Exp, scale=-1.0),
             reads=[t2], writes=[t2])
        s.op("scalar", lambda e: e.activation(t2[:, :], t2[:, :], AF.Ln,
                                              bias=b.misc128[:, 0:1]),
             reads=[t2, b.misc128], writes=[t2])
        s.op("vector", lambda e: e.tensor_scalar(sp_col[:, :], t1[:, :], 0.0, None,
                                                 ALU.max),
             reads=[t1], writes=[sp_col])
        s.op("vector", lambda e: e.tensor_add(sp_col[:, :], sp_col[:, :], t2[:, :]),
             reads=[sp_col, t2], writes=[sp_col])
        s.op("vector", lambda e: e.tensor_scalar(band[:, :], band[:, :], sp_col[:, :],
                                                 None, ALU.mult),
             reads=[band, sp_col], writes=[band])
        s.op("vector", lambda e: e.tensor_add(band[:, :], band[:, :], mstage[:, :]),
             reads=[band, mstage], writes=[band])

    k_fm = [b.sb(f"kfm{c}", [128, S], BF16) for c in range(2)]
    q_fm = [b.sb(f"qfm{c}", [128, T], BF16) for c in range(2)]
    r1p1_row = b.sb("r1p1", [1, T], F32)
    nu1_row = b.sb("nu1", [1, T], F32)
    with b.scope():
        kseg = [b.sb(f"kseg{i}", [128, 256], BF16) for i in range(2)]
        pst = b.ps("pst", [128, 128], BF16)
        for c in range(2):
            s.dma("sync", q_fm[c][:, :], qf.ap()[c * 128:(c + 1) * 128, :],
                  reads=[qf], writes=[q_fm[c]])
        s.dma("sync", r1p1_row[0:1, :], murho.ap()[0:1, :], reads=[murho], writes=[r1p1_row])
        s.dma("sync", nu1_row[0:1, :], murho.ap()[1:2, :], reads=[murho], writes=[nu1_row])
        s.op("vector", lambda e: e.tensor_scalar(r1p1_row[0:1, :], r1p1_row[0:1, :],
                                                 1.0, None, ALU.add),
             reads=[r1p1_row], writes=[r1p1_row])
        for kt in range(32):
            sg = kseg[kt % 2]
            s.dma("sync", sg[:, :], kvf.ap()[kvrow(kt):kvrow(kt) + 128, 0:256],
                  reads=[kvf], writes=[sg])
            for c in range(2):
                s.op("tensor", lambda e, sg=sg, c=c: e.transpose(
                    pst[:, :], sg[:, c * 128:(c + 1) * 128], b.i128[:, :]),
                     reads=[sg, b.i128], writes=[pst])
                s.op("vector", lambda e, kt=kt, c=c: e.tensor_copy(
                    k_fm[c][:, kt * 128:(kt + 1) * 128], pst[:, :]),
                     reads=[pst], writes=[k_fm[c]])

    with b.scope():
        probsT = [b.sb(f"pT{i}", [128, 512], BF16) for i in range(32)]
        ps_sc = [b.ps(f"pssc{i}", [128, 512]) for i in range(2)]
        ps_z = b.ps("psz", [1, 512])
        ps_oh = [b.ps(f"psoh{i}", [128, 1024]) for i in range(2)]
        ps_t = b.ps("pst", [128, 512], BF16)
        sc_f = [b.sb(f"scf{i}", [128, 512], F32) for i in range(2)]
        vbuf = [b.sb(f"vb{i}", [128, 2048], BF16) for i in range(2)]
        zrow = b.sb("zrow", [1, 512], F32)
        rz = b.sb("rz", [1, 512], F32)
        rzc = b.sb("rzc", [128, 1], F32)
        atm2 = [b.sb(f"attm{i}", [128, 2048], BF16) for i in range(2)]
        afm = b.sb("afm", [128, 16 * 512], BF16)
        glbuf = b.sb("glbuf", [128, 16 * 512], BF16)
        pbufF = b.sb("pbufF", [128, 32 * 128], F32)
        pbufB = b.sb("pbufB", [128, 32 * 128], BF16)
        xbuf = [b.sb(f"xb{i}", [128, 512], F32) for i in range(2)]
        x1t = b.sb("x1t", [128, 512], F32)
        rb_bc = b.sb("rbbc", [128, 512], F32)
        nb_bc = b.sb("nbbc", [128, 512], F32)
        for qg in range(4):
            nkt = nkt_of(qg)
            qsl = slice(qg * 512, qg * 512 + 512)
            for kt in range(nkt):
                pa = ps_sc[kt % 2]
                sf = sc_f[kt % 2]
                for c in range(2):
                    s.op("tensor", lambda e, kt=kt, c=c, pa=pa, qsl=qsl: e.matmul(
                        pa[:, :], k_fm[c][:, kt * 128:(kt + 1) * 128], q_fm[c][:, qsl],
                        start=(c == 0), stop=(c == 1)),
                         reads=[k_fm[c], q_fm[c]], writes=[pa])
                s.op("vector", lambda e, pa=pa, sf=sf, qg=qg, kt=kt: e.scalar_tensor_tensor(
                    ps_view(sf), ps_view(pa), 1.0, band_view(band, qg, kt),
                    ALU.mult, ALU.add),
                     reads=[pa, band], writes=[sf])
                s.op("scalar", lambda e, sf=sf, kt=kt: e.activation(
                    probsT[kt][:, :], sf[:, :], AF.Exp),
                     reads=[sf], writes=[probsT[kt]])
            for kt in range(nkt):
                s.op("tensor", lambda e, kt=kt: e.matmul(
                    ps_z[:, :], b.onesc_b[:, :], probsT[kt][:, :],
                    start=(kt == 0), stop=(kt == nkt - 1)),
                     reads=[b.onesc_b, probsT[kt]], writes=[ps_z])
            s.op("vector", lambda e: e.tensor_copy(zrow[0:1, :], ps_z[:, :]),
                 reads=[ps_z], writes=[zrow])
            s.op("vector", lambda e: e.reciprocal(rz[0:1, :], zrow[0:1, :]),
                 reads=[zrow], writes=[rz])
            for qp in range(2):
                for vh in range(2):
                    for kt in range(nkt):
                        vb = vbuf[kt % 2]
                        s.dma("sync", vb[:, 0:1024],
                              kvf.ap()[kvrow(kt):kvrow(kt) + 128,
                                       256 + vh * 1024:256 + (vh + 1) * 1024],
                              reads=[kvf], writes=[vb])
                        for qi in range(2):
                            qsub = qp * 2 + qi
                            po = ps_oh[qi]
                            for vd in range(2):
                                s.op("tensor", lambda e, kt=kt, vd=vd, vb=vb,
                                     qsub=qsub, po=po, nkt=nkt: e.matmul(
                                         po[:, vd * 512:(vd + 1) * 512],
                                         probsT[kt][:, qsub * 128:(qsub + 1) * 128],
                                         vb[:, vd * 512:(vd + 1) * 512],
                                         start=(kt == 0), stop=(kt == nkt - 1)),
                                     reads=[probsT[kt], vb], writes=[po])
                    for qi in range(2):
                        qsub = qp * 2 + qi
                        s.op("tensor", lambda e, qsub=qsub: e.matmul(
                            ps_sc[0][:, 0:128], rz[0:1, qsub * 128:(qsub + 1) * 128],
                            b.onesr_f[0:1, :], start=True, stop=True),
                             reads=[rz, b.onesr_f], writes=[ps_sc[0]])
                        s.op("vector", lambda e: e.tensor_copy(rzc[:, :], ps_sc[0][:, 0:1]),
                             reads=[ps_sc[0]], writes=[rzc])
                        s.op("vector", lambda e, qi=qi, vh=vh, qsub=qsub: e.tensor_scalar(
                            attn_tm[:, vh * 1024:(vh + 1) * 1024] if False else
                            atm2[qsub % 2][:, vh * 1024:(vh + 1) * 1024],
                            ps_oh[qi][:, :], rzc[:, :], None, ALU.mult),
                             reads=[ps_oh[qi], rzc], writes=[atm2[qsub % 2]])
                for qi in range(2):
                    qsub = qp * 2 + qi
                    at = atm2[qsub % 2]
                    for vg in range(4):
                        for i in range(4):
                            vc = vg * 4 + i
                            s.op("tensor", lambda e, vc=vc, i=i, at=at: e.transpose(
                                ps_t[:, i * 128:(i + 1) * 128],
                                at[:, vc * 128:(vc + 1) * 128], b.i128[:, :]),
                                 reads=[at, b.i128], writes=[ps_t])
                        s.op("vector", lambda e, vg=vg, qsub=qsub: e.tensor_copy(
                            afm[:, :].rearrange("p (c t) -> p c t", t=512)
                            [:, vg * 4:(vg + 1) * 4, qsub * 128:(qsub + 1) * 128],
                            ps_t[:, :].rearrange("p (c t) -> p c t", t=128)),
                             reads=[ps_t], writes=[afm])
            s.dma("sync", glbuf[:, :].rearrange("p (c t) -> p c t", t=512),
                  _rearr_cpT(gl.ap())[:, :, qsl], reads=[gl], writes=[glbuf])
            s.op("tensor", lambda e, qsl=qsl: e.matmul(
                ps_sc[0][:, :], b.onesr_f[:, :], r1p1_row[0:1, qsl], start=True, stop=True),
                 reads=[b.onesr_f, r1p1_row], writes=[ps_sc[0]])
            s.op("vector", lambda e: e.tensor_copy(rb_bc[:, :], ps_sc[0][:, :]),
                 reads=[ps_sc[0]], writes=[rb_bc])
            s.op("tensor", lambda e, qsl=qsl: e.matmul(
                ps_sc[1][:, :], b.onesr_f[:, :], nu1_row[0:1, qsl], start=True, stop=True),
                 reads=[b.onesr_f, nu1_row], writes=[ps_sc[1]])
            s.op("vector", lambda e: e.tensor_copy(nb_bc[:, :], ps_sc[1][:, :]),
                 reads=[ps_sc[1]], writes=[nb_bc])
            for dm in range(16):
                projv = projT.ap().rearrange("(c p) o -> p c o", p=128)
                s.dma("sync", pbufF[:, 0:16 * 128].rearrange("p (c o) -> p c o", o=128),
                      projv[:, 0:16, dm * 128:(dm + 1) * 128],
                      reads=[projT], writes=[pbufF])
                s.dma("scalar", pbufF[:, 16 * 128:32 * 128].rearrange("p (c o) -> p c o", o=128),
                      projv[:, 16:32, dm * 128:(dm + 1) * 128],
                      reads=[projT], writes=[pbufF])
                s.op("vector", lambda e: e.tensor_copy(pbufB[:, :], pbufF[:, :]),
                     reads=[pbufF], writes=[pbufB])
                xb = xbuf[dm % 2]
                s.dma("scalar", xb[:, :], xin.ap()[dm * 128:(dm + 1) * 128, qsl],
                      reads=[xin], writes=[xb])
                pp = ps_sc[dm % 2]
                for ec in range(32):
                    rhs_t = glbuf if ec < 16 else afm
                    rhs_ap = (glbuf[:, (ec % 16) * 512:(ec % 16 + 1) * 512] if ec < 16
                              else afm[:, (ec - 16) * 512:(ec - 15) * 512])
                    s.op("tensor", lambda e, ec=ec, pp=pp, rhs_ap=rhs_ap: e.matmul(
                        pp[:, :], pbufB[:, ec * 128:(ec + 1) * 128], rhs_ap,
                        start=(ec == 0), stop=(ec == 31)),
                         reads=[pbufB, rhs_t], writes=[pp])
                s.op("vector", lambda e, xb=xb: e.tensor_mul(x1t[:, :], xb[:, :],
                                                             rb_bc[:, :]),
                     reads=[xb, rb_bc], writes=[x1t])
                s.op("vector", lambda e, pp=pp: e.tensor_add(x1t[:, :], x1t[:, :],
                                                             pp[:, :]),
                     reads=[x1t, pp], writes=[x1t])
                s.op("vector", lambda e: e.tensor_sub(x1t[:, :], x1t[:, :], nb_bc[:, :]),
                     reads=[x1t, nb_bc], writes=[x1t])
                s.dma("scalar", x1.ap()[dm * 128:(dm + 1) * 128, qsl], x1t[:, :],
                      reads=[x1t], writes=[x1])
    return b.finish() if standalone else None



def build_l0f_pre(b=None, io=None):
    standalone = b is None
    if standalone:
        b = Build("l0f_pre")
    s = b.s
    if io is None:
        io = {"xin": b.din("xin", [D, T], F32),
              "gwT": b.din("gwT", [128, 16 * 128], F32),
              "sgw": b.din("sgw", [128, 16], F32),
              "xln2": b.dout("xln2", [D, T], BF16),
              "comb": b.dout("comb", [8, T], BF16),
              "grow": b.dout("grow", [1, T], F32)}
    xin, gwT, sgw = io["xin"], io["gwT"], io["sgw"]
    xln2, comb, grow = io.get("xln2"), io["comb"], io["grow"]
    if standalone:
        b.load_consts()
    x_ln = io.get("xln_sb") or [b.sb(f"xln{ch}", [128, T], BF16) for ch in range(16)]
    with b.scope():
        emit_layernorm(b, xin, x_ln, gamma_beta=None, rho_mode="single")
    if xln2 is not None:
        for ch in range(16):
            s.dma("scalar", xln2.ap()[ch * 128:(ch + 1) * 128, :], x_ln[ch][:, :],
                  reads=[x_ln[ch]], writes=[xln2])
    with b.scope():
        gwf = b.sb("gwf", [128, 16 * 128], F32)
        gwb = b.sb("gwb", [128, 16 * 128], BF16)
        sgf = b.sb("sgf", [128, 16], F32)
        sgb = b.sb("sgb", [128, 16], BF16)
        lg_all = b.sb("lgall", [128, 128], F32)
        e_all = b.sb("eall", [128, 128], F32)
        eq1 = b.sb("eq1", [128, 128], F32)
        neq = b.sb("neq", [128, 128], F32)
        me_all = b.sb("meall", [128, 128], F32)
        m1a = b.sb("m1a", [128, 16], F32)
        e2a = b.sb("e2a", [128, 16], F32)
        dena = b.sb("dena", [128, 16], F32)
        rden = b.sb("rden", [128, 16], F32)
        cmb_all = b.sb("cmba", [128, 128], F32)
        cmb_b = b.sb("cmbb", [128, 128], BF16)
        cfm = b.sb("cfm", [8, 16 * 128], BF16)
        grow_row = b.sb("growr", [1, T], F32)
        ps_g = b.ps("psg", [128, 128])
        ps_t8 = b.ps("pst8", [8, 128], BF16)
        ps_sg = b.ps("pssg", [1, 512])
        s.dma("sync", gwf[:, :], gwT.ap(), reads=[gwT], writes=[gwf])
        s.op("vector", lambda e: e.tensor_copy(gwb[:, :], gwf[:, :]),
             reads=[gwf], writes=[gwb])
        s.dma("sync", sgf[:, :], sgw.ap(), reads=[sgw], writes=[sgf])
        s.op("vector", lambda e: e.tensor_copy(sgb[:, :], sgf[:, :]),
             reads=[sgf], writes=[sgb])
        v3 = lambda t, o=8: t[:, :].rearrange("p (g o) -> p g o", o=o)
        for tt in range(16):
            t0 = tt * 128
            for k in range(16):
                s.op("tensor", lambda e, k=k, t0=t0: e.matmul(
                    ps_g[:, :], x_ln[k][:, t0:t0 + 128], gwb[:, k * 128:(k + 1) * 128],
                    start=(k == 0), stop=(k == 15)),
                     reads=[x_ln[k], gwb], writes=[ps_g])
            s.op("vector", lambda e, tt=tt: e.tensor_copy(lg_all[:, tt * 8:(tt + 1) * 8],
                                                          ps_g[:, 0:8]),
                 reads=[ps_g], writes=[lg_all])
        s.op("scalar", lambda e: e.activation(e_all[:, :], lg_all[:, :], AF.Exp),
             reads=[lg_all], writes=[e_all], short=1200)
        s.op("vector", lambda e: e.reduce_max(v3(m1a, 1), v3(e_all), AX.X),
             reads=[e_all], writes=[m1a])
        s.op("vector", lambda e: e.tensor_tensor(v3(eq1), v3(e_all),
                                                 v3(m1a, 1).broadcast_to([128, 16, 8]),
                                                 ALU.is_equal),
             reads=[e_all, m1a], writes=[eq1], short=700)
        s.op("vector", lambda e: e.tensor_scalar(neq[:, :], eq1[:, :], -1.0, 1.0,
                                                 ALU.mult, ALU.add),
             reads=[eq1], writes=[neq])
        s.op("vector", lambda e: e.tensor_mul(me_all[:, :], e_all[:, :], neq[:, :]),
             reads=[e_all, neq], writes=[me_all], short=700)
        s.op("vector", lambda e: e.reduce_max(v3(e2a, 1), v3(me_all), AX.X),
             reads=[me_all], writes=[e2a])
        s.op("vector", lambda e: e.tensor_add(dena[:, :], m1a[:, :], e2a[:, :]),
             reads=[m1a, e2a], writes=[dena])
        s.op("vector", lambda e: e.reciprocal(rden[:, :], dena[:, :]),
             reads=[dena], writes=[rden])
        s.op("vector", lambda e: e.tensor_tensor(v3(neq), v3(me_all),
                                                 v3(e2a, 1).broadcast_to([128, 16, 8]),
                                                 ALU.is_equal),
             reads=[me_all, e2a], writes=[neq], short=700)
        s.op("vector", lambda e: e.tensor_add(eq1[:, :], eq1[:, :], neq[:, :]),
             reads=[eq1, neq], writes=[eq1], short=700)
        s.op("vector", lambda e: e.tensor_mul(cmb_all[:, :], e_all[:, :], eq1[:, :]),
             reads=[e_all, eq1], writes=[cmb_all], short=700)
        s.op("vector", lambda e: e.tensor_tensor(v3(cmb_all), v3(cmb_all),
                                                 v3(rden, 1).broadcast_to([128, 16, 8]),
                                                 ALU.mult),
             reads=[cmb_all, rden], writes=[cmb_all], short=700)
        s.op("vector", lambda e: e.tensor_copy(cmb_b[:, :], cmb_all[:, :]),
             reads=[cmb_all], writes=[cmb_b])
        for tt in range(16):
            t0 = tt * 128
            s.op("tensor", lambda e, tt=tt: e.transpose(
                ps_t8[:, :], cmb_b[:, tt * 8:(tt + 1) * 8], b.i128[:, :]),
                 reads=[cmb_b, b.i128], writes=[ps_t8])
            s.op("vector", lambda e, t0=t0: e.tensor_copy(cfm[:, t0:t0 + 128],
                                                          ps_t8[:, :]),
                 reads=[ps_t8], writes=[cfm])
        s.dma("scalar", comb.ap(), cfm[:, :], reads=[cfm], writes=[comb])
        for stt in range(4):
            sl = slice(stt * 512, stt * 512 + 512)
            for k in range(16):
                s.op("tensor", lambda e, k=k, sl=sl: e.matmul(
                    ps_sg[:, :], sgb[:, k:k + 1], x_ln[k][:, sl],
                    start=(k == 0), stop=(k == 15)),
                     reads=[sgb, x_ln[k]], writes=[ps_sg])
            s.op("scalar", lambda e, sl=sl: e.activation(grow_row[0:1, sl], ps_sg[:, :],
                                                         AF.Sigmoid),
                 reads=[ps_sg], writes=[grow_row])
        s.dma("scalar", grow.ap(), grow_row[0:1, :], reads=[grow_row], writes=[grow])
    return b.finish() if standalone else None



I_MOE = 1408
NIC = I_MOE // 128

def build_l0f_hh(b=None, io=None):
    standalone = b is None
    if standalone:
        b = Build("l0f_hh")
    s = b.s
    if io is None:
        io = {"xln2": b.din("xln2", [D, T], BF16),
              "wg": b.din("wg", [8 * D, I_MOE], F32),
              "wu": b.din("wu", [8 * D, I_MOE], F32),
              "comb": b.din("comb", [8, T], BF16),
              "hh": b.dout("hh", [8 * I_MOE, T], BF16)}
    xln2, wg, wu, comb, hh = io.get("xln2"), io["wg"], io["wu"], io["comb"], io["hh"]
    if standalone:
        b.load_consts()
    x_ln = io.get("xln_sb")
    if x_ln is None:
        x_ln = [b.sb(f"xln{ch}", [128, T], BF16) for ch in range(16)]
        for ch in range(16):
            s.dma("sync", x_ln[ch][:, :], xln2.ap()[ch * 128:(ch + 1) * 128, :],
                  reads=[xln2], writes=[x_ln[ch]])
    with b.scope():
        crow = b.sb("crow", [1, T], BF16)
        cbc = b.sb("cbc", [128, T], F32)
        wgF = [b.sb(f"wgF{i}", [128, 16 * 128], F32) for i in range(2)]
        wgB = [b.sb(f"wgB{i}", [128, 16 * 128], BF16) for i in range(2)]
        wuF = [b.sb(f"wuF{i}", [128, 16 * 128], F32) for i in range(2)]
        wuB = [b.sb(f"wuB{i}", [128, 16 * 128], BF16) for i in range(2)]
        sg = b.sb("sg", [128, 512], F32)
        ue = b.sb("ue", [128, 512], F32)
        hhT = b.sb("hhT", [128, 4 * 512], BF16)
        psG = [b.ps(f"psG{i}", [128, 512]) for i in range(2)]
        psU = [b.ps(f"psU{i}", [128, 512]) for i in range(2)]
        psC = b.ps("psC", [128, 512])
        for ei in range(8):
            s.dma("sync", crow[0:1, :], comb.ap()[ei:ei + 1, :], reads=[comb],
                  writes=[crow])
            for stt in range(4):
                sl = slice(stt * 512, stt * 512 + 512)
                s.op("tensor", lambda e, sl=sl: e.matmul(
                    psC[:, :], b.onesr_b[:, :], crow[0:1, sl], start=True, stop=True),
                     reads=[b.onesr_b, crow], writes=[psC])
                s.op("vector", lambda e, sl=sl: e.tensor_copy(cbc[:, sl], psC[:, :]),
                     reads=[psC], writes=[cbc])
            for ic in range(NIC):
                wgf, wgb = wgF[ic % 2], wgB[ic % 2]
                wuf, wub = wuF[ic % 2], wuB[ic % 2]
                col = slice(ic * 128, ic * 128 + 128)
                s.dma("sync", wgf[:, :].rearrange("p (k o) -> p k o", o=128),
                      wg.ap()[ei * D:(ei + 1) * D, col].rearrange("(k p) o -> p k o", p=128),
                      reads=[wg], writes=[wgf])
                s.op("vector", lambda e, wgf=wgf, wgb=wgb: e.tensor_copy(wgb[:, :], wgf[:, :]),
                     reads=[wgf], writes=[wgb])
                s.dma("sync", wuf[:, :].rearrange("p (k o) -> p k o", o=128),
                      wu.ap()[ei * D:(ei + 1) * D, col].rearrange("(k p) o -> p k o", p=128),
                      reads=[wu], writes=[wuf])
                s.op("vector", lambda e, wuf=wuf, wub=wub: e.tensor_copy(wub[:, :], wuf[:, :]),
                     reads=[wuf], writes=[wub])
                for stt in range(4):
                    sl = slice(stt * 512, stt * 512 + 512)
                    pg, pu = psG[stt % 2], psU[stt % 2]
                    for k in range(16):
                        s.op("tensor", lambda e, k=k, sl=sl, pg=pg, wgb=wgb: e.matmul(
                            pg[:, :], wgb[:, k * 128:(k + 1) * 128], x_ln[k][:, sl],
                            start=(k == 0), stop=(k == 15)),
                             reads=[wgb, x_ln[k]], writes=[pg])
                    for k in range(16):
                        s.op("tensor", lambda e, k=k, sl=sl, pu=pu, wub=wub: e.matmul(
                            pu[:, :], wub[:, k * 128:(k + 1) * 128], x_ln[k][:, sl],
                            start=(k == 0), stop=(k == 15)),
                             reads=[wub, x_ln[k]], writes=[pu])
                    s.op("scalar", lambda e, pg=pg: e.activation(sg[:, :], pg[:, :],
                                                                 AF.Silu),
                         reads=[pg], writes=[sg], short=500)
                    s.op("vector", lambda e, pu=pu, sl=sl: e.tensor_mul(
                        ue[:, :], pu[:, :], cbc[:, sl]),
                         reads=[pu, cbc], writes=[ue])
                    s.op("vector", lambda e, sl=sl, hhT=hhT: e.tensor_mul(
                        hhT[:, sl], sg[:, :], ue[:, :]),
                         reads=[sg, ue], writes=[hhT])
                r0 = ei * I_MOE + ic * 128
                s.dma("scalar", hh.ap()[r0:r0 + 128, :], hhT[:, :],
                      reads=[hhT], writes=[hh])
    return b.finish() if standalone else None



IS = 5632
NICS = IS // 128

def build_l0f_sh(b=None, io=None):
    standalone = b is None
    if standalone:
        b = Build("l0f_sh")
    s = b.s
    if io is None:
        io = {"xln2": b.din("xln2", [D, T], BF16), "swg": b.din("swg", [D, IS], F32),
              "swu": b.din("swu", [D, IS], F32), "grow": b.din("grow", [1, T], F32),
              "wdf": b.din("wdf", [8 * I_MOE, D], F32),
              "swdf": b.din("swdf", [IS, D], F32),
              "sh": b.dout("sh", [IS, T], BF16),
              "wdb": b.dout("wdb", [8 * I_MOE, D], BF16),
              "swdb": b.dout("swdb", [IS, D], BF16)}
    xln2, swg, swu, grow = io.get("xln2"), io["swg"], io["swu"], io["grow"]
    wdf, swdf, sh, wdb, swdb = io["wdf"], io["swdf"], io["sh"], io["wdb"], io["swdb"]
    if standalone:
        b.load_consts()
    x_ln = io.get("xln_sb")
    if x_ln is None:
        x_ln = [b.sb(f"xln{ch}", [128, T], BF16) for ch in range(16)]
        for ch in range(16):
            s.dma("sync", x_ln[ch][:, :], xln2.ap()[ch * 128:(ch + 1) * 128, :],
                  reads=[xln2], writes=[x_ln[ch]])
    with b.scope():
        grow_row = b.sb("growr", [1, T], F32)
        gbc = b.sb("gbc", [128, T], F32)
        wgF = [b.sb(f"wgF{i}", [128, 16 * 128], F32) for i in range(2)]
        wgB = [b.sb(f"wgB{i}", [128, 16 * 128], BF16) for i in range(2)]
        wuF = [b.sb(f"wuF{i}", [128, 16 * 128], F32) for i in range(2)]
        wuB = [b.sb(f"wuB{i}", [128, 16 * 128], BF16) for i in range(2)]
        cstF = [b.sb(f"cstF{i}", [128, 2048], F32) for i in range(2)]
        cstB = [b.sb(f"cstB{i}", [128, 2048], BF16) for i in range(2)]
        sg = b.sb("sg", [128, 512], F32)
        ue = b.sb("ue", [128, 512], F32)
        shT = b.sb("shT", [128, 4 * 512], BF16)
        psG = [b.ps(f"psG{i}", [128, 512]) for i in range(2)]
        psU = [b.ps(f"psU{i}", [128, 512]) for i in range(2)]
        psC = b.ps("psC", [128, 512])
        s.dma("sync", grow_row[0:1, :], grow.ap(), reads=[grow], writes=[grow_row])
        for stt in range(4):
            sl = slice(stt * 512, stt * 512 + 512)
            s.op("tensor", lambda e, sl=sl: e.matmul(
                psC[:, :], b.onesr_f[:, :], grow_row[0:1, sl], start=True, stop=True),
                 reads=[b.onesr_f, grow_row], writes=[psC])
            s.op("vector", lambda e, sl=sl: e.tensor_copy(gbc[:, sl], psC[:, :]),
                 reads=[psC], writes=[gbc])
        cast_jobs = [(wdf, wdb, r) for r in range(8 * I_MOE // 128)]
        cast_jobs += [(swdf, swdb, r) for r in range(IS // 128)]
        cast_i = [0]

        def emit_casts(n):
            for _ in range(n):
                if cast_i[0] >= len(cast_jobs):
                    return
                src, dst, r = cast_jobs[cast_i[0]]
                cast_i[0] += 1
                cf, cb = cstF[r % 2], cstB[r % 2]
                s.dma("sync", cf[:, :], src.ap()[r * 128:(r + 1) * 128, :],
                      reads=[src], writes=[cf])
                s.op("vector", lambda e, cf=cf, cb=cb: e.tensor_copy(cb[:, :], cf[:, :]),
                     reads=[cf], writes=[cb])
                s.dma("scalar", dst.ap()[r * 128:(r + 1) * 128, :], cb[:, :],
                      reads=[cb], writes=[dst])

        for ic in range(NICS):
            emit_casts(3)
            wgf, wgb = wgF[ic % 2], wgB[ic % 2]
            wuf, wub = wuF[ic % 2], wuB[ic % 2]
            col = slice(ic * 128, ic * 128 + 128)
            s.dma("sync", wgf[:, :].rearrange("p (k o) -> p k o", o=128),
                  swg.ap()[:, col].rearrange("(k p) o -> p k o", p=128),
                  reads=[swg], writes=[wgf])
            s.op("vector", lambda e, wgf=wgf, wgb=wgb: e.tensor_copy(wgb[:, :], wgf[:, :]),
                 reads=[wgf], writes=[wgb])
            s.dma("sync", wuf[:, :].rearrange("p (k o) -> p k o", o=128),
                  swu.ap()[:, col].rearrange("(k p) o -> p k o", p=128),
                  reads=[swu], writes=[wuf])
            s.op("vector", lambda e, wuf=wuf, wub=wub: e.tensor_copy(wub[:, :], wuf[:, :]),
                 reads=[wuf], writes=[wub])
            for stt in range(4):
                sl = slice(stt * 512, stt * 512 + 512)
                pg, pu = psG[stt % 2], psU[stt % 2]
                for k in range(16):
                    s.op("tensor", lambda e, k=k, sl=sl, pg=pg, wgb=wgb: e.matmul(
                        pg[:, :], wgb[:, k * 128:(k + 1) * 128], x_ln[k][:, sl],
                        start=(k == 0), stop=(k == 15)),
                         reads=[wgb, x_ln[k]], writes=[pg])
                for k in range(16):
                    s.op("tensor", lambda e, k=k, sl=sl, pu=pu, wub=wub: e.matmul(
                        pu[:, :], wub[:, k * 128:(k + 1) * 128], x_ln[k][:, sl],
                        start=(k == 0), stop=(k == 15)),
                         reads=[wub, x_ln[k]], writes=[pu])
                s.op("scalar", lambda e, pg=pg: e.activation(sg[:, :], pg[:, :], AF.Silu),
                     reads=[pg], writes=[sg], short=500)
                s.op("vector", lambda e, pu=pu, sl=sl: e.tensor_mul(
                    ue[:, :], pu[:, :], gbc[:, sl]),
                     reads=[pu, gbc], writes=[ue])
                s.op("vector", lambda e, sl=sl, shT=shT: e.tensor_mul(
                    shT[:, sl], sg[:, :], ue[:, :]),
                     reads=[sg, ue], writes=[shT])
            r0 = ic * 128
            s.dma("scalar", sh.ap()[r0:r0 + 128, :], shT[:, :],
                  reads=[shT], writes=[sh])
    return b.finish() if standalone else None



def build_l0f_down(b=None, io=None):
    standalone = b is None
    if standalone:
        b = Build("l0f_down")
    s = b.s
    if io is None:
        io = {"hh": b.din("hh", [8 * I_MOE, T], BF16), "sh": b.din("sh", [IS, T], BF16),
              "wdb": b.din("wdb", [8 * I_MOE, D], BF16),
              "swdb": b.din("swdb", [IS, D], BF16),
              "x1": b.din("x1", [D, T], F32), "x2": b.dout("x2", [D, T], F32)}
    hh, sh, wdb, swdb, x1, x2 = (io["hh"], io["sh"], io["wdb"], io["swdb"],
                                 io["x1"], io["x2"])
    if standalone:
        b.load_consts()
    with b.scope():
        big = b.sb("big", [128, 88 * 512], BF16)
        tsb = b.sb("tsb", [128, 16 * 512], F32)
        wcols = [b.sb(f"wcol{i}", [128, 88 * 128], BF16) for i in range(2)]
        swcols = [b.sb(f"swcol{i}", [128, 44 * 128], BF16) for i in range(2)]
        x1t = [b.sb(f"x1t{i}", [128, 512], F32) for i in range(2)]
        ut = b.sb("ut", [128, 512], F32)
        psD = [b.ps(f"psD{i}", [128, 512]) for i in range(2)]
        hhv = hh.ap().rearrange("(c p) t -> p c t", p=128)
        shv = sh.ap().rearrange("(c p) t -> p c t", p=128)
        for stt in range(4):
            sl = slice(stt * 512, stt * 512 + 512)
            s.dma("sync", big[:, 0:44 * 512].rearrange("p (c t) -> p c t", t=512),
                  hhv[:, 0:44, sl], reads=[hh], writes=[big])
            s.dma("gpsimd", big[:, 44 * 512:88 * 512].rearrange("p (c t) -> p c t", t=512),
                  hhv[:, 44:88, sl], reads=[hh], writes=[big])
            for dm in range(16):
                dcol = slice(dm * 128, dm * 128 + 128)
                wcol = wcols[dm % 2]
                s.dma("sync", wcol[:, :].rearrange("p (c o) -> p c o", o=128),
                      wdb.ap().rearrange("(c p) o -> p c o", p=128)[:, :, dcol],
                      reads=[wdb], writes=[wcol])
                pd = psD[dm % 2]
                for ic in range(88):
                    s.op("tensor", lambda e, ic=ic, pd=pd, wcol=wcol: e.matmul(
                        pd[:, :], wcol[:, ic * 128:(ic + 1) * 128],
                        big[:, ic * 512:(ic + 1) * 512],
                        start=(ic == 0), stop=(ic == 87)),
                         reads=[wcol, big], writes=[pd])
                s.op("vector", lambda e, pd=pd, dm=dm: e.tensor_copy(
                    tsb[:, dm * 512:(dm + 1) * 512], pd[:, :]),
                     reads=[pd], writes=[tsb])
            s.dma("sync", big[:, 0:22 * 512].rearrange("p (c t) -> p c t", t=512),
                  shv[:, 0:22, sl], reads=[sh], writes=[big])
            s.dma("gpsimd", big[:, 22 * 512:44 * 512].rearrange("p (c t) -> p c t", t=512),
                  shv[:, 22:44, sl], reads=[sh], writes=[big])
            for dm in range(16):
                dcol = slice(dm * 128, dm * 128 + 128)
                swcol = swcols[dm % 2]
                s.dma("sync", swcol[:, :].rearrange("p (c o) -> p c o", o=128),
                      swdb.ap().rearrange("(c p) o -> p c o", p=128)[:, :, dcol],
                      reads=[swdb], writes=[swcol])
                pd = psD[dm % 2]
                for ic in range(44):
                    s.op("tensor", lambda e, ic=ic, pd=pd, swcol=swcol: e.matmul(
                        pd[:, :], swcol[:, ic * 128:(ic + 1) * 128],
                        big[:, ic * 512:(ic + 1) * 512],
                        start=(ic == 0), stop=(ic == 43)),
                         reads=[swcol, big], writes=[pd])
                xb = x1t[dm % 2]
                s.dma("scalar", xb[:, :], x1.ap()[dm * 128:(dm + 1) * 128, sl],
                      reads=[x1], writes=[xb])
                s.op("vector", lambda e, pd=pd, dm=dm: e.tensor_add(
                    ut[:, :], pd[:, :], tsb[:, dm * 512:(dm + 1) * 512]),
                     reads=[pd, tsb], writes=[ut])
                s.op("vector", lambda e, xb=xb: e.tensor_add(ut[:, :], ut[:, :], xb[:, :]),
                     reads=[ut, xb], writes=[ut])
                s.dma("scalar", x2.ap()[dm * 128:(dm + 1) * 128, sl], ut[:, :],
                      reads=[ut], writes=[x2])
    return b.finish() if standalone else None


def host_inputs_l0f(inputs, x1_arrs, resC=None, resD=None, resD2=None, stage="pre"):
    consts = mkconsts()
    maps = []
    if stage == "pre":
        gwT_small = inputs["L0_ffn_gate_w"].T.reshape(16, 128, 8).transpose(1, 0, 2)
        gwT = np.zeros((128, 16, 128), np.float32)
        gwT[:, :, 0:8] = gwT_small
        gwT = np.ascontiguousarray(gwT.reshape(128, 16 * 128))
        sgw = np.ascontiguousarray(inputs["L0_ffn_shared_gate_w"].reshape(16, 128).T)
        for c in range(NCORE):
            m = {"xin": x1_arrs[c], "gwT": gwT, "sgw": sgw}
            m.update(consts)
            maps.append(m)
    elif stage == "hh":
        wg = np.ascontiguousarray(inputs["L0_ffn_w_gate"].reshape(8 * D, I_MOE))
        wu = np.ascontiguousarray(inputs["L0_ffn_w_up"].reshape(8 * D, I_MOE))
        for c in range(NCORE):
            m = {"xln2": resC[c]["xln2"], "wg": wg, "wu": wu, "comb": resC[c]["comb"]}
            m.update(consts)
            maps.append(m)
    elif stage == "sh":
        swg = np.ascontiguousarray(inputs["L0_ffn_sw_gate"])
        swu = np.ascontiguousarray(inputs["L0_ffn_sw_up"])
        wdf = np.ascontiguousarray(inputs["L0_ffn_w_down"].reshape(8 * I_MOE, D))
        swdf = np.ascontiguousarray(inputs["L0_ffn_sw_down"])
        for c in range(NCORE):
            m = {"xln2": resC[c]["xln2"], "swg": swg, "swu": swu,
                 "grow": resC[c]["grow"], "wdf": wdf, "swdf": swdf}
            m.update(consts)
            maps.append(m)
    elif stage == "down":
        for c in range(NCORE):
            m = {"hh": resD[c]["hh"], "sh": resD2[c]["sh"], "wdb": resD2[c]["wdb"],
                 "swdb": resD2[c]["swdb"], "x1": x1_arrs[c]}
            m.update(consts)
            maps.append(m)
    return maps



def build_l1a_pre(b=None, io=None):
    standalone = b is None
    if standalone:
        b = Build("l1a_pre")
    s = b.s
    if io is None:
        io = {"xin": b.din("xin", [D, T], F32), "wq": b.din("wq", [D, D], F32),
              "wk": b.din("wk", [D, 512], F32), "wv": b.din("wv", [D, 512], F32),
              "ctab": b.din("ctab", [128, 16 * 512], BF16),
              "stab": b.din("stab", [128, 16 * 512], BF16),
              "q_tm": b.dout("q_tm", [T, D], BF16),
              "kv1": b.dout("kv1", [T, 1024], BF16)}
    xin, wq, wk, wv = io["xin"], io["wq"], io["wk"], io["wv"]
    ctab, stab, q_tm, kv1 = io["ctab"], io["stab"], io["q_tm"], io["kv1"]
    if standalone:
        b.load_consts()
    x_ln = [b.sb(f"xln{ch}", [128, T], BF16) for ch in range(16)]
    with b.scope():
        emit_layernorm(b, xin, x_ln, gamma_beta=None, rho_mode="single")
    ct_sb = b.sb("ctsb", [128, 16 * 512], BF16)
    st_sb = b.sb("stsb", [128, 16 * 512], BF16)
    s.dma("sync", ct_sb[:, :], ctab.ap(), reads=[ctab], writes=[ct_sb])
    s.dma("sync", st_sb[:, :], stab.ap(), reads=[stab], writes=[st_sb])
    with b.scope():
        wf = b.sb("wf", [128, 16 * 512], F32)
        wb = [b.sb(f"wb{i}", [128, 16 * 512], BF16) for i in range(2)]
        ps = [b.ps(f"ps{i}", [128, 512]) for i in range(2)]
        uu = b.sb("uu", [128, 512], F32)
        vv = b.sb("vv", [128, 512], F32)
        ot = b.sb("ot", [128, 512], BF16)
        r4 = lambda ap: ap.rearrange("p (h a j) -> p h a j", a=2, j=32)
        parts = [("k", wk, 0, 1.0, True, kv1, 0), ("v", wv, 0, 1.0, False, kv1, 512)]
        parts += [("q", wq, oc, 0.125, True, q_tm, oc * 512) for oc in range(4)]
        for pi, (nm, src, oc, scale, rope, dst, dcol) in enumerate(parts):
            if pi == 2 and io.get("after_kv"):
                io["after_kv"]()
            wbb = wb[pi % 2]
            s.dma("sync", wf[:, :].rearrange("p (k o) -> p k o", o=512),
                  src.ap().rearrange("(k p) o -> p k o", p=128)[:, :, oc * 512:(oc + 1) * 512],
                  reads=[src], writes=[wf])
            s.op("vector", lambda e, wbb=wbb: e.tensor_copy(wbb[:, :], wf[:, :]),
                 reads=[wf], writes=[wbb])
            for tt in range(16):
                t0 = tt * 128
                pp = ps[tt % 2]
                for k in range(16):
                    s.op("tensor", lambda e, k=k, t0=t0, pp=pp, wbb=wbb: e.matmul(
                        pp[:, :], x_ln[k][:, t0:t0 + 128], wbb[:, k * 512:(k + 1) * 512],
                        start=(k == 0), stop=(k == 15)),
                         reads=[x_ln[k], wbb], writes=[pp])
                tsl = slice(tt * 512, tt * 512 + 512)
                if rope:
                    s.op("vector", lambda e, pp=pp, tsl=tsl, scale=scale:
                         e.scalar_tensor_tensor(uu[:, :], pp[:, :], scale,
                                                ct_sb[:, tsl], ALU.mult, ALU.mult),
                         reads=[pp, ct_sb], writes=[uu])
                    s.op("vector", lambda e, pp=pp, tsl=tsl, scale=scale:
                         e.scalar_tensor_tensor(
                             r4(vv[:, :])[:, :, 0:1, :], r4(pp[:, :])[:, :, 1:2, :],
                             scale, r4(st_sb[:, tsl])[:, :, 0:1, :], ALU.mult, ALU.mult),
                         reads=[pp, st_sb], writes=[vv], short=700)
                    s.op("vector", lambda e, pp=pp, tsl=tsl, scale=scale:
                         e.scalar_tensor_tensor(
                             r4(vv[:, :])[:, :, 1:2, :], r4(pp[:, :])[:, :, 0:1, :],
                             scale, r4(st_sb[:, tsl])[:, :, 1:2, :], ALU.mult, ALU.mult),
                         reads=[pp, st_sb], writes=[vv], short=700)
                    s.op("vector", lambda e: e.tensor_add(ot[:, :], uu[:, :], vv[:, :]),
                         reads=[uu, vv], writes=[ot])
                else:
                    s.op("vector", lambda e, pp=pp: e.tensor_copy(ot[:, :], pp[:, :]),
                         reads=[pp], writes=[ot])
                if nm == "q":
                    dv = dst.ap()[t0:t0 + 128, dcol:dcol + 512].rearrange(
                        "t (c a j) -> t c a j", c=4, a=2, j=64)
                    for a in range(2):
                        s.dma("scalar", dv[:, :, a:a + 1, :],
                              ot[:, a * 256:(a + 1) * 256].rearrange(
                                  "t (c x j) -> t c x j", c=4, x=1, j=64),
                              reads=[ot], writes=[dst])
                else:
                    s.dma("scalar", dst.ap()[t0:t0 + 128, dcol:dcol + 512], ot[:, :],
                          reads=[ot], writes=[dst])
    return b.finish() if standalone else None



def build_l1a_attn(b=None, io=None):
    standalone = b is None
    if standalone:
        b = Build("l1a_attn")
    s = b.s
    if io is None:
        io = {"q_tm": b.din("q_tm", [T, D], BF16), "kvf": b.din("kvf", [S, 1024], BF16),
              "wo": b.din("wo", [D, D], F32), "xin": b.din("xin", [D, T], F32),
              "bandM": b.din("bandM", [128, 2048], F32),
              "x3": b.dout("x3", [D, T], F32)}
    q_tm, kvf, wo, xin, bandM, x3 = (io["q_tm"], io["kvf"], io["wo"], io["xin"],
                                     io["bandM"], io["x3"])
    kvrow = io.get("kvrow", lambda kt: kt * 128)
    if standalone:
        b.load_consts()
    band = b.sb("band", [128, 2048], F32)
    s.dma("sync", band[:, :], bandM.ap(), reads=[bandM], writes=[band])
    k_fm = [b.sb(f"kfm{c}", [128, S], BF16) for c in range(4)]
    vs_sb = b.sb("vs", [128, 32 * 520], BF16)
    with b.scope():
        kseg = [b.sb(f"kseg{i}", [128, 512], BF16) for i in range(2)]
        pst = b.ps("pst", [128, 128], BF16)
        for kt in range(32):
            sg = kseg[kt % 2]
            s.dma("sync", sg[:, :], kvf.ap()[kvrow(kt):kvrow(kt) + 128, 0:512],
                  reads=[kvf], writes=[sg])
            for c in range(4):
                s.op("tensor", lambda e, sg=sg, c=c: e.transpose(
                    pst[:, :], sg[:, c * 128:(c + 1) * 128], b.i128[:, :]),
                     reads=[sg, b.i128], writes=[pst])
                s.op("vector", lambda e, kt=kt, c=c: e.tensor_copy(
                    k_fm[c][:, kt * 128:(kt + 1) * 128], pst[:, :]),
                     reads=[pst], writes=[k_fm[c]])
            s.dma("scalar",
                  vs_sb[:, kt * 520:(kt + 1) * 520].rearrange("p (h j) -> p h j", j=65)[:, :, 0:64],
                  kvf.ap()[kvrow(kt):kvrow(kt) + 128, 512:1024].rearrange("p (h j) -> p h j", j=64),
                  reads=[kvf], writes=[vs_sb])
            s.op("vector", lambda e, kt=kt: e.memset(
                vs_sb[:, kt * 520:(kt + 1) * 520].rearrange("p (h j) -> p h j", j=65)[:, :, 64:65],
                1.0),
                 reads=[], writes=[vs_sb])
    with b.scope():
        q_fm = [b.sb(f"qfm{c}", [128, 512], BF16) for c in range(16)]
        qtl = [b.sb(f"qtl{i}", [128, 2048], BF16) for i in range(2)]
        probsT = [[b.sb(f"pT{i}_{j}", [128, 512], BF16) for j in range(2)]
                  for i in range(4)]
        sc_f = [b.sb(f"scf{i}", [128, 512], F32) for i in range(2)]
        zrow = b.sb("zrow", [1, 512], F32)
        rz = b.sb("rz", [1, 512], F32)
        zb_sb = b.sb("zbsb", [64, 512], F32)
        y_sb = [b.sb(f"ysb{c}", [128, 512], BF16) for c in range(16)]
        xb = [b.sb(f"xb{i}", [128, 512], F32) for i in range(2)]
        x3t = b.sb("x3t", [128, 512], F32)
        pst = b.ps("pstq", [128, 128], BF16)
        ps_sc = [b.ps(f"pssc{i}", [128, 512]) for i in range(2)]
        ps_pv = [b.ps(f"pspv{i}", [65, 512]) for i in range(4)]
        ps_bc = b.ps("psbc", [64, 512])
        wocf = b.sb("wocf", [128, 16 * 128], F32)
        wocb = [b.sb(f"wocb{i}", [128, 16 * 128], BF16) for i in range(2)]
        for qg in range(4):
            nkt = nkt_of(qg)
            for tts in range(4):
                qt = qtl[tts % 2]
                s.dma("sync", qt[:, :],
                      q_tm.ap()[qg * 512 + tts * 128:qg * 512 + (tts + 1) * 128, :],
                      reads=[q_tm], writes=[qt])
                for ch in range(16):
                    s.op("tensor", lambda e, qt=qt, ch=ch: e.transpose(
                        pst[:, :], qt[:, ch * 128:(ch + 1) * 128], b.i128[:, :]),
                         reads=[qt, b.i128], writes=[pst])
                    s.op("vector", lambda e, ch=ch, tts=tts: e.tensor_copy(
                        q_fm[ch][:, tts * 128:(tts + 1) * 128], pst[:, :]),
                         reads=[pst], writes=[q_fm[ch]])
            for kvh in range(8):
                kch, krow = kvh // 2, (kvh % 2) * 64

                def emit_pv(kt, nkt=None, kvh=kvh):
                    for h4 in range(4):
                        pT = probsT[h4][kt % 2]
                        s.op("tensor", lambda e, h4=h4, kt=kt, kvh=kvh, pT=pT,
                             nkt=nkt: e.matmul(
                                 ps_pv[h4][:, :],
                                 vs_sb[:, kt * 520 + kvh * 65:kt * 520 + (kvh + 1) * 65],
                                 pT[:, :],
                                 start=(kt == 0), stop=(kt == nkt - 1)),
                             reads=[vs_sb, pT], writes=[ps_pv[h4]])

                for kt in range(nkt):
                    for h4 in range(4):
                        h = kvh * 4 + h4
                        slot = (h // 8) * 8 + (h % 4) * 2 + (h // 4) % 2
                        qch, qrow = slot // 2, (slot % 2) * 64
                        pa = ps_sc[(kt * 4 + h4) % 2]
                        sf = sc_f[(kt * 4 + h4) % 2]
                        s.op("tensor", lambda e, pa=pa, kt=kt, kch=kch, krow=krow,
                             qch=qch, qrow=qrow: e.matmul(
                                 pa[:, :],
                                 k_fm[kch][krow:krow + 64, kt * 128:(kt + 1) * 128],
                                 q_fm[qch][qrow:qrow + 64, :],
                                 start=True, stop=True),
                             reads=[k_fm[kch], q_fm[qch]], writes=[pa])
                        if kt >= 8 * qg:
                            s.op("vector", lambda e, pa=pa, sf=sf, qg=qg, kt=kt:
                                 e.scalar_tensor_tensor(ps_view(sf), ps_view(pa), 1.0,
                                                        band_view(band, qg, kt),
                                                        ALU.mult, ALU.add),
                                 reads=[pa, band], writes=[sf])
                            esrc = sf
                        else:
                            esrc = pa
                        s.op("scalar", lambda e, esrc=esrc, h4=h4, kt=kt: e.activation(
                            probsT[h4][kt % 2][:, :], esrc[:, :], AF.Exp),
                             reads=[esrc], writes=[probsT[h4][kt % 2]])
                    if kt > 0:
                        emit_pv(kt - 1, nkt=nkt)
                emit_pv(nkt - 1, nkt=nkt)
                for h4 in range(4):
                    h = kvh * 4 + h4
                    ych, yrow = h // 2, (h % 2) * 64
                    s.op("vector", lambda e, h4=h4: e.reciprocal(
                        rz[0:1, :], ps_pv[h4][64:65, :]),
                         reads=[ps_pv[h4]], writes=[rz])
                    s.op("tensor", lambda e: e.matmul(
                        ps_bc[:, :], b.onesr_f[0:1, 0:64], rz[0:1, :],
                        start=True, stop=True),
                         reads=[b.onesr_f, rz], writes=[ps_bc])
                    s.op("vector", lambda e: e.tensor_copy(zb_sb[:, :], ps_bc[:, :]),
                         reads=[ps_bc], writes=[zb_sb])
                    s.op("vector", lambda e, h4=h4, ych=ych, yrow=yrow: e.tensor_mul(
                        y_sb[ych][yrow:yrow + 64, :], ps_pv[h4][0:64, :], zb_sb[:, :]),
                         reads=[ps_pv[h4], zb_sb], writes=[y_sb[ych]])
            qsl = slice(qg * 512, qg * 512 + 512)
            for dm in range(16):
                pp = ps_sc[dm % 2]
                xbb = xb[dm % 2]
                wob = wocb[dm % 2]
                s.dma("sync", wocf[:, :].rearrange("p (k o) -> p k o", o=128),
                      wo.ap().rearrange("(k p) o -> p k o", p=128)[:, :, dm * 128:(dm + 1) * 128],
                      reads=[wo], writes=[wocf])
                s.op("vector", lambda e, wob=wob: e.tensor_copy(wob[:, :], wocf[:, :]),
                     reads=[wocf], writes=[wob])
                s.dma("scalar", xbb[:, :], xin.ap()[dm * 128:(dm + 1) * 128, qsl],
                      reads=[xin], writes=[xbb])
                for k in range(16):
                    s.op("tensor", lambda e, k=k, pp=pp, wob=wob: e.matmul(
                        pp[:, :], wob[:, k * 128:(k + 1) * 128],
                        y_sb[k][:, :], start=(k == 0), stop=(k == 15)),
                         reads=[wob, y_sb[k]], writes=[pp])
                s.op("vector", lambda e, pp=pp, xbb=xbb: e.tensor_add(
                    x3t[:, :], pp[:, :], xbb[:, :]),
                     reads=[pp, xbb], writes=[x3t])
                s.dma("scalar", x3.ap()[dm * 128:(dm + 1) * 128, qsl], x3t[:, :],
                      reads=[x3t], writes=[x3])
    return b.finish() if standalone else None



def build_l1f_pre(b=None, io=None):
    standalone = b is None
    if standalone:
        b = Build("l1f_pre")
    s = b.s
    if io is None:
        io = {"xin": b.din("xin", [D, T], F32),
              "wqkvT": b.din("wqkvT", [D, 6144], F32),
              "wgT": b.din("wgT", [128, 16 * 128], F32),
              "q_fm": b.dout("q_fm", [D, T], BF16),
              "ksvs": b.dout("ksvs", [T, 257], BF16)}
    xin, wqkvT, wgT = io["xin"], io["wqkvT"], io["wgT"]
    q_fm_o, ksvs = io["q_fm"], io["ksvs"]
    if standalone:
        b.load_consts()
    x_ln = [b.sb(f"xln{ch}", [128, T], BF16) for ch in range(16)]
    with b.scope():
        emit_layernorm(b, xin, x_ln, gamma_beta=None, rho_mode="single")
    ind_all = b.sb("indall", [128, 16 * 16], F32)
    wv_all = b.sb("wvall", [128, 16 * 16], F32)
    with b.scope():
        gwf = b.sb("gwf", [128, 16 * 128], F32)
        gwb = b.sb("gwb", [128, 16 * 128], BF16)
        lg_all = b.sb("lgall", [128, 16 * 16], F32)
        e_all = b.sb("eall", [128, 16 * 16], F32)
        m1a = b.sb("m1a", [128, 16], F32)
        za = b.sb("za", [128, 16], F32)
        rza = b.sb("rza", [128, 16], F32)
        s1a = b.sb("s1a", [128, 16], F32)
        ps_g = b.ps("psg", [128, 128])
        v3 = lambda t, o=16: t[:, :].rearrange("p (g o) -> p g o", o=o)
        s.dma("sync", gwf[:, :], wgT.ap(), reads=[wgT], writes=[gwf])
        s.op("vector", lambda e: e.tensor_copy(gwb[:, :], gwf[:, :]),
             reads=[gwf], writes=[gwb])
        for tt in range(16):
            t0 = tt * 128
            for k in range(16):
                s.op("tensor", lambda e, k=k, t0=t0: e.matmul(
                    ps_g[:, :], x_ln[k][:, t0:t0 + 128], gwb[:, k * 128:(k + 1) * 128],
                    start=(k == 0), stop=(k == 15)),
                     reads=[x_ln[k], gwb], writes=[ps_g])
            s.op("vector", lambda e, tt=tt: e.tensor_copy(
                lg_all[:, tt * 16:(tt + 1) * 16], ps_g[:, 0:16]),
                 reads=[ps_g], writes=[lg_all])
        s.op("scalar", lambda e: e.activation(e_all[:, :], lg_all[:, :], AF.Exp),
             reads=[lg_all], writes=[e_all], short=1200)
        s.op("vector", lambda e: e.reduce_max(v3(m1a, 1), v3(e_all), AX.X),
             reads=[e_all], writes=[m1a])
        s.op("vector", lambda e: e.reduce_sum(v3(za, 1), v3(e_all), AX.X),
             reads=[e_all], writes=[za])
        s.op("vector", lambda e: e.reciprocal(rza[:, :], za[:, :]),
             reads=[za], writes=[rza])
        s.op("vector", lambda e: e.tensor_mul(s1a[:, :], m1a[:, :], rza[:, :]),
             reads=[m1a, rza], writes=[s1a])
        s.op("vector", lambda e: e.tensor_tensor(v3(ind_all), v3(e_all),
                                                 v3(m1a, 1).broadcast_to([128, 16, 16]),
                                                 ALU.is_equal),
             reads=[e_all, m1a], writes=[ind_all], short=700)
        s.op("vector", lambda e: e.tensor_tensor(v3(wv_all), v3(ind_all),
                                                 v3(s1a, 1).broadcast_to([128, 16, 16]),
                                                 ALU.mult),
             reads=[ind_all, s1a], writes=[wv_all], short=700)
    with b.scope():
        ebuf = b.sb("ebuf", [128, 16 * 128], F32)
        ebufb = b.sb("ebufb", [128, 16 * 128], BF16)
        qsc = b.sb("qsc", [128, 512], BF16)
        psq = [b.ps(f"psq{i}", [128, 512]) for i in range(2)]
        for och in range(16):
            s.dma("sync", ebuf[:, :].rearrange("p (k o) -> p k o", o=128),
                  wqkvT.ap().rearrange("(k p) o -> p k o", p=128)[:, :, och * 128:(och + 1) * 128],
                  reads=[wqkvT], writes=[ebuf])
            s.op("vector", lambda e: e.tensor_copy(ebufb[:, :], ebuf[:, :]),
                 reads=[ebuf], writes=[ebufb])
            for stt in range(4):
                sl = slice(stt * 512, stt * 512 + 512)
                pp = psq[stt % 2]
                for k in range(16):
                    s.op("tensor", lambda e, k=k, sl=sl, pp=pp: e.matmul(
                        pp[:, :], ebufb[:, k * 128:(k + 1) * 128], x_ln[k][:, sl],
                        start=(k == 0), stop=(k == 15)),
                         reads=[ebufb, x_ln[k]], writes=[pp])
                s.op("scalar", lambda e, pp=pp: e.mul(qsc[:, :], pp[:, :],
                                                      1.0 / math_sqrt128()),
                     reads=[pp], writes=[qsc])
                s.dma("scalar", q_fm_o.ap()[och * 128:(och + 1) * 128, sl], qsc[:, :],
                      reads=[qsc], writes=[q_fm_o])
    with b.scope():
        wbuf = b.sb("wbuf", [128, 16 * 512], F32)
        wbufb = [b.sb(f"wbufb{i}", [128, 16 * 512], BF16) for i in range(2)]
        ktf = b.sb("ktf", [128, 512], F32)
        ksacc = b.sb("ksacc", [128, 16 * 128], F32)
        vsacc = b.sb("vsacc", [128, 16 * 128], F32)
        kv257 = b.sb("kv257", [128, 257], BF16)
        psk = [b.ps(f"psk{i}", [128, 512]) for i in range(2)]
        for part, wsel in [(0, ind_all), (1, wv_all)]:
            acc = ksacc if part == 0 else vsacc
            for ob in range(4):
                wbb = wbufb[ob % 2]
                col0 = 2048 + part * 2048 + ob * 512
                s.dma("sync", wbuf[:, :].rearrange("p (k o) -> p k o", o=512),
                      wqkvT.ap().rearrange("(k p) o -> p k o", p=128)[:, :, col0:col0 + 512],
                      reads=[wqkvT], writes=[wbuf])
                s.op("vector", lambda e, wbb=wbb: e.tensor_copy(wbb[:, :], wbuf[:, :]),
                     reads=[wbuf], writes=[wbb])
                for tt in range(16):
                    t0 = tt * 128
                    pp = psk[tt % 2]
                    for k in range(16):
                        s.op("tensor", lambda e, k=k, t0=t0, pp=pp, wbb=wbb: e.matmul(
                            pp[:, :], x_ln[k][:, t0:t0 + 128],
                            wbb[:, k * 512:(k + 1) * 512],
                            start=(k == 0), stop=(k == 15)),
                             reads=[x_ln[k], wbb], writes=[pp])
                    s.op("vector", lambda e, pp=pp: e.tensor_copy(ktf[:, :], pp[:, :]),
                         reads=[pp], writes=[ktf])
                    for h4 in range(4):
                        h = ob * 4 + h4
                        scal = wsel[:, tt * 16 + h:tt * 16 + h + 1]
                        if ob == 0 and h4 == 0:
                            s.op("vector", lambda e, h4=h4, tt=tt, scal=scal, acc=acc:
                                 e.tensor_scalar(acc[:, tt * 128:(tt + 1) * 128],
                                                 ktf[:, h4 * 128:(h4 + 1) * 128],
                                                 scal, None, ALU.mult),
                                 reads=[ktf, wsel], writes=[acc])
                        else:
                            s.op("vector", lambda e, h4=h4, tt=tt, scal=scal, acc=acc:
                                 e.scalar_tensor_tensor(
                                     acc[:, tt * 128:(tt + 1) * 128],
                                     ktf[:, h4 * 128:(h4 + 1) * 128], scal,
                                     acc[:, tt * 128:(tt + 1) * 128],
                                     ALU.mult, ALU.add),
                                 reads=[ktf, wsel, acc], writes=[acc])
        for tt in range(16):
            t0 = tt * 128
            s.op("vector", lambda e, tt=tt: e.tensor_copy(
                kv257[:, 0:128], ksacc[:, tt * 128:(tt + 1) * 128]),
                 reads=[ksacc], writes=[kv257])
            s.op("vector", lambda e, tt=tt: e.tensor_copy(
                kv257[:, 128:256], vsacc[:, tt * 128:(tt + 1) * 128]),
                 reads=[vsacc], writes=[kv257])
            s.op("vector", lambda e: e.memset(kv257[:, 256:257], 1.0),
                 reads=[], writes=[kv257])
            s.dma("scalar", ksvs.ap()[t0:t0 + 128, :], kv257[:, :],
                  reads=[kv257], writes=[ksvs])
    return b.finish() if standalone else None


def math_sqrt128():
    return float(np.sqrt(128.0))



def build_l1f_attn(b=None, io=None):
    standalone = b is None
    if standalone:
        b = Build("l1f_attn")
    s = b.s
    if io is None:
        io = {"q_fm": b.din("q_fm", [D, T], BF16),
              "ksvsf": b.din("ksvsf", [S, 257], BF16),
              "xin": b.din("xin", [D, T], F32),
              "bandM": b.din("bandM", [128, 2048], F32),
              "x4": b.dout("x4", [D, T], F32)}
    q_fm_i, ksvsf, xin, bandM, x4 = (io["q_fm"], io["ksvsf"], io["xin"],
                                     io["bandM"], io["x4"])
    kvrow = io.get("kvrow", lambda kt: kt * 128)
    if standalone:
        b.load_consts()
    band = b.sb("band", [128, 2048], F32)
    s.dma("sync", band[:, :], bandM.ap(), reads=[bandM], writes=[band])
    ks_fm = b.sb("ksfm", [128, S], BF16)
    vs_sb = b.sb("vs", [128, 32 * 129], BF16)
    with b.scope():
        kseg = [b.sb(f"kseg{i}", [128, 128], BF16) for i in range(2)]
        pst = b.ps("pst", [128, 128], BF16)
        for kt in range(32):
            sg = kseg[kt % 2]
            s.dma("sync", sg[:, :], ksvsf.ap()[kvrow(kt):kvrow(kt) + 128, 0:128],
                  reads=[ksvsf], writes=[sg])
            s.op("tensor", lambda e, sg=sg: e.transpose(pst[:, :], sg[:, :], b.i128[:, :]),
                 reads=[sg, b.i128], writes=[pst])
            s.op("vector", lambda e, kt=kt: e.tensor_copy(
                ks_fm[:, kt * 128:(kt + 1) * 128], pst[:, :]),
                 reads=[pst], writes=[ks_fm])
            s.dma("scalar", vs_sb[:, kt * 129:(kt + 1) * 129],
                  ksvsf.ap()[kvrow(kt):kvrow(kt) + 128, 128:257],
                  reads=[ksvsf], writes=[vs_sb])
    with b.scope():
        q_sb = b.sb("qsb", [128, 16 * 512], BF16)
        probsT = [b.sb(f"pT{i}", [128, 512], BF16) for i in range(2)]
        sc_f = [b.sb(f"scf{i}", [128, 512], F32) for i in range(2)]
        rz = b.sb("rz", [1, 512], F32)
        zb_sb = b.sb("zbsb", [128, 512], F32)
        y_sb = b.sb("ysb", [128, 16 * 512], F32)
        xb = [b.sb(f"xb{i}", [128, 512], F32) for i in range(2)]
        x4t = b.sb("x4t", [128, 512], F32)
        ps_sc = [b.ps(f"pssc{i}", [128, 512]) for i in range(2)]
        ps_pv = b.ps("pspv", [128, 512])
        ps_z = b.ps("psz", [1, 512])
        ps_bc = b.ps("psbc", [128, 512])
        for qg in range(4):
            nkt = nkt_of(qg)
            qsl = slice(qg * 512, qg * 512 + 512)
            s.dma("sync", q_sb[:, :].rearrange("p (c t) -> p c t", t=512),
                  _rearr_cpT(q_fm_i.ap())[:, :, qsl], reads=[q_fm_i], writes=[q_sb])
            for h in range(16):
                def emit_pv2(kt, nkt=None):
                    pT = probsT[kt % 2]
                    s.op("tensor", lambda e, kt=kt, pT=pT, nkt=nkt: e.matmul(
                        ps_pv[:, :], vs_sb[:, kt * 129:kt * 129 + 128], pT[:, :],
                        start=(kt == 0), stop=(kt == nkt - 1)),
                         reads=[vs_sb, pT], writes=[ps_pv])
                    s.op("tensor", lambda e, kt=kt, pT=pT, nkt=nkt: e.matmul(
                        ps_z[:, :], vs_sb[:, kt * 129 + 128:kt * 129 + 129], pT[:, :],
                        start=(kt == 0), stop=(kt == nkt - 1)),
                         reads=[vs_sb, pT], writes=[ps_z])
                for kt in range(nkt):
                    pa = ps_sc[kt % 2]
                    sf = sc_f[kt % 2]
                    s.op("tensor", lambda e, pa=pa, kt=kt, h=h: e.matmul(
                        pa[:, :], ks_fm[:, kt * 128:(kt + 1) * 128],
                        q_sb[:, h * 512:(h + 1) * 512], start=True, stop=True),
                         reads=[ks_fm, q_sb], writes=[pa])
                    if kt >= 8 * qg:
                        s.op("vector", lambda e, pa=pa, sf=sf, qg=qg, kt=kt:
                             e.scalar_tensor_tensor(ps_view(sf), ps_view(pa), 1.0,
                                                    band_view(band, qg, kt),
                                                    ALU.mult, ALU.add),
                             reads=[pa, band], writes=[sf])
                        esrc = sf
                    else:
                        esrc = pa
                    pT = probsT[kt % 2]
                    s.op("scalar", lambda e, esrc=esrc, pT=pT: e.activation(
                        pT[:, :], esrc[:, :], AF.Exp),
                         reads=[esrc], writes=[pT])
                    if kt > 0:
                        emit_pv2(kt - 1, nkt=nkt)
                emit_pv2(nkt - 1, nkt=nkt)
                s.op("vector", lambda e: e.reciprocal(rz[0:1, :], ps_z[:, :]),
                     reads=[ps_z], writes=[rz])
                s.op("tensor", lambda e: e.matmul(ps_bc[:, :], b.onesr_f[:, :],
                                                  rz[0:1, :], start=True, stop=True),
                     reads=[b.onesr_f, rz], writes=[ps_bc])
                s.op("vector", lambda e: e.tensor_copy(zb_sb[:, :], ps_bc[:, :]),
                     reads=[ps_bc], writes=[zb_sb])
                s.op("vector", lambda e, h=h: e.tensor_mul(
                    y_sb[:, h * 512:(h + 1) * 512], ps_pv[:, :], zb_sb[:, :]),
                     reads=[ps_pv, zb_sb], writes=[y_sb])
            for ch in range(16):
                xbb = xb[ch % 2]
                s.dma("scalar", xbb[:, :], xin.ap()[ch * 128:(ch + 1) * 128, qsl],
                      reads=[xin], writes=[xbb])
                s.op("vector", lambda e, ch=ch, xbb=xbb: e.tensor_add(
                    x4t[:, :], y_sb[:, ch * 512:(ch + 1) * 512], xbb[:, :]),
                     reads=[y_sb, xbb], writes=[x4t])
                s.dma("scalar", x4.ap()[ch * 128:(ch + 1) * 128, qsl], x4t[:, :],
                      reads=[x4t], writes=[x4])
    return b.finish() if standalone else None


def host_rope_tables(c):
    half = c % 2
    pos = np.empty((128, 16), np.int64)
    for tt in range(16):
        pos[:, tt] = (2 * tt + half) * 128 + np.arange(128)
    inv = 1.0 / (10000.0 ** (np.arange(32) / 32.0))
    ang = pos[:, :, None].astype(np.float64) * inv[None, None, :]
    cosv = np.cos(ang); sinv = np.sin(ang)
    ct = np.empty((128, 16, 8, 64), np.float32)
    st = np.empty((128, 16, 8, 64), np.float32)
    ct[:, :, :, 0:32] = cosv[:, :, None, :]
    ct[:, :, :, 32:64] = cosv[:, :, None, :]
    st[:, :, :, 0:32] = -sinv[:, :, None, :]
    st[:, :, :, 32:64] = sinv[:, :, None, :]
    return (ct.reshape(128, 16 * 512).astype(BF16NP),
            st.reshape(128, 16 * 512).astype(BF16NP))


def host_inputs_l1(inputs, x2_arrs=None, resF=None, x3_arrs=None, resH1=None, stage="pre"):
    consts = mkconsts()
    maps = []
    if stage == "pre":
        for c in range(NCORE):
            ct, st = host_rope_tables(c)
            m = {"xin": x2_arrs[c], "wq": inputs["L1_attn_Wq"],
                 "wk": inputs["L1_attn_Wk"], "wv": inputs["L1_attn_Wv"],
                 "ctab": ct, "stab": st}
            m.update(consts)
            maps.append(m)
    elif stage == "attn":
        for (bb, half) in cores_bh():
            c = 2 * bb + half
            kvf = host_kv_global([resF[2 * bb]["kv1"], resF[2 * bb + 1]["kv1"]])
            m = {"q_tm": resF[c]["q_tm"], "kvf": kvf, "wo": inputs["L1_attn_Wo"],
                 "xin": x2_arrs[c], "bandM": np.ascontiguousarray(mk_band(half, "mask")[:, 0:2048])}
            m.update(consts)
            maps.append(m)
    elif stage == "fpre":
        wqkvT = np.ascontiguousarray(inputs["L1_ffn_W_qkv"].T)
        wgT_small = inputs["L1_ffn_W_gate"].T.reshape(16, 128, 16).transpose(1, 0, 2)
        wgT = np.zeros((128, 16, 128), np.float32)
        wgT[:, :, 0:16] = wgT_small
        wgT = np.ascontiguousarray(wgT.reshape(128, 16 * 128))
        for c in range(NCORE):
            m = {"xin": x3_arrs[c], "wqkvT": wqkvT, "wgT": wgT}
            m.update(consts)
            maps.append(m)
    elif stage == "fattn":
        for (bb, half) in cores_bh():
            c = 2 * bb + half
            ksvsf = host_kv_global([resH1[2 * bb]["ksvs"], resH1[2 * bb + 1]["ksvs"]])
            m = {"q_fm": resH1[c]["q_fm"], "ksvsf": ksvsf,
                 "xin": x3_arrs[c],
                 "bandM": np.ascontiguousarray(mk_band(half, "mask")[:, 0:2048])}
            m.update(consts)
            maps.append(m)
    return maps


def core_perm(c):
    half = c % 2
    return np.concatenate([np.arange((2 * j + half) * 128, (2 * j + half) * 128 + 128)
                           for j in range(16)])

def cores_bh():
    return [(c // 2, c % 2) for c in range(NCORE)]


def host_inputs_l0a_pre(inputs):
    x = inputs["L0_attn_x"]
    expT = np.ascontiguousarray(inputs["L0_attn_expand"].T)
    gam = np.ascontiguousarray(inputs["L0_attn_ln_gamma"].reshape(16, 128).T)
    bet = np.ascontiguousarray(inputs["L0_attn_ln_beta"].reshape(16, 128).T)
    consts = mkconsts()
    maps = []
    for (bb, half) in cores_bh():
        m = {"xin": np.ascontiguousarray(x[bb, core_perm(2 * bb + half), :].T),
             "expT": expT, "gam": gam, "bet": bet}
        m.update(consts)
        maps.append(m)
    return maps


def host_kv_global(kv_pair):
    ev, od = kv_pair
    W = ev.shape[1]
    out = np.empty((S, W), ev.dtype)
    out.reshape(32, 128, W)[0::2] = ev.reshape(16, 128, W)
    out.reshape(32, 128, W)[1::2] = od.reshape(16, 128, W)
    return out


def host_inputs_l0a_attn(inputs, resA):
    projT = np.ascontiguousarray(inputs["L0_attn_project"].T)
    x = inputs["L0_attn_x"]
    posm = np.full((1, 128), float(np.asarray(inputs["L0_attn_pos_mult"])), np.float32)
    consts = mkconsts()
    maps = []
    for (bb, half) in cores_bh():
        c = 2 * bb + half
        kvf = host_kv_global([resA[2 * bb]["kv"], resA[2 * bb + 1]["kv"]])
        Rb, Mb = mk_band(half, "alibi")
        m = {"qf": resA[c]["qf"], "kvf": kvf, "gl": resA[c]["gl"],
             "projT": projT,
             "xin": np.ascontiguousarray(x[bb, core_perm(c), :].T),
             "murho": resA[c]["murho"], "bandR": Rb, "bandM": Mb, "posm": posm}
        m.update(consts)
        maps.append(m)
    return maps



def _kvrow_ag(kt):
    return (kt % 2) * T + (kt // 2) * 128


def _cc_pair_gather(b, src, dst):
    b.s.op("gpsimd", lambda e: e.collective_compute(
        "AllGather", ALU.bypass,
        replica_groups=[[0, 1], [2, 3], [4, 5], [6, 7]],
        ins=[src.ap().opt()], outs=[dst.ap().opt()]),
        reads=[src], writes=[dst], kind="cc")


def build_fused():
    b = Build("fused")
    din, dint, dout = b.din, b.dint, b.dout
    xin = din("xin", [D, T], F32)
    expT = din("expT", [D, 8704], F32)
    gam = din("gam", [128, 16], F32)
    bet = din("bet", [128, 16], F32)
    projT = din("projT", [4096, D], F32)
    bandR = din("bandR", [128, BANDW], F32)
    bandM = din("bandM", [128, BANDW], F32)
    posm = din("posm", [1, 128], F32)
    gwT = din("gwT", [128, 16 * 128], F32)
    sgw = din("sgw", [128, 16], F32)
    wg = din("wg", [8 * D, I_MOE], F32)
    wu = din("wu", [8 * D, I_MOE], F32)
    swg = din("swg", [D, IS], F32)
    swu = din("swu", [D, IS], F32)
    wdf = din("wdf", [8 * I_MOE, D], F32)
    swdf = din("swdf", [IS, D], F32)
    wq = din("wq", [D, D], F32)
    wk = din("wk", [D, 512], F32)
    wv = din("wv", [D, 512], F32)
    ctab = din("ctab", [128, 16 * 512], BF16)
    stab = din("stab", [128, 16 * 512], BF16)
    wo = din("wo", [D, D], F32)
    bandM2 = din("bandM2", [128, 2048], F32)
    wqkvT = din("wqkvT", [D, 6144], F32)
    wgT2 = din("wgT2", [128, 16 * 128], F32)
    b.load_consts()
    kv = dint("i_kv", [T, 2304], BF16)
    kvg = dint("i_kvg", [S, 2304], BF16)
    qf = dint("i_qf", [256, T], BF16)
    gl = dint("i_gl", [D, T], BF16)
    murho = dint("i_murho", [2, T], F32)
    x1 = dint("i_x1", [D, T], F32)
    comb = dint("i_comb", [8, T], BF16)
    grow = dint("i_grow", [1, T], F32)
    hh = dint("i_hh", [8 * I_MOE, T], BF16)
    sh = dint("i_sh", [IS, T], BF16)
    wdb = dint("i_wdb", [8 * I_MOE, D], BF16)
    swdb = dint("i_swdb", [IS, D], BF16)
    x2 = dint("i_x2", [D, T], F32)
    q_tm = dint("i_qtm", [T, D], BF16)
    kv1 = dint("i_kv1", [T, 1024], BF16)
    kv1g = dint("i_kv1g", [S, 1024], BF16)
    x3 = dint("i_x3", [D, T], F32)
    q2 = dint("i_q2", [D, T], BF16)
    ksvs = dint("i_ksvs", [T, 257], BF16)
    ksvsg = dint("i_ksvsg", [S, 257], BF16)
    x4 = dout("x4", [D, T], F32)

    with b.scope():
        build_l0a_pre(b, io={"xin": xin, "expT": expT, "gam": gam, "bet": bet,
                             "kv": kv, "qf": qf, "gl": gl, "murho": murho})
    _cc_pair_gather(b, kv, kvg)
    with b.scope():
        build_l0a_attn(b, io={"qf": qf, "kvf": kvg, "gl": gl, "projT": projT,
                              "xin": xin, "murho": murho, "bandR": bandR,
                              "bandM": bandM, "posm": posm, "x1": x1,
                              "kvrow": _kvrow_ag})
    with b.scope():
        xln_sb = [b.sb(f"fxln{ch}", [128, T], BF16) for ch in range(16)]
        with b.scope():
            build_l0f_pre(b, io={"xin": x1, "gwT": gwT, "sgw": sgw,
                                 "comb": comb, "grow": grow, "xln_sb": xln_sb})
        with b.scope():
            build_l0f_hh(b, io={"wg": wg, "wu": wu, "comb": comb, "hh": hh,
                                "xln_sb": xln_sb})
        with b.scope():
            build_l0f_sh(b, io={"swg": swg, "swu": swu, "grow": grow,
                                "wdf": wdf, "swdf": swdf, "sh": sh, "wdb": wdb,
                                "swdb": swdb, "xln_sb": xln_sb})
    with b.scope():
        build_l0f_down(b, io={"hh": hh, "sh": sh, "wdb": wdb, "swdb": swdb,
                              "x1": x1, "x2": x2})
    with b.scope():
        build_l1a_pre(b, io={"xin": x2, "wq": wq, "wk": wk, "wv": wv,
                             "ctab": ctab, "stab": stab, "q_tm": q_tm, "kv1": kv1,
                             "after_kv": lambda: _cc_pair_gather(b, kv1, kv1g)})
    with b.scope():
        build_l1a_attn(b, io={"q_tm": q_tm, "kvf": kv1g, "wo": wo, "xin": x2,
                              "bandM": bandM2, "x3": x3, "kvrow": _kvrow_ag})
    with b.scope():
        build_l1f_pre(b, io={"xin": x3, "wqkvT": wqkvT, "wgT": wgT2,
                             "q_fm": q2, "ksvs": ksvs})
    _cc_pair_gather(b, ksvs, ksvsg)
    with b.scope():
        build_l1f_attn(b, io={"q_fm": q2, "ksvsf": ksvsg, "xin": x3,
                              "bandM": bandM2, "x4": x4, "kvrow": _kvrow_ag})
    return b.finish()


def host_inputs_fused(inputs):
    x = inputs["L0_attn_x"]
    consts = mkconsts()
    expT = np.ascontiguousarray(inputs["L0_attn_expand"].T)
    gam = np.ascontiguousarray(inputs["L0_attn_ln_gamma"].reshape(16, 128).T)
    bet = np.ascontiguousarray(inputs["L0_attn_ln_beta"].reshape(16, 128).T)
    projT = np.ascontiguousarray(inputs["L0_attn_project"].T)
    posm = np.full((1, 128), float(np.asarray(inputs["L0_attn_pos_mult"])), np.float32)
    gwT_small = inputs["L0_ffn_gate_w"].T.reshape(16, 128, 8).transpose(1, 0, 2)
    gwT = np.zeros((128, 16, 128), np.float32)
    gwT[:, :, 0:8] = gwT_small
    gwT = np.ascontiguousarray(gwT.reshape(128, 16 * 128))
    sgw = np.ascontiguousarray(inputs["L0_ffn_shared_gate_w"].reshape(16, 128).T)
    wg_ = np.ascontiguousarray(inputs["L0_ffn_w_gate"].reshape(8 * D, I_MOE))
    wu_ = np.ascontiguousarray(inputs["L0_ffn_w_up"].reshape(8 * D, I_MOE))
    wdf = np.ascontiguousarray(inputs["L0_ffn_w_down"].reshape(8 * I_MOE, D))
    wqkvT = np.ascontiguousarray(inputs["L1_ffn_W_qkv"].T)
    wgT2_small = inputs["L1_ffn_W_gate"].T.reshape(16, 128, 16).transpose(1, 0, 2)
    wgT2 = np.zeros((128, 16, 128), np.float32)
    wgT2[:, :, 0:16] = wgT2_small
    wgT2 = np.ascontiguousarray(wgT2.reshape(128, 16 * 128))
    maps = []
    for (bb, half) in cores_bh():
        c = 2 * bb + half
        Rb, Mb = mk_band(half, "alibi")
        ct, st = host_rope_tables(c)
        m = {"xin": np.ascontiguousarray(x[bb, core_perm(c), :].T),
             "expT": expT, "gam": gam, "bet": bet, "projT": projT,
             "bandR": Rb, "bandM": Mb, "posm": posm,
             "gwT": gwT, "sgw": sgw, "wg": wg_, "wu": wu_,
             "swg": inputs["L0_ffn_sw_gate"], "swu": inputs["L0_ffn_sw_up"],
             "wdf": wdf, "swdf": inputs["L0_ffn_sw_down"],
             "wq": inputs["L1_attn_Wq"], "wk": inputs["L1_attn_Wk"],
             "wv": inputs["L1_attn_Wv"], "ctab": ct, "stab": st,
             "wo": inputs["L1_attn_Wo"],
             "bandM2": np.ascontiguousarray(mk_band(half, "mask")[:, 0:2048]),
             "wqkvT": wqkvT, "wgT2": wgT2}
        m.update(consts)
        maps.append(m)
    return maps


_NC_CACHE = {}

def _get_nc(name, builder):
    if name not in _NC_CACHE:
        _NC_CACHE[name] = builder()
    return _NC_CACHE[name]


def kernel(**inputs):
    nc = _get_nc("fused", build_fused)
    res = run_bass_kernel_spmd(nc, host_inputs_fused(inputs),
                               core_ids=list(range(NCORE))).results
    out = np.empty((B, S, D), np.float32)
    for c in range(NCORE):
        out[c // 2, core_perm(c), :] = np.asarray(res[c]["x4"]).T
    return out


def kernel_unfused(**inputs):
    cores = list(range(NCORE))
    resA = run_bass_kernel_spmd(_get_nc("A", build_l0a_pre),
                                host_inputs_l0a_pre(inputs), core_ids=cores).results
    resB = run_bass_kernel_spmd(_get_nc("B", build_l0a_attn),
                                host_inputs_l0a_attn(inputs, resA), core_ids=cores).results
    x1 = [np.asarray(resB[c]["x1"]) for c in cores]
    resC = run_bass_kernel_spmd(_get_nc("C", build_l0f_pre),
                                host_inputs_l0f(inputs, x1, stage="pre"),
                                core_ids=cores).results
    resD = run_bass_kernel_spmd(_get_nc("D", build_l0f_hh),
                                host_inputs_l0f(inputs, x1, resC=resC, stage="hh"),
                                core_ids=cores).results
    resD2 = run_bass_kernel_spmd(_get_nc("D2", build_l0f_sh),
                                 host_inputs_l0f(inputs, x1, resC=resC, stage="sh"),
                                 core_ids=cores).results
    resE = run_bass_kernel_spmd(_get_nc("E", build_l0f_down),
                                host_inputs_l0f(inputs, x1, resD=resD, resD2=resD2,
                                                stage="down"),
                                core_ids=cores).results
    x2 = [np.asarray(resE[c]["x2"]) for c in cores]
    resF = run_bass_kernel_spmd(_get_nc("F", build_l1a_pre),
                                host_inputs_l1(inputs, x2_arrs=x2, stage="pre"),
                                core_ids=cores).results
    resG = run_bass_kernel_spmd(_get_nc("G", build_l1a_attn),
                                host_inputs_l1(inputs, x2_arrs=x2, resF=resF, stage="attn"),
                                core_ids=cores).results
    x3 = [np.asarray(resG[c]["x3"]) for c in cores]
    resH1 = run_bass_kernel_spmd(_get_nc("H1", build_l1f_pre),
                                 host_inputs_l1(inputs, x3_arrs=x3, stage="fpre"),
                                 core_ids=cores).results
    resH2 = run_bass_kernel_spmd(_get_nc("H2", build_l1f_attn),
                                 host_inputs_l1(inputs, x3_arrs=x3, resH1=resH1,
                                                stage="fattn"),
                                 core_ids=cores).results
    out = np.empty((B, S, D), np.float32)
    for c in cores:
        bb = c // 2
        out[bb, core_perm(c), :] = np.asarray(resH2[c]["x4"]).T
    return out


# version 14
# speedup vs baseline: 1.0249x; 1.0246x over previous
import numpy as np
import ml_dtypes
import concourse.bass as bass
import concourse.mybir as mybir
from concourse.bass_utils import run_bass_kernel_spmd
from contextlib import ExitStack

F32 = mybir.dt.float32
BF16 = mybir.dt.bfloat16
AF = mybir.ActivationFunctionType
ALU = mybir.AluOpType
AX = mybir.AxisListType
BF16NP = ml_dtypes.bfloat16

B, S, D = 4, 4096, 2048
T = 2048
NCORE = 8
EPS = 1e-5


ENGINES = ("tensor", "vector", "scalar", "gpsimd", "sync")

class _Op:
    __slots__ = ("eng", "stream", "fn", "reads", "writes", "idx", "inc", "value",
                 "needed", "short")
    def __init__(self, eng, stream, fn, reads, writes, idx, inc):
        self.eng, self.stream, self.fn = eng, stream, fn
        self.reads, self.writes = reads, writes
        self.idx, self.inc = idx, inc
        self.value = None
        self.needed = False
        self.short = 0

class Sched:
    def __init__(self, nc):
        self.nc = nc
        self.ops = []
        self.deps = {}
        self.writers = {}
        self.readers = {}
        self.last_barrier = []
        self.shapes = {}
        self.scalar_spacer = 0
        self.paranoid = False
        self.pads = {}
        self.dma_streams = {}
        self.dma_free = []
        self.n_dma_streams = 0

    def _track(self, op):
        d = set()
        for t in op.reads:
            for w in self.writers.get(id(t), self.last_barrier):
                d.add(w)
            self.readers.setdefault(id(t), []).append(op)
        for t in op.writes:
            for w in self.writers.get(id(t), self.last_barrier):
                d.add(w)
            for r in self.readers.get(id(t), ()):
                d.add(r)
            self.writers[id(t)] = [op]
            self.readers[id(t)] = []
        d.discard(op)
        self.deps[op] = d

    def op(self, eng, fn, reads=(), writes=(), kind="compute", short=0):
        if kind == "compute":
            stream, inc = eng, 1
        elif kind == "dma":
            key = id(writes[0]) if writes else "anon"
            stream = self.dma_streams.get(key)
            if stream is None:
                stream = (self.dma_free.pop() if self.dma_free
                          else f"dma{self.n_dma_streams}")
                if not self.dma_free and stream == f"dma{self.n_dma_streams}":
                    self.n_dma_streams += 1
                self.dma_streams[key] = stream
            inc = 16
        elif kind == "cc":
            stream, inc = "cc", 1
        else:
            raise ValueError(kind)
        o = _Op(eng, stream, fn, list(reads), list(writes), len(self.ops), inc)
        if short:
            o.short = short
        if kind == "compute" and not o.short:
            if eng == "gpsimd":
                o.short = 800
            elif eng == "scalar" and self.scalar_spacer:
                o.short = self.scalar_spacer
            if self.paranoid and eng in ("vector", "scalar"):
                o.short = 1024
            if not o.short:
                for t in o.writes:
                    shp = self.shapes.get(id(t))
                    if shp is not None and (int(np.prod(shp[1:])) < 128 or shp[0] <= 2):
                        o.short = 1024
                        break
        self.ops.append(o)
        self._track(o)
        return o

    def dma(self, eng, out_ap, in_ap, reads, writes, **kw):
        return self.op(eng, lambda e, o=out_ap, i=in_ap: e.dma_start(out=o, in_=i, **kw),
                       reads=reads, writes=writes, kind="dma")

    def barrier(self):
        used = sorted({o.eng for o in self.ops}) or ["sync"]
        dep_all = set()
        for tid in self.writers:
            dep_all.update(self.writers[tid])
        for tid in self.readers:
            dep_all.update(self.readers[tid])
        barws = []
        for eng in used:
            pad = self.pads.get(eng)
            if pad is not None:
                fn = (lambda e, p=pad: e.mul(p[:, :], p[:, :], 1.0)) if eng == "scalar" \
                    else (lambda e, p=pad: e.tensor_scalar(p[:, :], p[:, :], 1.0,
                                                           None, ALU.mult))
            else:
                fn = lambda e: e.nop()
            o = self.op(eng, fn, reads=[], writes=[])
            self.deps[o] = set(d for d in dep_all if d is not o)
            barws.append(o)
        for tid in list(self.writers.keys()):
            self.writers[tid] = list(barws)
        for tid in list(self.readers.keys()):
            self.readers[tid] = []
        self.last_barrier = list(barws)
        self.dma_free.extend(self.dma_streams.values())
        self.dma_streams = {}

    def emit(self):
        nc = self.nc
        for o, ds in self.deps.items():
            for d in ds:
                d.needed = True
        streams = sorted({o.stream for o in self.ops})
        last_per_stream = {}
        for o in self.ops:
            last_per_stream[o.stream] = o
        for o in last_per_stream.values():
            o.needed = True
        counts = {s: 0 for s in streams}
        for o in self.ops:
            if o.needed:
                counts[o.stream] += o.inc
                o.value = counts[o.stream]
        finals = dict(counts)
        with ExitStack() as ctx:
            sems = {s: ctx.enter_context(nc.semaphore(f"sem_{s.replace('.', '_')}"))
                    for s in streams}
            block = ctx.enter_context(nc.Block())
            by_eng = {}
            for o in self.ops:
                by_eng.setdefault(o.eng, []).append(o)

            def make_body(eng, ops):
                def body(e):
                    hwm = {}
                    for o in ops:
                        for d in sorted(self.deps[o], key=lambda x: x.idx):
                            if d.stream == eng and d.inc == 1:
                                continue
                            assert d.value is not None
                            if hwm.get(d.stream, 0) >= d.value:
                                continue
                            e.wait_ge(sems[d.stream], d.value)
                            hwm[d.stream] = d.value
                        inst = o.fn(e)
                        pad = self.pads.get(eng)
                        if o.short and pad is not None:
                            if eng == "scalar":
                                inst = e.mul(pad[:, :], pad[:, :], 1.0)
                            else:
                                inst = e.tensor_scalar(pad[:, :], pad[:, :], 1.0,
                                                       None, ALU.mult)
                        if o.needed:
                            inst.then_inc(sems[o.stream], o.inc)
                    for s in streams:
                        if finals[s] > hwm.get(s, 0) and not (s == eng):
                            e.wait_ge(sems[s], finals[s])
                return body
            for eng in ENGINES:
                if eng in by_eng:
                    getattr(block, eng)(make_body(eng, by_eng[eng]))
        return nc



class Build:
    def __init__(self, name):
        self.nc = bass.Bass("TRN2", target_bir_lowering=False)
        self.s = Sched(self.nc)
        self.ctx = ExitStack()
        self.n_sb = 0

    def _uniq(self, name):
        self.n_sb += 1
        return f"{name}_{self.n_sb}"

    def sb(self, name, shape, dt):
        t = self.ctx.enter_context(self.nc.sbuf_tensor(self._uniq(name), list(shape), dt))
        self.s.shapes[id(t)] = list(shape)
        return t

    def ps(self, name, shape, dt=F32):
        t = self.ctx.enter_context(self.nc.psum_tensor(self._uniq(name), list(shape), dt))
        self.s.shapes[id(t)] = list(shape)
        return t

    def din(self, name, shape, dt):
        t = self.nc.dram_tensor(name, list(shape), dt, kind="ExternalInput")
        self.s.shapes[id(t)] = list(shape)
        return t

    def dout(self, name, shape, dt):
        t = self.nc.dram_tensor(name, list(shape), dt, kind="ExternalOutput")
        self.s.shapes[id(t)] = list(shape)
        return t

    def dint(self, name, shape, dt):
        t = self.nc.dram_tensor(name, list(shape), dt)
        self.s.shapes[id(t)] = list(shape)
        return t

    def finish(self):
        self.s.emit()
        self.ctx.close()
        return self.nc

    from contextlib import contextmanager as _cm

    @_cm
    def scope(self):
        outer = self.ctx
        inner = ExitStack()
        self.ctx = inner
        try:
            yield
        finally:
            inner.close()
            self.ctx = outer
            self.s.barrier()

    def load_consts(self):
        self.i128_d = self.din("c_i128", [128, 128], BF16)
        self.onesr_b_d = self.din("c_onesr_b", [1, 128], BF16)
        self.onesc_b_d = self.din("c_onesc_b", [128, 1], BF16)
        self.onesr_f_d = self.din("c_onesr_f", [1, 128], F32)
        self.misc1_d = self.din("c_misc1", [1, 8], F32)
        self.misc128_d = self.din("c_misc128", [128, 4], F32)
        self.i128 = self.sb("i128", [128, 128], BF16)
        self.onesr_b = self.sb("onesr_b", [1, 128], BF16)
        self.onesc_b = self.sb("onesc_b", [128, 1], BF16)
        self.onesr_f = self.sb("onesr_f", [1, 128], F32)
        self.misc1 = self.sb("misc1", [1, 8], F32)
        self.misc128 = self.sb("misc128", [128, 4], F32)
        for eng in ("vector", "scalar", "gpsimd"):
            self.s.pads[eng] = self.sb(f"pad_{eng}", [128, 512], F32)
        s = self.s
        for dst, src in [(self.i128, self.i128_d), (self.onesr_b, self.onesr_b_d),
                         (self.onesc_b, self.onesc_b_d), (self.onesr_f, self.onesr_f_d),
                         (self.misc1, self.misc1_d), (self.misc128, self.misc128_d)]:
            s.dma("sync", dst[:, :], src.ap(), reads=[src], writes=[dst])

    def bcast_row_f32(self, row_ap, psum, dst, row_reads):
        s = self.s
        s.op("tensor", lambda e, r=row_ap: e.matmul(psum[:, :], self.onesr_f[:, :], r,
                                                    start=True, stop=True),
             reads=[self.onesr_f] + row_reads, writes=[psum])
        s.op("vector", lambda e: e.tensor_copy(dst[:, :], psum[:, :]),
             reads=[psum], writes=[dst])

def mkconsts():
    return {
        "c_i128": np.eye(128, dtype=BF16NP),
        "c_onesr_b": np.ones((1, 128), BF16NP),
        "c_onesc_b": np.ones((128, 1), BF16NP),
        "c_onesr_f": np.ones((1, 128), np.float32),
        "c_misc1": np.array([[1.0, EPS, EPS * EPS, 0, 0, 0, 0, 0]], np.float32),
        "c_misc128": np.tile(np.array([[1.0, EPS, EPS * EPS, 0]], np.float32), (128, 1)),
    }

def _rearr_cpT(dram_ap, p=128):
    return dram_ap.rearrange("(c p) t -> p c t", p=p)



def emit_layernorm(b: Build, x_dram, x_ln, gamma_beta=None, rho_mode="single",
                   want_r1=False, tag=""):
    s = b.s
    nst = T // 512
    x_stts = [b.sb(f"lnx{tag}", [128, 16 * 512], F32) for _ in range(2)]
    x_bfs = [b.sb(f"lnxb{tag}", [128, 16 * 512], BF16)] * 2
    xsq_bfs = [b.sb(f"lnxs{tag}", [128, 16 * 512], BF16)] * 2
    _rows = b.sb(f"lnrows{tag}", [1, 6 * 512], F32)
    mu_row = _rows[0:1, 0:512]
    var_row = _rows[0:1, 512:1024]
    rho_row = _rows[0:1, 1024:1536]
    nurho_row = _rows[0:1, 1536:2048]
    tmp_row = _rows[0:1, 2048:2560]
    tmp2_row = _rows[0:1, 2560:3072]
    r1_row = b.sb(f"lnr1{tag}", [1, T], F32) if want_r1 else None
    nu1_row = b.sb(f"lnnu1{tag}", [1, T], F32) if want_r1 else None
    ps_sum = b.ps(f"lnps1{tag}", [1, 512])
    ps_sq = b.ps(f"lnps2{tag}", [1, 512])
    ps_bc1 = b.ps(f"lnbc1{tag}", [128, 512])
    ps_bc2 = b.ps(f"lnbc2{tag}", [128, 512])
    rho_bc = b.sb(f"lnrbc{tag}", [128, 512], F32)
    nur_bc = b.sb(f"lnnbc{tag}", [128, 512], F32)
    u1 = b.sb(f"lnu1{tag}", [128, 512], F32)
    xv = _rearr_cpT(x_dram.ap())
    for stt in range(nst):
        x_stt, x_bf, xsq_bf = x_stts[stt % 2], x_bfs[stt % 2], xsq_bfs[stt % 2]
        sl = slice(stt * 512, stt * 512 + 512)
        s.dma("sync", x_stt[:, :].rearrange("p (c t) -> p c t", t=512),
              xv[:, :, sl], reads=[x_dram], writes=[x_stt])
        s.op("vector", lambda e, x_bf=x_bf, x_stt=x_stt: e.tensor_copy(
            x_bf[:, :], x_stt[:, :]),
             reads=[x_stt], writes=[x_bf])
        s.op("scalar", lambda e, xsq_bf=xsq_bf, x_stt=x_stt: e.activation(
            xsq_bf[:, :], x_stt[:, :], AF.Square),
             reads=[x_stt], writes=[xsq_bf])
        for ch in range(16):
            c0 = ch * 512
            s.op("tensor", lambda e, c0=c0, st=(ch == 0), x_bf=x_bf: e.matmul(
                ps_sum[:, :], b.onesc_b[:, :], x_bf[:, c0:c0 + 512],
                start=st, stop=(c0 == 15 * 512)),
                 reads=[b.onesc_b, x_bf], writes=[ps_sum])
        for ch in range(16):
            c0 = ch * 512
            s.op("tensor", lambda e, c0=c0, st=(ch == 0), xsq_bf=xsq_bf: e.matmul(
                ps_sq[:, :], b.onesc_b[:, :], xsq_bf[:, c0:c0 + 512],
                start=st, stop=(c0 == 15 * 512)),
                 reads=[b.onesc_b, xsq_bf], writes=[ps_sq])
        s.op("vector", lambda e, sl=sl: e.tensor_scalar(mu_row[0:1, :], ps_sum[:, :],
                                                        1.0 / D, None, ALU.mult),
             reads=[ps_sum], writes=[mu_row])
        s.op("scalar", lambda e, sl=sl: e.activation(tmp_row[0:1, :], mu_row[0:1, :],
                                                     AF.Square),
             reads=[mu_row], writes=[tmp_row])
        s.op("vector", lambda e, sl=sl: e.scalar_tensor_tensor(
            var_row[0:1, :], ps_sq[:, :], 1.0 / D, tmp_row[0:1, :],
            ALU.mult, ALU.subtract),
             reads=[ps_sq, tmp_row], writes=[var_row])
        if rho_mode == "double":
            s.op("scalar", lambda e, sl=sl: e.activation(
                tmp_row[0:1, :], var_row[0:1, :], AF.Sqrt,
                bias=b.misc1[0:1, 2:3], scale=1.0 + EPS),
                 reads=[var_row, b.misc1], writes=[tmp_row])
        else:
            s.op("scalar", lambda e, sl=sl: e.activation(
                tmp_row[0:1, :], var_row[0:1, :], AF.Sqrt, bias=b.misc1[0:1, 1:2]),
                 reads=[var_row, b.misc1], writes=[tmp_row])
        s.op("vector", lambda e, sl=sl: e.reciprocal(rho_row[0:1, :], tmp_row[0:1, :]),
             reads=[tmp_row], writes=[rho_row])
        s.op("vector", lambda e, sl=sl: e.tensor_mul(nurho_row[0:1, :],
                                                     mu_row[0:1, :], rho_row[0:1, :]),
             reads=[mu_row, rho_row], writes=[nurho_row])
        if want_r1:
            s.op("scalar", lambda e, sl=sl: e.activation(
                tmp2_row[0:1, :], var_row[0:1, :], AF.Sqrt, bias=b.misc1[0:1, 1:2]),
                 reads=[var_row, b.misc1], writes=[tmp2_row])
            s.op("vector", lambda e, sl=sl: e.reciprocal(r1_row[0:1, sl], tmp2_row[0:1, :]),
                 reads=[tmp2_row], writes=[r1_row])
            s.op("vector", lambda e, sl=sl: e.tensor_mul(nu1_row[0:1, sl],
                                                         mu_row[0:1, :], r1_row[0:1, sl]),
                 reads=[mu_row, r1_row], writes=[nu1_row])
        s.op("tensor", lambda e, sl=sl: e.matmul(ps_bc1[:, :], b.onesr_f[:, :],
                                                 rho_row[0:1, :], start=True, stop=True),
             reads=[b.onesr_f, rho_row], writes=[ps_bc1])
        s.op("vector", lambda e: e.tensor_copy(rho_bc[:, :], ps_bc1[:, :]),
             reads=[ps_bc1], writes=[rho_bc])
        s.op("tensor", lambda e, sl=sl: e.matmul(ps_bc2[:, :], b.onesr_f[:, :],
                                                 nurho_row[0:1, :], start=True, stop=True),
             reads=[b.onesr_f, nurho_row], writes=[ps_bc2])
        s.op("vector", lambda e: e.tensor_copy(nur_bc[:, :], ps_bc2[:, :]),
             reads=[ps_bc2], writes=[nur_bc])
        for ch in range(16):
            c0 = ch * 512
            s.op("vector", lambda e, c0=c0, x_stt=x_stt: e.tensor_mul(
                u1[:, :], x_stt[:, c0:c0 + 512], rho_bc[:, :]),
                 reads=[x_stt, rho_bc], writes=[u1])
            if gamma_beta is None:
                s.op("vector", lambda e, c0=c0, sl=sl, ch=ch: e.tensor_sub(
                    x_ln[ch][:, sl], u1[:, :], nur_bc[:, :]),
                     reads=[u1, nur_bc], writes=[x_ln[ch]])
            else:
                gfm, bfm = gamma_beta
                s.op("vector", lambda e, c0=c0: e.tensor_sub(u1[:, :], u1[:, :],
                                                             nur_bc[:, :]),
                     reads=[u1, nur_bc], writes=[u1])
                s.op("vector", lambda e, sl=sl, ch=ch: e.tensor_scalar(
                    x_ln[ch][:, sl], u1[:, :], gfm[:, ch:ch + 1], bfm[:, ch:ch + 1],
                    ALU.mult, ALU.add),
                     reads=[u1, gfm, bfm], writes=[x_ln[ch]])
    return r1_row, nu1_row



def build_l0a_pre(b=None, io=None):
    standalone = b is None
    if standalone:
        b = Build("l0a_pre")
    s = b.s
    if io is None:
        io = {"xin": b.din("xin", [D, T], F32), "expT": b.din("expT", [D, 8704], F32),
              "gam": b.din("gam", [128, 16], F32), "bet": b.din("bet", [128, 16], F32),
              "kv": b.dout("kv", [T, 2304], BF16), "qf": b.dout("qf", [256, T], BF16),
              "gl": b.dout("gl", [D, T], BF16), "murho": b.dout("murho", [2, T], F32)}
    xin, expT, gam, bet = io["xin"], io["expT"], io["gam"], io["bet"]
    kv, qf, gl, murho = io["kv"], io["qf"], io["gl"], io["murho"]
    if standalone:
        b.load_consts()
    gfm = b.sb("gfm", [128, 16], F32)
    bfm = b.sb("bfm", [128, 16], F32)
    s.dma("sync", gfm[:, :], gam.ap(), reads=[gam], writes=[gfm])
    s.dma("sync", bfm[:, :], bet.ap(), reads=[bet], writes=[bfm])
    x_ln = [b.sb(f"xln{ch}", [128, T], BF16) for ch in range(16)]
    with b.scope():
        r1_row, nu1_row = emit_layernorm(b, xin, x_ln, gamma_beta=(gfm, bfm),
                                         rho_mode="double", want_r1=True)
        s.dma("sync", murho.ap()[0:1, :], r1_row[0:1, :], reads=[r1_row], writes=[murho])
        s.dma("sync", murho.ap()[1:2, :], nu1_row[0:1, :], reads=[nu1_row], writes=[murho])

    expTv = expT.ap()
    with b.scope():
        ebufAs = [b.sb("ebA", [128, 16 * 128], F32) for _ in range(2)]
        ebufAbs = [b.sb("ebAb", [128, 16 * 128], BF16) for _ in range(2)]
        ebufBs = [b.sb("ebB", [128, 16 * 128], F32) for _ in range(2)]
        ebufBbs = [b.sb("ebBb", [128, 16 * 128], BF16) for _ in range(2)]
        psA = [b.ps(f"psA{i}", [128, 512]) for i in range(2)]
        psB = [b.ps(f"psB{i}", [128, 512]) for i in range(2)]
        gsc = b.sb("gsc", [128, 512], F32)
        qsc = b.sb("qsc", [128, 512], BF16)
        glsc = b.sb("glsc", [128, 512], BF16)

        def load_ecol(buf, bufb, col0):
            s.dma("sync", buf[:, :].rearrange("p (k o) -> p k o", o=128),
                  expTv.rearrange("(k p) o -> p k o", p=128)[:, :, col0:col0 + 128],
                  reads=[expT], writes=[buf])
            s.op("vector", lambda e, buf=buf, bufb=bufb: e.tensor_copy(bufb[:, :], buf[:, :]),
                 reads=[buf], writes=[bufb])

        for oc in range(2):
            ebufAb = ebufAbs[oc % 2]
            load_ecol(ebufAs[oc % 2], ebufAb, oc * 128)
            for stt in range(4):
                sl = slice(stt * 512, stt * 512 + 512)
                pa = psA[stt % 2]
                for k in range(16):
                    s.op("tensor", lambda e, k=k, sl=sl, pa=pa, ebufAb=ebufAb: e.matmul(
                        pa[:, :], ebufAb[:, k * 128:(k + 1) * 128], x_ln[k][:, sl],
                        start=(k == 0), stop=(k == 15)),
                         reads=[ebufAb, x_ln[k]], writes=[pa])
                s.op("scalar", lambda e, pa=pa: e.mul(qsc[:, :], pa[:, :], 1.0 / 16.0),
                     reads=[pa], writes=[qsc])
                s.dma("scalar", qf.ap()[oc * 128:(oc + 1) * 128, sl], qsc[:, :],
                      reads=[qsc], writes=[qf])
        for j in range(16):
            ebufAb, ebufBb = ebufAbs[j % 2], ebufBbs[j % 2]
            load_ecol(ebufAs[j % 2], ebufAb, 512 + j * 128)
            load_ecol(ebufBs[j % 2], ebufBb, 4608 + j * 128)
            for stt in range(4):
                sl = slice(stt * 512, stt * 512 + 512)
                pa, pb = psA[stt % 2], psB[stt % 2]
                for k in range(16):
                    s.op("tensor", lambda e, k=k, sl=sl, pa=pa, ebufAb=ebufAb: e.matmul(
                        pa[:, :], ebufAb[:, k * 128:(k + 1) * 128], x_ln[k][:, sl],
                        start=(k == 0), stop=(k == 15)),
                         reads=[ebufAb, x_ln[k]], writes=[pa])
                for k in range(16):
                    s.op("tensor", lambda e, k=k, sl=sl, pb=pb, ebufBb=ebufBb: e.matmul(
                        pb[:, :], ebufBb[:, k * 128:(k + 1) * 128], x_ln[k][:, sl],
                        start=(k == 0), stop=(k == 15)),
                         reads=[ebufBb, x_ln[k]], writes=[pb])
                s.op("scalar", lambda e, pb=pb: e.activation(gsc[:, :], pb[:, :], AF.Gelu),
                     reads=[pb], writes=[gsc], short=500)
                s.op("vector", lambda e, pa=pa: e.tensor_mul(glsc[:, :], pa[:, :], gsc[:, :]),
                     reads=[pa, gsc], writes=[glsc])
                s.dma("scalar", gl.ap()[j * 128:(j + 1) * 128, sl], glsc[:, :],
                      reads=[glsc], writes=[gl])

    with b.scope():
        ebigFs = [b.sb("egF", [128, 16 * 512], F32) for _ in range(2)]
        ebigAbs = [b.sb("egAb", [128, 16 * 512], BF16) for _ in range(2)]
        ebigBbs = [b.sb("egBb", [128, 16 * 512], BF16) for _ in range(2)]
        psA = [b.ps(f"psA{i}", [128, 512]) for i in range(2)]
        psB = [b.ps(f"psB{i}", [128, 512]) for i in range(2)]
        gsc = b.sb("gsc", [128, 512], F32)
        ksc = b.sb("ksc", [128, 256], BF16)
        vsc = b.sb("vsc", [128, 512], BF16)
        psK = b.ps("psK", [128, 256])

        def load_ebig(bufb, col0, width, par):
            ebigF = ebigFs[par]
            s.dma("sync", ebigF[:, 0:16 * width].rearrange("p (k o) -> p k o", o=width),
                  expTv.rearrange("(k p) o -> p k o", p=128)[:, :, col0:col0 + width],
                  reads=[expT], writes=[ebigF])
            s.op("vector", lambda e, w=width, bufb=bufb, ebigF=ebigF: e.tensor_copy(
                bufb[:, 0:16 * w], ebigF[:, 0:16 * w]),
                 reads=[ebigF], writes=[bufb])

        ebigAb = ebigAbs[0]
        load_ebig(ebigAb, 256, 256, 0)
        for tt in range(16):
            t0 = tt * 128
            for k in range(16):
                s.op("tensor", lambda e, k=k, t0=t0, ebigAb=ebigAb: e.matmul(
                    psK[:, :], x_ln[k][:, t0:t0 + 128], ebigAb[:, k * 256:(k + 1) * 256],
                    start=(k == 0), stop=(k == 15)),
                     reads=[x_ln[k], ebigAb], writes=[psK])
            s.op("vector", lambda e: e.tensor_copy(ksc[:, :], psK[:, :]),
                 reads=[psK], writes=[ksc])
            s.dma("scalar", kv.ap()[t0:t0 + 128, 0:256], ksc[:, :], reads=[ksc], writes=[kv])
        for j in range(4):
            ebigAb, ebigBb = ebigAbs[j % 2], ebigBbs[j % 2]
            load_ebig(ebigAb, 2560 + j * 512, 512, j % 2)
            load_ebig(ebigBb, 6656 + j * 512, 512, (j + 1) % 2)
            for tt in range(16):
                t0 = tt * 128
                pa, pb = psA[tt % 2], psB[tt % 2]
                for k in range(16):
                    s.op("tensor", lambda e, k=k, t0=t0, pa=pa, ebigAb=ebigAb: e.matmul(
                        pa[:, :], x_ln[k][:, t0:t0 + 128], ebigAb[:, k * 512:(k + 1) * 512],
                        start=(k == 0), stop=(k == 15)),
                         reads=[x_ln[k], ebigAb], writes=[pa])
                for k in range(16):
                    s.op("tensor", lambda e, k=k, t0=t0, pb=pb, ebigBb=ebigBb: e.matmul(
                        pb[:, :], x_ln[k][:, t0:t0 + 128], ebigBb[:, k * 512:(k + 1) * 512],
                        start=(k == 0), stop=(k == 15)),
                         reads=[x_ln[k], ebigBb], writes=[pb])
                s.op("scalar", lambda e, pb=pb: e.activation(gsc[:, :], pb[:, :], AF.Gelu),
                     reads=[pb], writes=[gsc], short=500)
                s.op("vector", lambda e, pa=pa: e.tensor_mul(vsc[:, :], pa[:, :], gsc[:, :]),
                     reads=[pa, gsc], writes=[vsc])
                s.dma("scalar", kv.ap()[t0:t0 + 128, 256 + j * 512:256 + (j + 1) * 512],
                      vsc[:, :], reads=[vsc], writes=[kv])
    return b.finish() if standalone else None



BANDW = 5120
OFFC = 1024

def band_view(band, qg, kt):
    colbase = (8 * qg - kt) * 128 + OFFC
    return band[:, colbase:colbase + 1024].rearrange(
        "p (j t u) -> p j t u", t=2, u=128)[:, :, 0:1, :]

def ps_view(pa):
    return pa[:, 0:512].rearrange("p (j t u) -> p j t u", t=1, u=128)

def mk_band(half, kind):
    jj = np.arange(BANDW)[None, :]
    ii = np.arange(128)[:, None]
    rel = (ii + OFFC - jj - 128 * half).astype(np.float32)
    M = np.where(rel <= 0, 0.0, -60000.0).astype(np.float32)
    if kind == "alibi":
        R = np.minimum(rel, 0.0).astype(np.float32)
        return R, M
    return M

def nkt_of(qg):
    return 8 * qg + 8


def build_l0a_attn(b=None, io=None):
    standalone = b is None
    if standalone:
        b = Build("l0a_attn")
    s = b.s
    if io is None:
        io = {"qf": b.din("qf", [256, T], BF16), "kvf": b.din("kvf", [S, 2304], BF16),
              "gl": b.din("gl", [D, T], BF16), "projT": b.din("projT", [4096, D], F32),
              "xin": b.din("xin", [D, T], F32), "murho": b.din("murho", [2, T], F32),
              "bandR": b.din("bandR", [128, BANDW], F32),
              "bandM": b.din("bandM", [128, BANDW], F32),
              "posm": b.din("posm", [1, 128], F32), "x1": b.dout("x1", [D, T], F32)}
    qf, kvf, gl, projT, xin = io["qf"], io["kvf"], io["gl"], io["projT"], io["xin"]
    murho, bandR, bandM, posm, x1 = (io["murho"], io["bandR"], io["bandM"],
                                     io["posm"], io["x1"])
    kvrow = io.get("kvrow", lambda kt: kt * 128)
    if standalone:
        b.load_consts()

    band = b.sb("band", [128, BANDW], F32)
    with b.scope():
        mstage = b.sb("mstage", [128, BANDW], F32)
        pm = b.sb("pm", [1, 128], F32)
        sp_col = b.sb("spcol", [128, 1], F32)
        t1 = b.sb("t1", [128, 1], F32)
        t2 = b.sb("t2", [128, 1], F32)
        psb = b.ps("psb", [128, 128])
        s.dma("sync", band[:, :], bandR.ap(), reads=[bandR], writes=[band])
        s.dma("sync", mstage[:, :], bandM.ap(), reads=[bandM], writes=[mstage])
        s.dma("sync", pm[:, :], posm.ap(), reads=[posm], writes=[pm])
        s.op("tensor", lambda e: e.matmul(psb[:, :], b.onesr_f[:, :], pm[0:1, :],
                                          start=True, stop=True),
             reads=[b.onesr_f, pm], writes=[psb])
        s.op("vector", lambda e: e.tensor_copy(t1[:, :], psb[:, 0:1]),
             reads=[psb], writes=[t1])
        s.op("scalar", lambda e: e.activation(t2[:, :], t1[:, :], AF.Abs),
             reads=[t1], writes=[t2])
        s.op("scalar", lambda e: e.activation(t2[:, :], t2[:, :], AF.Exp, scale=-1.0),
             reads=[t2], writes=[t2])
        s.op("scalar", lambda e: e.activation(t2[:, :], t2[:, :], AF.Ln,
                                              bias=b.misc128[:, 0:1]),
             reads=[t2, b.misc128], writes=[t2])
        s.op("vector", lambda e: e.tensor_scalar(sp_col[:, :], t1[:, :], 0.0, None,
                                                 ALU.max),
             reads=[t1], writes=[sp_col])
        s.op("vector", lambda e: e.tensor_add(sp_col[:, :], sp_col[:, :], t2[:, :]),
             reads=[sp_col, t2], writes=[sp_col])
        s.op("vector", lambda e: e.tensor_scalar(band[:, :], band[:, :], sp_col[:, :],
                                                 None, ALU.mult),
             reads=[band, sp_col], writes=[band])
        s.op("vector", lambda e: e.tensor_add(band[:, :], band[:, :], mstage[:, :]),
             reads=[band, mstage], writes=[band])

    k_fm = [b.sb(f"kfm{c}", [128, S], BF16) for c in range(2)]
    q_fm = [b.sb(f"qfm{c}", [128, T], BF16) for c in range(2)]
    r1p1_row = b.sb("r1p1", [1, T], F32)
    nu1_row = b.sb("nu1", [1, T], F32)
    with b.scope():
        kseg = [b.sb(f"kseg{i}", [128, 256], BF16) for i in range(2)]
        pst = b.ps("pst", [128, 128], BF16)
        for c in range(2):
            s.dma("sync", q_fm[c][:, :], qf.ap()[c * 128:(c + 1) * 128, :],
                  reads=[qf], writes=[q_fm[c]])
        s.dma("sync", r1p1_row[0:1, :], murho.ap()[0:1, :], reads=[murho], writes=[r1p1_row])
        s.dma("sync", nu1_row[0:1, :], murho.ap()[1:2, :], reads=[murho], writes=[nu1_row])
        s.op("vector", lambda e: e.tensor_scalar(r1p1_row[0:1, :], r1p1_row[0:1, :],
                                                 1.0, None, ALU.add),
             reads=[r1p1_row], writes=[r1p1_row])
        for kt in range(32):
            sg = kseg[kt % 2]
            s.dma("sync", sg[:, :], kvf.ap()[kvrow(kt):kvrow(kt) + 128, 0:256],
                  reads=[kvf], writes=[sg])
            for c in range(2):
                s.op("tensor", lambda e, sg=sg, c=c: e.transpose(
                    pst[:, :], sg[:, c * 128:(c + 1) * 128], b.i128[:, :]),
                     reads=[sg, b.i128], writes=[pst])
                s.op("vector", lambda e, kt=kt, c=c: e.tensor_copy(
                    k_fm[c][:, kt * 128:(kt + 1) * 128], pst[:, :]),
                     reads=[pst], writes=[k_fm[c]])

    with b.scope():
        probsT = [b.sb(f"pT{i}", [128, 512], BF16) for i in range(32)]
        ps_sc = [b.ps(f"pssc{i}", [128, 512]) for i in range(2)]
        ps_z = b.ps("psz", [1, 512])
        ps_oh = [b.ps(f"psoh{i}", [128, 1024]) for i in range(2)]
        ps_t = b.ps("pst", [128, 512], BF16)
        sc_f = [b.sb(f"scf{i}", [128, 512], F32) for i in range(2)]
        vbuf = [b.sb(f"vb{i}", [128, 1024], BF16) for i in range(2)]
        zrow = b.sb("zrow", [1, 512], F32)
        rz = b.sb("rz", [1, 512], F32)
        rzc = b.sb("rzc", [128, 1], F32)
        atm2 = [b.sb(f"attm{i}", [128, 2048], BF16) for i in range(2)]
        afm = b.sb("afm", [128, 16 * 512], BF16)
        glbuf = b.sb("glbuf", [128, 16 * 512], BF16)
        pbufF = b.sb("pbufF", [128, 32 * 128], F32)
        pbufB = [b.sb(f"pbufB{i}", [128, 32 * 128], BF16) for i in range(2)]
        xbuf = [b.sb(f"xb{i}", [128, 512], F32) for i in range(2)]
        x1t = b.sb("x1t", [128, 512], F32)
        rb_bc = b.sb("rbbc", [128, 512], F32)
        nb_bc = b.sb("nbbc", [128, 512], F32)
        for qg in range(4):
            nkt = nkt_of(qg)
            qsl = slice(qg * 512, qg * 512 + 512)
            for kt in range(nkt):
                pa = ps_sc[kt % 2]
                sf = sc_f[kt % 2]
                for c in range(2):
                    s.op("tensor", lambda e, kt=kt, c=c, pa=pa, qsl=qsl: e.matmul(
                        pa[:, :], k_fm[c][:, kt * 128:(kt + 1) * 128], q_fm[c][:, qsl],
                        start=(c == 0), stop=(c == 1)),
                         reads=[k_fm[c], q_fm[c]], writes=[pa])
                s.op("vector", lambda e, pa=pa, sf=sf, qg=qg, kt=kt: e.scalar_tensor_tensor(
                    ps_view(sf), ps_view(pa), 1.0, band_view(band, qg, kt),
                    ALU.mult, ALU.add),
                     reads=[pa, band], writes=[sf])
                s.op("scalar", lambda e, sf=sf, kt=kt: e.activation(
                    probsT[kt][:, :], sf[:, :], AF.Exp),
                     reads=[sf], writes=[probsT[kt]])
            for kt in range(nkt):
                s.op("tensor", lambda e, kt=kt: e.matmul(
                    ps_z[:, :], b.onesc_b[:, :], probsT[kt][:, :],
                    start=(kt == 0), stop=(kt == nkt - 1)),
                     reads=[b.onesc_b, probsT[kt]], writes=[ps_z])
            s.op("vector", lambda e: e.tensor_copy(zrow[0:1, :], ps_z[:, :]),
                 reads=[ps_z], writes=[zrow])
            s.op("vector", lambda e: e.reciprocal(rz[0:1, :], zrow[0:1, :]),
                 reads=[zrow], writes=[rz])
            for qp in range(2):
                for vh in range(2):
                    for kt in range(nkt):
                        vb = vbuf[kt % 2]
                        s.dma("sync", vb[:, 0:1024],
                              kvf.ap()[kvrow(kt):kvrow(kt) + 128,
                                       256 + vh * 1024:256 + (vh + 1) * 1024],
                              reads=[kvf], writes=[vb])
                        for qi in range(2):
                            qsub = qp * 2 + qi
                            po = ps_oh[qi]
                            for vd in range(2):
                                s.op("tensor", lambda e, kt=kt, vd=vd, vb=vb,
                                     qsub=qsub, po=po, nkt=nkt: e.matmul(
                                         po[:, vd * 512:(vd + 1) * 512],
                                         probsT[kt][:, qsub * 128:(qsub + 1) * 128],
                                         vb[:, vd * 512:(vd + 1) * 512],
                                         start=(kt == 0), stop=(kt == nkt - 1)),
                                     reads=[probsT[kt], vb], writes=[po])
                    for qi in range(2):
                        qsub = qp * 2 + qi
                        s.op("tensor", lambda e, qsub=qsub: e.matmul(
                            ps_sc[0][:, 0:128], rz[0:1, qsub * 128:(qsub + 1) * 128],
                            b.onesr_f[0:1, :], start=True, stop=True),
                             reads=[rz, b.onesr_f], writes=[ps_sc[0]])
                        s.op("vector", lambda e: e.tensor_copy(rzc[:, :], ps_sc[0][:, 0:1]),
                             reads=[ps_sc[0]], writes=[rzc])
                        s.op("vector", lambda e, qi=qi, vh=vh, qsub=qsub: e.tensor_scalar(
                            attn_tm[:, vh * 1024:(vh + 1) * 1024] if False else
                            atm2[qsub % 2][:, vh * 1024:(vh + 1) * 1024],
                            ps_oh[qi][:, :], rzc[:, :], None, ALU.mult),
                             reads=[ps_oh[qi], rzc], writes=[atm2[qsub % 2]])
                for qi in range(2):
                    qsub = qp * 2 + qi
                    at = atm2[qsub % 2]
                    for vg in range(4):
                        for i in range(4):
                            vc = vg * 4 + i
                            s.op("tensor", lambda e, vc=vc, i=i, at=at: e.transpose(
                                ps_t[:, i * 128:(i + 1) * 128],
                                at[:, vc * 128:(vc + 1) * 128], b.i128[:, :]),
                                 reads=[at, b.i128], writes=[ps_t])
                        s.op("vector", lambda e, vg=vg, qsub=qsub: e.tensor_copy(
                            afm[:, :].rearrange("p (c t) -> p c t", t=512)
                            [:, vg * 4:(vg + 1) * 4, qsub * 128:(qsub + 1) * 128],
                            ps_t[:, :].rearrange("p (c t) -> p c t", t=128)),
                             reads=[ps_t], writes=[afm])
            s.dma("sync", glbuf[:, :].rearrange("p (c t) -> p c t", t=512),
                  _rearr_cpT(gl.ap())[:, :, qsl], reads=[gl], writes=[glbuf])
            s.op("tensor", lambda e, qsl=qsl: e.matmul(
                ps_sc[0][:, :], b.onesr_f[:, :], r1p1_row[0:1, qsl], start=True, stop=True),
                 reads=[b.onesr_f, r1p1_row], writes=[ps_sc[0]])
            s.op("vector", lambda e: e.tensor_copy(rb_bc[:, :], ps_sc[0][:, :]),
                 reads=[ps_sc[0]], writes=[rb_bc])
            s.op("tensor", lambda e, qsl=qsl: e.matmul(
                ps_sc[1][:, :], b.onesr_f[:, :], nu1_row[0:1, qsl], start=True, stop=True),
                 reads=[b.onesr_f, nu1_row], writes=[ps_sc[1]])
            s.op("vector", lambda e: e.tensor_copy(nb_bc[:, :], ps_sc[1][:, :]),
                 reads=[ps_sc[1]], writes=[nb_bc])
            for dm in range(16):
                pbb = pbufB[dm % 2]
                s.dma("sync", pbufF[:, :].rearrange("p (c o) -> p c o", o=128),
                      projT.ap().rearrange("(c p) o -> p c o", p=128)[:, :, dm * 128:(dm + 1) * 128],
                      reads=[projT], writes=[pbufF])
                s.op("vector", lambda e, pbb=pbb: e.tensor_copy(pbb[:, :], pbufF[:, :]),
                     reads=[pbufF], writes=[pbb])
                xb = xbuf[dm % 2]
                s.dma("scalar", xb[:, :], xin.ap()[dm * 128:(dm + 1) * 128, qsl],
                      reads=[xin], writes=[xb])
                pp = ps_sc[dm % 2]
                for ec in range(32):
                    rhs_t = glbuf if ec < 16 else afm
                    rhs_ap = (glbuf[:, (ec % 16) * 512:(ec % 16 + 1) * 512] if ec < 16
                              else afm[:, (ec - 16) * 512:(ec - 15) * 512])
                    s.op("tensor", lambda e, ec=ec, pp=pp, rhs_ap=rhs_ap, pbb=pbb: e.matmul(
                        pp[:, :], pbb[:, ec * 128:(ec + 1) * 128], rhs_ap,
                        start=(ec == 0), stop=(ec == 31)),
                         reads=[pbb, rhs_t], writes=[pp])
                s.op("vector", lambda e, xb=xb: e.tensor_mul(x1t[:, :], xb[:, :],
                                                             rb_bc[:, :]),
                     reads=[xb, rb_bc], writes=[x1t])
                s.op("vector", lambda e, pp=pp: e.tensor_add(x1t[:, :], x1t[:, :],
                                                             pp[:, :]),
                     reads=[x1t, pp], writes=[x1t])
                s.op("vector", lambda e: e.tensor_sub(x1t[:, :], x1t[:, :], nb_bc[:, :]),
                     reads=[x1t, nb_bc], writes=[x1t])
                s.dma("scalar", x1.ap()[dm * 128:(dm + 1) * 128, qsl], x1t[:, :],
                      reads=[x1t], writes=[x1])
    return b.finish() if standalone else None



def build_l0f_pre(b=None, io=None):
    standalone = b is None
    if standalone:
        b = Build("l0f_pre")
    s = b.s
    if io is None:
        io = {"xin": b.din("xin", [D, T], F32),
              "gwT": b.din("gwT", [128, 16 * 128], F32),
              "sgw": b.din("sgw", [128, 16], F32),
              "xln2": b.dout("xln2", [D, T], BF16),
              "comb": b.dout("comb", [8, T], BF16),
              "grow": b.dout("grow", [1, T], F32)}
    xin, gwT, sgw = io["xin"], io["gwT"], io["sgw"]
    xln2, comb, grow = io.get("xln2"), io["comb"], io["grow"]
    if standalone:
        b.load_consts()
    x_ln = io.get("xln_sb") or [b.sb(f"xln{ch}", [128, T], BF16) for ch in range(16)]
    with b.scope():
        emit_layernorm(b, xin, x_ln, gamma_beta=None, rho_mode="single")
    if xln2 is not None:
        for ch in range(16):
            s.dma("scalar", xln2.ap()[ch * 128:(ch + 1) * 128, :], x_ln[ch][:, :],
                  reads=[x_ln[ch]], writes=[xln2])
    with b.scope():
        gwf = b.sb("gwf", [128, 16 * 128], F32)
        gwb = b.sb("gwb", [128, 16 * 128], BF16)
        sgf = b.sb("sgf", [128, 16], F32)
        sgb = b.sb("sgb", [128, 16], BF16)
        lg_all = b.sb("lgall", [128, 128], F32)
        e_all = b.sb("eall", [128, 128], F32)
        eq1 = b.sb("eq1", [128, 128], F32)
        neq = b.sb("neq", [128, 128], F32)
        me_all = b.sb("meall", [128, 128], F32)
        m1a = b.sb("m1a", [128, 16], F32)
        e2a = b.sb("e2a", [128, 16], F32)
        dena = b.sb("dena", [128, 16], F32)
        rden = b.sb("rden", [128, 16], F32)
        cmb_all = b.sb("cmba", [128, 128], F32)
        cmb_b = b.sb("cmbb", [128, 128], BF16)
        cfm = b.sb("cfm", [8, 16 * 128], BF16)
        grow_row = b.sb("growr", [1, T], F32)
        ps_g = b.ps("psg", [128, 128])
        ps_t8 = b.ps("pst8", [8, 128], BF16)
        ps_sg = b.ps("pssg", [1, 512])
        s.dma("sync", gwf[:, :], gwT.ap(), reads=[gwT], writes=[gwf])
        s.op("vector", lambda e: e.tensor_copy(gwb[:, :], gwf[:, :]),
             reads=[gwf], writes=[gwb])
        s.dma("sync", sgf[:, :], sgw.ap(), reads=[sgw], writes=[sgf])
        s.op("vector", lambda e: e.tensor_copy(sgb[:, :], sgf[:, :]),
             reads=[sgf], writes=[sgb])
        v3 = lambda t, o=8: t[:, :].rearrange("p (g o) -> p g o", o=o)
        for tt in range(16):
            t0 = tt * 128
            for k in range(16):
                s.op("tensor", lambda e, k=k, t0=t0: e.matmul(
                    ps_g[:, :], x_ln[k][:, t0:t0 + 128], gwb[:, k * 128:(k + 1) * 128],
                    start=(k == 0), stop=(k == 15)),
                     reads=[x_ln[k], gwb], writes=[ps_g])
            s.op("vector", lambda e, tt=tt: e.tensor_copy(lg_all[:, tt * 8:(tt + 1) * 8],
                                                          ps_g[:, 0:8]),
                 reads=[ps_g], writes=[lg_all])
        s.op("scalar", lambda e: e.activation(e_all[:, :], lg_all[:, :], AF.Exp),
             reads=[lg_all], writes=[e_all], short=1200)
        s.op("vector", lambda e: e.reduce_max(v3(m1a, 1), v3(e_all), AX.X),
             reads=[e_all], writes=[m1a])
        s.op("vector", lambda e: e.tensor_tensor(v3(eq1), v3(e_all),
                                                 v3(m1a, 1).broadcast_to([128, 16, 8]),
                                                 ALU.is_equal),
             reads=[e_all, m1a], writes=[eq1], short=700)
        s.op("vector", lambda e: e.tensor_scalar(neq[:, :], eq1[:, :], -1.0, 1.0,
                                                 ALU.mult, ALU.add),
             reads=[eq1], writes=[neq])
        s.op("vector", lambda e: e.tensor_mul(me_all[:, :], e_all[:, :], neq[:, :]),
             reads=[e_all, neq], writes=[me_all], short=700)
        s.op("vector", lambda e: e.reduce_max(v3(e2a, 1), v3(me_all), AX.X),
             reads=[me_all], writes=[e2a])
        s.op("vector", lambda e: e.tensor_add(dena[:, :], m1a[:, :], e2a[:, :]),
             reads=[m1a, e2a], writes=[dena])
        s.op("vector", lambda e: e.reciprocal(rden[:, :], dena[:, :]),
             reads=[dena], writes=[rden])
        s.op("vector", lambda e: e.tensor_tensor(v3(neq), v3(me_all),
                                                 v3(e2a, 1).broadcast_to([128, 16, 8]),
                                                 ALU.is_equal),
             reads=[me_all, e2a], writes=[neq], short=700)
        s.op("vector", lambda e: e.tensor_add(eq1[:, :], eq1[:, :], neq[:, :]),
             reads=[eq1, neq], writes=[eq1], short=700)
        s.op("vector", lambda e: e.tensor_mul(cmb_all[:, :], e_all[:, :], eq1[:, :]),
             reads=[e_all, eq1], writes=[cmb_all], short=700)
        s.op("vector", lambda e: e.tensor_tensor(v3(cmb_all), v3(cmb_all),
                                                 v3(rden, 1).broadcast_to([128, 16, 8]),
                                                 ALU.mult),
             reads=[cmb_all, rden], writes=[cmb_all], short=700)
        s.op("vector", lambda e: e.tensor_copy(cmb_b[:, :], cmb_all[:, :]),
             reads=[cmb_all], writes=[cmb_b])
        for tt in range(16):
            t0 = tt * 128
            s.op("tensor", lambda e, tt=tt: e.transpose(
                ps_t8[:, :], cmb_b[:, tt * 8:(tt + 1) * 8], b.i128[:, :]),
                 reads=[cmb_b, b.i128], writes=[ps_t8])
            s.op("vector", lambda e, t0=t0: e.tensor_copy(cfm[:, t0:t0 + 128],
                                                          ps_t8[:, :]),
                 reads=[ps_t8], writes=[cfm])
        s.dma("scalar", comb.ap(), cfm[:, :], reads=[cfm], writes=[comb])
        for stt in range(4):
            sl = slice(stt * 512, stt * 512 + 512)
            for k in range(16):
                s.op("tensor", lambda e, k=k, sl=sl: e.matmul(
                    ps_sg[:, :], sgb[:, k:k + 1], x_ln[k][:, sl],
                    start=(k == 0), stop=(k == 15)),
                     reads=[sgb, x_ln[k]], writes=[ps_sg])
            s.op("scalar", lambda e, sl=sl: e.activation(grow_row[0:1, sl], ps_sg[:, :],
                                                         AF.Sigmoid),
                 reads=[ps_sg], writes=[grow_row])
        s.dma("scalar", grow.ap(), grow_row[0:1, :], reads=[grow_row], writes=[grow])
    return b.finish() if standalone else None



I_MOE = 1408
NIC = I_MOE // 128

def build_l0f_hh(b=None, io=None):
    standalone = b is None
    if standalone:
        b = Build("l0f_hh")
    s = b.s
    if io is None:
        io = {"xln2": b.din("xln2", [D, T], BF16),
              "wg": b.din("wg", [8 * D, I_MOE], F32),
              "wu": b.din("wu", [8 * D, I_MOE], F32),
              "comb": b.din("comb", [8, T], BF16),
              "hh": b.dout("hh", [8 * I_MOE, T], BF16)}
    xln2, wg, wu, comb, hh = io.get("xln2"), io["wg"], io["wu"], io["comb"], io["hh"]
    if standalone:
        b.load_consts()
    x_ln = io.get("xln_sb")
    if x_ln is None:
        x_ln = [b.sb(f"xln{ch}", [128, T], BF16) for ch in range(16)]
        for ch in range(16):
            s.dma("sync", x_ln[ch][:, :], xln2.ap()[ch * 128:(ch + 1) * 128, :],
                  reads=[xln2], writes=[x_ln[ch]])
    with b.scope():
        crow = b.sb("crow", [1, T], BF16)
        cbc = b.sb("cbc", [128, T], F32)
        wgF = [b.sb(f"wgF{i}", [128, 16 * 128], F32) for i in range(2)]
        wgB = [b.sb(f"wgB{i}", [128, 16 * 128], BF16) for i in range(2)]
        wuF = [b.sb(f"wuF{i}", [128, 16 * 128], F32) for i in range(2)]
        wuB = [b.sb(f"wuB{i}", [128, 16 * 128], BF16) for i in range(2)]
        sg = b.sb("sg", [128, 512], F32)
        ue = b.sb("ue", [128, 512], F32)
        hhT = b.sb("hhT", [128, 4 * 512], BF16)
        psG = [b.ps(f"psG{i}", [128, 512]) for i in range(2)]
        psU = [b.ps(f"psU{i}", [128, 512]) for i in range(2)]
        psC = b.ps("psC", [128, 512])
        for ei in range(8):
            s.dma("sync", crow[0:1, :], comb.ap()[ei:ei + 1, :], reads=[comb],
                  writes=[crow])
            for stt in range(4):
                sl = slice(stt * 512, stt * 512 + 512)
                s.op("tensor", lambda e, sl=sl: e.matmul(
                    psC[:, :], b.onesr_b[:, :], crow[0:1, sl], start=True, stop=True),
                     reads=[b.onesr_b, crow], writes=[psC])
                s.op("vector", lambda e, sl=sl: e.tensor_copy(cbc[:, sl], psC[:, :]),
                     reads=[psC], writes=[cbc])
            for ic in range(NIC):
                wgf, wgb = wgF[ic % 2], wgB[ic % 2]
                wuf, wub = wuF[ic % 2], wuB[ic % 2]
                col = slice(ic * 128, ic * 128 + 128)
                s.dma("sync", wgf[:, :].rearrange("p (k o) -> p k o", o=128),
                      wg.ap()[ei * D:(ei + 1) * D, col].rearrange("(k p) o -> p k o", p=128),
                      reads=[wg], writes=[wgf])
                s.op("vector", lambda e, wgf=wgf, wgb=wgb: e.tensor_copy(wgb[:, :], wgf[:, :]),
                     reads=[wgf], writes=[wgb])
                s.dma("sync", wuf[:, :].rearrange("p (k o) -> p k o", o=128),
                      wu.ap()[ei * D:(ei + 1) * D, col].rearrange("(k p) o -> p k o", p=128),
                      reads=[wu], writes=[wuf])
                s.op("vector", lambda e, wuf=wuf, wub=wub: e.tensor_copy(wub[:, :], wuf[:, :]),
                     reads=[wuf], writes=[wub])
                for stt in range(4):
                    sl = slice(stt * 512, stt * 512 + 512)
                    pg, pu = psG[stt % 2], psU[stt % 2]
                    for k in range(16):
                        s.op("tensor", lambda e, k=k, sl=sl, pg=pg, wgb=wgb: e.matmul(
                            pg[:, :], wgb[:, k * 128:(k + 1) * 128], x_ln[k][:, sl],
                            start=(k == 0), stop=(k == 15)),
                             reads=[wgb, x_ln[k]], writes=[pg])
                    for k in range(16):
                        s.op("tensor", lambda e, k=k, sl=sl, pu=pu, wub=wub: e.matmul(
                            pu[:, :], wub[:, k * 128:(k + 1) * 128], x_ln[k][:, sl],
                            start=(k == 0), stop=(k == 15)),
                             reads=[wub, x_ln[k]], writes=[pu])
                    s.op("scalar", lambda e, pg=pg: e.activation(sg[:, :], pg[:, :],
                                                                 AF.Silu),
                         reads=[pg], writes=[sg], short=500)
                    s.op("vector", lambda e, pu=pu, sl=sl: e.tensor_mul(
                        ue[:, :], pu[:, :], cbc[:, sl]),
                         reads=[pu, cbc], writes=[ue])
                    s.op("vector", lambda e, sl=sl, hhT=hhT: e.tensor_mul(
                        hhT[:, sl], sg[:, :], ue[:, :]),
                         reads=[sg, ue], writes=[hhT])
                r0 = ei * I_MOE + ic * 128
                s.dma("scalar", hh.ap()[r0:r0 + 128, :], hhT[:, :],
                      reads=[hhT], writes=[hh])
    return b.finish() if standalone else None



IS = 5632
NICS = IS // 128

def build_l0f_sh(b=None, io=None):
    standalone = b is None
    if standalone:
        b = Build("l0f_sh")
    s = b.s
    if io is None:
        io = {"xln2": b.din("xln2", [D, T], BF16), "swg": b.din("swg", [D, IS], F32),
              "swu": b.din("swu", [D, IS], F32), "grow": b.din("grow", [1, T], F32),
              "wdf": b.din("wdf", [8 * I_MOE, D], F32),
              "swdf": b.din("swdf", [IS, D], F32),
              "sh": b.dout("sh", [IS, T], BF16),
              "wdb": b.dout("wdb", [8 * I_MOE, D], BF16),
              "swdb": b.dout("swdb", [IS, D], BF16)}
    xln2, swg, swu, grow = io.get("xln2"), io["swg"], io["swu"], io["grow"]
    wdf, swdf, sh, wdb, swdb = io["wdf"], io["swdf"], io["sh"], io["wdb"], io["swdb"]
    if standalone:
        b.load_consts()
    x_ln = io.get("xln_sb")
    if x_ln is None:
        x_ln = [b.sb(f"xln{ch}", [128, T], BF16) for ch in range(16)]
        for ch in range(16):
            s.dma("sync", x_ln[ch][:, :], xln2.ap()[ch * 128:(ch + 1) * 128, :],
                  reads=[xln2], writes=[x_ln[ch]])
    with b.scope():
        grow_row = b.sb("growr", [1, T], F32)
        gbc = b.sb("gbc", [128, T], F32)
        wgF = [b.sb(f"wgF{i}", [128, 16 * 128], F32) for i in range(2)]
        wgB = [b.sb(f"wgB{i}", [128, 16 * 128], BF16) for i in range(2)]
        wuF = [b.sb(f"wuF{i}", [128, 16 * 128], F32) for i in range(2)]
        wuB = [b.sb(f"wuB{i}", [128, 16 * 128], BF16) for i in range(2)]
        cstF = [b.sb(f"cstF{i}", [128, 2048], F32) for i in range(2)]
        cstB = [b.sb(f"cstB{i}", [128, 2048], BF16) for i in range(2)]
        sg = b.sb("sg", [128, 512], F32)
        ue = b.sb("ue", [128, 512], F32)
        shT = b.sb("shT", [128, 4 * 512], BF16)
        psG = [b.ps(f"psG{i}", [128, 512]) for i in range(2)]
        psU = [b.ps(f"psU{i}", [128, 512]) for i in range(2)]
        psC = b.ps("psC", [128, 512])
        s.dma("sync", grow_row[0:1, :], grow.ap(), reads=[grow], writes=[grow_row])
        for stt in range(4):
            sl = slice(stt * 512, stt * 512 + 512)
            s.op("tensor", lambda e, sl=sl: e.matmul(
                psC[:, :], b.onesr_f[:, :], grow_row[0:1, sl], start=True, stop=True),
                 reads=[b.onesr_f, grow_row], writes=[psC])
            s.op("vector", lambda e, sl=sl: e.tensor_copy(gbc[:, sl], psC[:, :]),
                 reads=[psC], writes=[gbc])
        cast_jobs = [(wdf, wdb, r) for r in range(8 * I_MOE // 128)]
        cast_jobs += [(swdf, swdb, r) for r in range(IS // 128)]
        cast_i = [0]

        def emit_casts(n):
            for _ in range(n):
                if cast_i[0] >= len(cast_jobs):
                    return
                src, dst, r = cast_jobs[cast_i[0]]
                cast_i[0] += 1
                cf, cb = cstF[r % 2], cstB[r % 2]
                s.dma("sync", cf[:, :], src.ap()[r * 128:(r + 1) * 128, :],
                      reads=[src], writes=[cf])
                s.op("vector", lambda e, cf=cf, cb=cb: e.tensor_copy(cb[:, :], cf[:, :]),
                     reads=[cf], writes=[cb])
                s.dma("scalar", dst.ap()[r * 128:(r + 1) * 128, :], cb[:, :],
                      reads=[cb], writes=[dst])

        for ic in range(NICS):
            emit_casts(3)
            wgf, wgb = wgF[ic % 2], wgB[ic % 2]
            wuf, wub = wuF[ic % 2], wuB[ic % 2]
            col = slice(ic * 128, ic * 128 + 128)
            s.dma("sync", wgf[:, :].rearrange("p (k o) -> p k o", o=128),
                  swg.ap()[:, col].rearrange("(k p) o -> p k o", p=128),
                  reads=[swg], writes=[wgf])
            s.op("vector", lambda e, wgf=wgf, wgb=wgb: e.tensor_copy(wgb[:, :], wgf[:, :]),
                 reads=[wgf], writes=[wgb])
            s.dma("sync", wuf[:, :].rearrange("p (k o) -> p k o", o=128),
                  swu.ap()[:, col].rearrange("(k p) o -> p k o", p=128),
                  reads=[swu], writes=[wuf])
            s.op("vector", lambda e, wuf=wuf, wub=wub: e.tensor_copy(wub[:, :], wuf[:, :]),
                 reads=[wuf], writes=[wub])
            for stt in range(4):
                sl = slice(stt * 512, stt * 512 + 512)
                pg, pu = psG[stt % 2], psU[stt % 2]
                for k in range(16):
                    s.op("tensor", lambda e, k=k, sl=sl, pg=pg, wgb=wgb: e.matmul(
                        pg[:, :], wgb[:, k * 128:(k + 1) * 128], x_ln[k][:, sl],
                        start=(k == 0), stop=(k == 15)),
                         reads=[wgb, x_ln[k]], writes=[pg])
                for k in range(16):
                    s.op("tensor", lambda e, k=k, sl=sl, pu=pu, wub=wub: e.matmul(
                        pu[:, :], wub[:, k * 128:(k + 1) * 128], x_ln[k][:, sl],
                        start=(k == 0), stop=(k == 15)),
                         reads=[wub, x_ln[k]], writes=[pu])
                s.op("scalar", lambda e, pg=pg: e.activation(sg[:, :], pg[:, :], AF.Silu),
                     reads=[pg], writes=[sg], short=500)
                s.op("vector", lambda e, pu=pu, sl=sl: e.tensor_mul(
                    ue[:, :], pu[:, :], gbc[:, sl]),
                     reads=[pu, gbc], writes=[ue])
                s.op("vector", lambda e, sl=sl, shT=shT: e.tensor_mul(
                    shT[:, sl], sg[:, :], ue[:, :]),
                     reads=[sg, ue], writes=[shT])
            r0 = ic * 128
            s.dma("scalar", sh.ap()[r0:r0 + 128, :], shT[:, :],
                  reads=[shT], writes=[sh])
    return b.finish() if standalone else None



def build_l0f_down(b=None, io=None):
    standalone = b is None
    if standalone:
        b = Build("l0f_down")
    s = b.s
    if io is None:
        io = {"hh": b.din("hh", [8 * I_MOE, T], BF16), "sh": b.din("sh", [IS, T], BF16),
              "wdb": b.din("wdb", [8 * I_MOE, D], BF16),
              "swdb": b.din("swdb", [IS, D], BF16),
              "x1": b.din("x1", [D, T], F32), "x2": b.dout("x2", [D, T], F32)}
    hh, sh, wdb, swdb, x1, x2 = (io["hh"], io["sh"], io["wdb"], io["swdb"],
                                 io["x1"], io["x2"])
    if standalone:
        b.load_consts()
    with b.scope():
        big = b.sb("big", [128, 88 * 512], BF16)
        tsb = b.sb("tsb", [128, 16 * 512], F32)
        wcols = [b.sb(f"wcol{i}", [128, 88 * 128], BF16) for i in range(2)]
        swcols = [b.sb(f"swcol{i}", [128, 44 * 128], BF16) for i in range(2)]
        x1t = [b.sb(f"x1t{i}", [128, 512], F32) for i in range(2)]
        ut = b.sb("ut", [128, 512], F32)
        psD = [b.ps(f"psD{i}", [128, 512]) for i in range(2)]
        hhv = hh.ap().rearrange("(c p) t -> p c t", p=128)
        shv = sh.ap().rearrange("(c p) t -> p c t", p=128)
        for stt in range(4):
            sl = slice(stt * 512, stt * 512 + 512)
            s.dma("sync", big[:, 0:44 * 512].rearrange("p (c t) -> p c t", t=512),
                  hhv[:, 0:44, sl], reads=[hh], writes=[big])
            s.dma("gpsimd", big[:, 44 * 512:88 * 512].rearrange("p (c t) -> p c t", t=512),
                  hhv[:, 44:88, sl], reads=[hh], writes=[big])
            for dm in range(16):
                dcol = slice(dm * 128, dm * 128 + 128)
                wcol = wcols[dm % 2]
                s.dma("sync", wcol[:, :].rearrange("p (c o) -> p c o", o=128),
                      wdb.ap().rearrange("(c p) o -> p c o", p=128)[:, :, dcol],
                      reads=[wdb], writes=[wcol])
                pd = psD[dm % 2]
                for ic in range(88):
                    s.op("tensor", lambda e, ic=ic, pd=pd, wcol=wcol: e.matmul(
                        pd[:, :], wcol[:, ic * 128:(ic + 1) * 128],
                        big[:, ic * 512:(ic + 1) * 512],
                        start=(ic == 0), stop=(ic == 87)),
                         reads=[wcol, big], writes=[pd])
                s.op("vector", lambda e, pd=pd, dm=dm: e.tensor_copy(
                    tsb[:, dm * 512:(dm + 1) * 512], pd[:, :]),
                     reads=[pd], writes=[tsb])
            s.dma("sync", big[:, 0:22 * 512].rearrange("p (c t) -> p c t", t=512),
                  shv[:, 0:22, sl], reads=[sh], writes=[big])
            s.dma("gpsimd", big[:, 22 * 512:44 * 512].rearrange("p (c t) -> p c t", t=512),
                  shv[:, 22:44, sl], reads=[sh], writes=[big])
            for dm in range(16):
                dcol = slice(dm * 128, dm * 128 + 128)
                swcol = swcols[dm % 2]
                s.dma("sync", swcol[:, :].rearrange("p (c o) -> p c o", o=128),
                      swdb.ap().rearrange("(c p) o -> p c o", p=128)[:, :, dcol],
                      reads=[swdb], writes=[swcol])
                pd = psD[dm % 2]
                for ic in range(44):
                    s.op("tensor", lambda e, ic=ic, pd=pd, swcol=swcol: e.matmul(
                        pd[:, :], swcol[:, ic * 128:(ic + 1) * 128],
                        big[:, ic * 512:(ic + 1) * 512],
                        start=(ic == 0), stop=(ic == 43)),
                         reads=[swcol, big], writes=[pd])
                xb = x1t[dm % 2]
                s.dma("scalar", xb[:, :], x1.ap()[dm * 128:(dm + 1) * 128, sl],
                      reads=[x1], writes=[xb])
                s.op("vector", lambda e, pd=pd, dm=dm: e.tensor_add(
                    ut[:, :], pd[:, :], tsb[:, dm * 512:(dm + 1) * 512]),
                     reads=[pd, tsb], writes=[ut])
                s.op("vector", lambda e, xb=xb: e.tensor_add(ut[:, :], ut[:, :], xb[:, :]),
                     reads=[ut, xb], writes=[ut])
                s.dma("scalar", x2.ap()[dm * 128:(dm + 1) * 128, sl], ut[:, :],
                      reads=[ut], writes=[x2])
    return b.finish() if standalone else None


def host_inputs_l0f(inputs, x1_arrs, resC=None, resD=None, resD2=None, stage="pre"):
    consts = mkconsts()
    maps = []
    if stage == "pre":
        gwT_small = inputs["L0_ffn_gate_w"].T.reshape(16, 128, 8).transpose(1, 0, 2)
        gwT = np.zeros((128, 16, 128), np.float32)
        gwT[:, :, 0:8] = gwT_small
        gwT = np.ascontiguousarray(gwT.reshape(128, 16 * 128))
        sgw = np.ascontiguousarray(inputs["L0_ffn_shared_gate_w"].reshape(16, 128).T)
        for c in range(NCORE):
            m = {"xin": x1_arrs[c], "gwT": gwT, "sgw": sgw}
            m.update(consts)
            maps.append(m)
    elif stage == "hh":
        wg = np.ascontiguousarray(inputs["L0_ffn_w_gate"].reshape(8 * D, I_MOE))
        wu = np.ascontiguousarray(inputs["L0_ffn_w_up"].reshape(8 * D, I_MOE))
        for c in range(NCORE):
            m = {"xln2": resC[c]["xln2"], "wg": wg, "wu": wu, "comb": resC[c]["comb"]}
            m.update(consts)
            maps.append(m)
    elif stage == "sh":
        swg = np.ascontiguousarray(inputs["L0_ffn_sw_gate"])
        swu = np.ascontiguousarray(inputs["L0_ffn_sw_up"])
        wdf = np.ascontiguousarray(inputs["L0_ffn_w_down"].reshape(8 * I_MOE, D))
        swdf = np.ascontiguousarray(inputs["L0_ffn_sw_down"])
        for c in range(NCORE):
            m = {"xln2": resC[c]["xln2"], "swg": swg, "swu": swu,
                 "grow": resC[c]["grow"], "wdf": wdf, "swdf": swdf}
            m.update(consts)
            maps.append(m)
    elif stage == "down":
        for c in range(NCORE):
            m = {"hh": resD[c]["hh"], "sh": resD2[c]["sh"], "wdb": resD2[c]["wdb"],
                 "swdb": resD2[c]["swdb"], "x1": x1_arrs[c]}
            m.update(consts)
            maps.append(m)
    return maps



def build_l1a_pre(b=None, io=None):
    standalone = b is None
    if standalone:
        b = Build("l1a_pre")
    s = b.s
    if io is None:
        io = {"xin": b.din("xin", [D, T], F32), "wq": b.din("wq", [D, D], F32),
              "wk": b.din("wk", [D, 512], F32), "wv": b.din("wv", [D, 512], F32),
              "ctab": b.din("ctab", [128, 16 * 512], BF16),
              "stab": b.din("stab", [128, 16 * 512], BF16),
              "q_tm": b.dout("q_tm", [T, D], BF16),
              "kv1": b.dout("kv1", [T, 1024], BF16)}
    xin, wq, wk, wv = io["xin"], io["wq"], io["wk"], io["wv"]
    ctab, stab, q_tm, kv1 = io["ctab"], io["stab"], io["q_tm"], io["kv1"]
    if standalone:
        b.load_consts()
    x_ln = [b.sb(f"xln{ch}", [128, T], BF16) for ch in range(16)]
    with b.scope():
        emit_layernorm(b, xin, x_ln, gamma_beta=None, rho_mode="single")
    ct_sb = b.sb("ctsb", [128, 16 * 512], BF16)
    st_sb = b.sb("stsb", [128, 16 * 512], BF16)
    s.dma("sync", ct_sb[:, :], ctab.ap(), reads=[ctab], writes=[ct_sb])
    s.dma("sync", st_sb[:, :], stab.ap(), reads=[stab], writes=[st_sb])
    with b.scope():
        wf = b.sb("wf", [128, 16 * 512], F32)
        wb = [b.sb(f"wb{i}", [128, 16 * 512], BF16) for i in range(2)]
        ps = [b.ps(f"ps{i}", [128, 512]) for i in range(2)]
        uu = b.sb("uu", [128, 512], F32)
        vv = b.sb("vv", [128, 512], F32)
        ot = b.sb("ot", [128, 512], BF16)
        r4 = lambda ap: ap.rearrange("p (h a j) -> p h a j", a=2, j=32)
        parts = [("k", wk, 0, 1.0, True, kv1, 0), ("v", wv, 0, 1.0, False, kv1, 512)]
        parts += [("q", wq, oc, 0.125, True, q_tm, oc * 512) for oc in range(4)]
        for pi, (nm, src, oc, scale, rope, dst, dcol) in enumerate(parts):
            if pi == 2 and io.get("after_kv"):
                io["after_kv"]()
            wbb = wb[pi % 2]
            s.dma("sync", wf[:, :].rearrange("p (k o) -> p k o", o=512),
                  src.ap().rearrange("(k p) o -> p k o", p=128)[:, :, oc * 512:(oc + 1) * 512],
                  reads=[src], writes=[wf])
            s.op("vector", lambda e, wbb=wbb: e.tensor_copy(wbb[:, :], wf[:, :]),
                 reads=[wf], writes=[wbb])
            for tt in range(16):
                t0 = tt * 128
                pp = ps[tt % 2]
                for k in range(16):
                    s.op("tensor", lambda e, k=k, t0=t0, pp=pp, wbb=wbb: e.matmul(
                        pp[:, :], x_ln[k][:, t0:t0 + 128], wbb[:, k * 512:(k + 1) * 512],
                        start=(k == 0), stop=(k == 15)),
                         reads=[x_ln[k], wbb], writes=[pp])
                tsl = slice(tt * 512, tt * 512 + 512)
                if rope:
                    s.op("vector", lambda e, pp=pp, tsl=tsl, scale=scale:
                         e.scalar_tensor_tensor(uu[:, :], pp[:, :], scale,
                                                ct_sb[:, tsl], ALU.mult, ALU.mult),
                         reads=[pp, ct_sb], writes=[uu])
                    s.op("vector", lambda e, pp=pp, tsl=tsl, scale=scale:
                         e.scalar_tensor_tensor(
                             r4(vv[:, :])[:, :, 0:1, :], r4(pp[:, :])[:, :, 1:2, :],
                             scale, r4(st_sb[:, tsl])[:, :, 0:1, :], ALU.mult, ALU.mult),
                         reads=[pp, st_sb], writes=[vv], short=700)
                    s.op("vector", lambda e, pp=pp, tsl=tsl, scale=scale:
                         e.scalar_tensor_tensor(
                             r4(vv[:, :])[:, :, 1:2, :], r4(pp[:, :])[:, :, 0:1, :],
                             scale, r4(st_sb[:, tsl])[:, :, 1:2, :], ALU.mult, ALU.mult),
                         reads=[pp, st_sb], writes=[vv], short=700)
                    s.op("vector", lambda e: e.tensor_add(ot[:, :], uu[:, :], vv[:, :]),
                         reads=[uu, vv], writes=[ot])
                else:
                    s.op("vector", lambda e, pp=pp: e.tensor_copy(ot[:, :], pp[:, :]),
                         reads=[pp], writes=[ot])
                if nm == "q":
                    dv = dst.ap()[t0:t0 + 128, dcol:dcol + 512].rearrange(
                        "t (c a j) -> t c a j", c=4, a=2, j=64)
                    for a in range(2):
                        s.dma("scalar", dv[:, :, a:a + 1, :],
                              ot[:, a * 256:(a + 1) * 256].rearrange(
                                  "t (c x j) -> t c x j", c=4, x=1, j=64),
                              reads=[ot], writes=[dst])
                else:
                    s.dma("scalar", dst.ap()[t0:t0 + 128, dcol:dcol + 512], ot[:, :],
                          reads=[ot], writes=[dst])
    return b.finish() if standalone else None



def build_l1a_attn(b=None, io=None):
    standalone = b is None
    if standalone:
        b = Build("l1a_attn")
    s = b.s
    if io is None:
        io = {"q_tm": b.din("q_tm", [T, D], BF16), "kvf": b.din("kvf", [S, 1024], BF16),
              "wo": b.din("wo", [D, D], F32), "xin": b.din("xin", [D, T], F32),
              "bandM": b.din("bandM", [128, 2048], F32),
              "x3": b.dout("x3", [D, T], F32)}
    q_tm, kvf, wo, xin, bandM, x3 = (io["q_tm"], io["kvf"], io["wo"], io["xin"],
                                     io["bandM"], io["x3"])
    kvrow = io.get("kvrow", lambda kt: kt * 128)
    if standalone:
        b.load_consts()
    band = b.sb("band", [128, 2048], F32)
    s.dma("sync", band[:, :], bandM.ap(), reads=[bandM], writes=[band])
    k_fm = [b.sb(f"kfm{c}", [128, S], BF16) for c in range(4)]
    vs_sb = b.sb("vs", [128, 32 * 520], BF16)
    with b.scope():
        kseg = [b.sb(f"kseg{i}", [128, 512], BF16) for i in range(2)]
        pst = b.ps("pst", [128, 128], BF16)
        for kt in range(32):
            sg = kseg[kt % 2]
            s.dma("sync", sg[:, :], kvf.ap()[kvrow(kt):kvrow(kt) + 128, 0:512],
                  reads=[kvf], writes=[sg])
            for c in range(4):
                s.op("tensor", lambda e, sg=sg, c=c: e.transpose(
                    pst[:, :], sg[:, c * 128:(c + 1) * 128], b.i128[:, :]),
                     reads=[sg, b.i128], writes=[pst])
                s.op("vector", lambda e, kt=kt, c=c: e.tensor_copy(
                    k_fm[c][:, kt * 128:(kt + 1) * 128], pst[:, :]),
                     reads=[pst], writes=[k_fm[c]])
            s.dma("scalar",
                  vs_sb[:, kt * 520:(kt + 1) * 520].rearrange("p (h j) -> p h j", j=65)[:, :, 0:64],
                  kvf.ap()[kvrow(kt):kvrow(kt) + 128, 512:1024].rearrange("p (h j) -> p h j", j=64),
                  reads=[kvf], writes=[vs_sb])
            s.op("vector", lambda e, kt=kt: e.memset(
                vs_sb[:, kt * 520:(kt + 1) * 520].rearrange("p (h j) -> p h j", j=65)[:, :, 64:65],
                1.0),
                 reads=[], writes=[vs_sb])
    with b.scope():
        q_fm = [b.sb(f"qfm{c}", [128, 512], BF16) for c in range(16)]
        qtl = [b.sb(f"qtl{i}", [128, 2048], BF16) for i in range(2)]
        probsT = [[b.sb(f"pT{i}_{j}", [128, 512], BF16) for j in range(2)]
                  for i in range(4)]
        sc_f = [b.sb(f"scf{i}", [128, 512], F32) for i in range(2)]
        zrow = b.sb("zrow", [1, 512], F32)
        rz = b.sb("rz", [1, 512], F32)
        zb_sb = b.sb("zbsb", [64, 512], F32)
        y_sb = [b.sb(f"ysb{c}", [128, 512], BF16) for c in range(16)]
        xb = [b.sb(f"xb{i}", [128, 512], F32) for i in range(2)]
        x3t = b.sb("x3t", [128, 512], F32)
        pst = b.ps("pstq", [128, 128], BF16)
        ps_sc = [b.ps(f"pssc{i}", [128, 512]) for i in range(2)]
        ps_pv = [b.ps(f"pspv{i}", [65, 512]) for i in range(4)]
        ps_bc = b.ps("psbc", [64, 512])
        wocf = b.sb("wocf", [128, 16 * 128], F32)
        wocb = [b.sb(f"wocb{i}", [128, 16 * 128], BF16) for i in range(2)]
        for qg in range(4):
            nkt = nkt_of(qg)
            for tts in range(4):
                qt = qtl[tts % 2]
                s.dma("sync", qt[:, :],
                      q_tm.ap()[qg * 512 + tts * 128:qg * 512 + (tts + 1) * 128, :],
                      reads=[q_tm], writes=[qt])
                for ch in range(16):
                    s.op("tensor", lambda e, qt=qt, ch=ch: e.transpose(
                        pst[:, :], qt[:, ch * 128:(ch + 1) * 128], b.i128[:, :]),
                         reads=[qt, b.i128], writes=[pst])
                    s.op("vector", lambda e, ch=ch, tts=tts: e.tensor_copy(
                        q_fm[ch][:, tts * 128:(tts + 1) * 128], pst[:, :]),
                         reads=[pst], writes=[q_fm[ch]])
            for kvh in range(8):
                kch, krow = kvh // 2, (kvh % 2) * 64

                def emit_pv(kt, nkt=None, kvh=kvh):
                    for h4 in range(4):
                        pT = probsT[h4][kt % 2]
                        s.op("tensor", lambda e, h4=h4, kt=kt, kvh=kvh, pT=pT,
                             nkt=nkt: e.matmul(
                                 ps_pv[h4][:, :],
                                 vs_sb[:, kt * 520 + kvh * 65:kt * 520 + (kvh + 1) * 65],
                                 pT[:, :],
                                 start=(kt == 0), stop=(kt == nkt - 1)),
                             reads=[vs_sb, pT], writes=[ps_pv[h4]])

                for kt in range(nkt):
                    for h4 in range(4):
                        h = kvh * 4 + h4
                        slot = (h // 8) * 8 + (h % 4) * 2 + (h // 4) % 2
                        qch, qrow = slot // 2, (slot % 2) * 64
                        pa = ps_sc[(kt * 4 + h4) % 2]
                        sf = sc_f[(kt * 4 + h4) % 2]
                        s.op("tensor", lambda e, pa=pa, kt=kt, kch=kch, krow=krow,
                             qch=qch, qrow=qrow: e.matmul(
                                 pa[:, :],
                                 k_fm[kch][krow:krow + 64, kt * 128:(kt + 1) * 128],
                                 q_fm[qch][qrow:qrow + 64, :],
                                 start=True, stop=True),
                             reads=[k_fm[kch], q_fm[qch]], writes=[pa])
                        if kt >= 8 * qg:
                            s.op("vector", lambda e, pa=pa, sf=sf, qg=qg, kt=kt:
                                 e.scalar_tensor_tensor(ps_view(sf), ps_view(pa), 1.0,
                                                        band_view(band, qg, kt),
                                                        ALU.mult, ALU.add),
                                 reads=[pa, band], writes=[sf])
                            esrc = sf
                        else:
                            esrc = pa
                        s.op("scalar", lambda e, esrc=esrc, h4=h4, kt=kt: e.activation(
                            probsT[h4][kt % 2][:, :], esrc[:, :], AF.Exp),
                             reads=[esrc], writes=[probsT[h4][kt % 2]])
                    if kt > 0:
                        emit_pv(kt - 1, nkt=nkt)
                emit_pv(nkt - 1, nkt=nkt)
                for h4 in range(4):
                    h = kvh * 4 + h4
                    ych, yrow = h // 2, (h % 2) * 64
                    s.op("vector", lambda e, h4=h4: e.reciprocal(
                        rz[0:1, :], ps_pv[h4][64:65, :]),
                         reads=[ps_pv[h4]], writes=[rz])
                    s.op("tensor", lambda e: e.matmul(
                        ps_bc[:, :], b.onesr_f[0:1, 0:64], rz[0:1, :],
                        start=True, stop=True),
                         reads=[b.onesr_f, rz], writes=[ps_bc])
                    s.op("vector", lambda e: e.tensor_copy(zb_sb[:, :], ps_bc[:, :]),
                         reads=[ps_bc], writes=[zb_sb])
                    s.op("vector", lambda e, h4=h4, ych=ych, yrow=yrow: e.tensor_mul(
                        y_sb[ych][yrow:yrow + 64, :], ps_pv[h4][0:64, :], zb_sb[:, :]),
                         reads=[ps_pv[h4], zb_sb], writes=[y_sb[ych]])
            qsl = slice(qg * 512, qg * 512 + 512)
            for dm in range(16):
                pp = ps_sc[dm % 2]
                xbb = xb[dm % 2]
                wob = wocb[dm % 2]
                s.dma("sync", wocf[:, :].rearrange("p (k o) -> p k o", o=128),
                      wo.ap().rearrange("(k p) o -> p k o", p=128)[:, :, dm * 128:(dm + 1) * 128],
                      reads=[wo], writes=[wocf])
                s.op("vector", lambda e, wob=wob: e.tensor_copy(wob[:, :], wocf[:, :]),
                     reads=[wocf], writes=[wob])
                s.dma("scalar", xbb[:, :], xin.ap()[dm * 128:(dm + 1) * 128, qsl],
                      reads=[xin], writes=[xbb])
                for k in range(16):
                    s.op("tensor", lambda e, k=k, pp=pp, wob=wob: e.matmul(
                        pp[:, :], wob[:, k * 128:(k + 1) * 128],
                        y_sb[k][:, :], start=(k == 0), stop=(k == 15)),
                         reads=[wob, y_sb[k]], writes=[pp])
                s.op("vector", lambda e, pp=pp, xbb=xbb: e.tensor_add(
                    x3t[:, :], pp[:, :], xbb[:, :]),
                     reads=[pp, xbb], writes=[x3t])
                s.dma("scalar", x3.ap()[dm * 128:(dm + 1) * 128, qsl], x3t[:, :],
                      reads=[x3t], writes=[x3])
    return b.finish() if standalone else None



def build_l1f_pre(b=None, io=None):
    standalone = b is None
    if standalone:
        b = Build("l1f_pre")
    s = b.s
    if io is None:
        io = {"xin": b.din("xin", [D, T], F32),
              "wqkvT": b.din("wqkvT", [D, 6144], F32),
              "wgT": b.din("wgT", [128, 16 * 128], F32),
              "q_fm": b.dout("q_fm", [D, T], BF16),
              "ksvs": b.dout("ksvs", [T, 257], BF16)}
    xin, wqkvT, wgT = io["xin"], io["wqkvT"], io["wgT"]
    q_fm_o, ksvs = io["q_fm"], io["ksvs"]
    if standalone:
        b.load_consts()
    x_ln = [b.sb(f"xln{ch}", [128, T], BF16) for ch in range(16)]
    with b.scope():
        emit_layernorm(b, xin, x_ln, gamma_beta=None, rho_mode="single")
    ind_all = b.sb("indall", [128, 16 * 16], F32)
    wv_all = b.sb("wvall", [128, 16 * 16], F32)
    with b.scope():
        gwf = b.sb("gwf", [128, 16 * 128], F32)
        gwb = b.sb("gwb", [128, 16 * 128], BF16)
        lg_all = b.sb("lgall", [128, 16 * 16], F32)
        e_all = b.sb("eall", [128, 16 * 16], F32)
        m1a = b.sb("m1a", [128, 16], F32)
        za = b.sb("za", [128, 16], F32)
        rza = b.sb("rza", [128, 16], F32)
        s1a = b.sb("s1a", [128, 16], F32)
        ps_g = b.ps("psg", [128, 128])
        v3 = lambda t, o=16: t[:, :].rearrange("p (g o) -> p g o", o=o)
        s.dma("sync", gwf[:, :], wgT.ap(), reads=[wgT], writes=[gwf])
        s.op("vector", lambda e: e.tensor_copy(gwb[:, :], gwf[:, :]),
             reads=[gwf], writes=[gwb])
        for tt in range(16):
            t0 = tt * 128
            for k in range(16):
                s.op("tensor", lambda e, k=k, t0=t0: e.matmul(
                    ps_g[:, :], x_ln[k][:, t0:t0 + 128], gwb[:, k * 128:(k + 1) * 128],
                    start=(k == 0), stop=(k == 15)),
                     reads=[x_ln[k], gwb], writes=[ps_g])
            s.op("vector", lambda e, tt=tt: e.tensor_copy(
                lg_all[:, tt * 16:(tt + 1) * 16], ps_g[:, 0:16]),
                 reads=[ps_g], writes=[lg_all])
        s.op("scalar", lambda e: e.activation(e_all[:, :], lg_all[:, :], AF.Exp),
             reads=[lg_all], writes=[e_all], short=1200)
        s.op("vector", lambda e: e.reduce_max(v3(m1a, 1), v3(e_all), AX.X),
             reads=[e_all], writes=[m1a])
        s.op("vector", lambda e: e.reduce_sum(v3(za, 1), v3(e_all), AX.X),
             reads=[e_all], writes=[za])
        s.op("vector", lambda e: e.reciprocal(rza[:, :], za[:, :]),
             reads=[za], writes=[rza])
        s.op("vector", lambda e: e.tensor_mul(s1a[:, :], m1a[:, :], rza[:, :]),
             reads=[m1a, rza], writes=[s1a])
        s.op("vector", lambda e: e.tensor_tensor(v3(ind_all), v3(e_all),
                                                 v3(m1a, 1).broadcast_to([128, 16, 16]),
                                                 ALU.is_equal),
             reads=[e_all, m1a], writes=[ind_all], short=700)
        s.op("vector", lambda e: e.tensor_tensor(v3(wv_all), v3(ind_all),
                                                 v3(s1a, 1).broadcast_to([128, 16, 16]),
                                                 ALU.mult),
             reads=[ind_all, s1a], writes=[wv_all], short=700)
    with b.scope():
        ebuf = b.sb("ebuf", [128, 16 * 128], F32)
        ebufb = b.sb("ebufb", [128, 16 * 128], BF16)
        qsc = b.sb("qsc", [128, 512], BF16)
        psq = [b.ps(f"psq{i}", [128, 512]) for i in range(2)]
        for och in range(16):
            s.dma("sync", ebuf[:, :].rearrange("p (k o) -> p k o", o=128),
                  wqkvT.ap().rearrange("(k p) o -> p k o", p=128)[:, :, och * 128:(och + 1) * 128],
                  reads=[wqkvT], writes=[ebuf])
            s.op("vector", lambda e: e.tensor_copy(ebufb[:, :], ebuf[:, :]),
                 reads=[ebuf], writes=[ebufb])
            for stt in range(4):
                sl = slice(stt * 512, stt * 512 + 512)
                pp = psq[stt % 2]
                for k in range(16):
                    s.op("tensor", lambda e, k=k, sl=sl, pp=pp: e.matmul(
                        pp[:, :], ebufb[:, k * 128:(k + 1) * 128], x_ln[k][:, sl],
                        start=(k == 0), stop=(k == 15)),
                         reads=[ebufb, x_ln[k]], writes=[pp])
                s.op("scalar", lambda e, pp=pp: e.mul(qsc[:, :], pp[:, :],
                                                      1.0 / math_sqrt128()),
                     reads=[pp], writes=[qsc])
                s.dma("scalar", q_fm_o.ap()[och * 128:(och + 1) * 128, sl], qsc[:, :],
                      reads=[qsc], writes=[q_fm_o])
    with b.scope():
        wbuf = b.sb("wbuf", [128, 16 * 512], F32)
        wbufb = [b.sb(f"wbufb{i}", [128, 16 * 512], BF16) for i in range(2)]
        ktf = b.sb("ktf", [128, 512], F32)
        ksacc = b.sb("ksacc", [128, 16 * 128], F32)
        vsacc = b.sb("vsacc", [128, 16 * 128], F32)
        kv257 = b.sb("kv257", [128, 257], BF16)
        psk = [b.ps(f"psk{i}", [128, 512]) for i in range(2)]
        for part, wsel in [(0, ind_all), (1, wv_all)]:
            acc = ksacc if part == 0 else vsacc
            for ob in range(4):
                wbb = wbufb[ob % 2]
                col0 = 2048 + part * 2048 + ob * 512
                s.dma("sync", wbuf[:, :].rearrange("p (k o) -> p k o", o=512),
                      wqkvT.ap().rearrange("(k p) o -> p k o", p=128)[:, :, col0:col0 + 512],
                      reads=[wqkvT], writes=[wbuf])
                s.op("vector", lambda e, wbb=wbb: e.tensor_copy(wbb[:, :], wbuf[:, :]),
                     reads=[wbuf], writes=[wbb])
                for tt in range(16):
                    t0 = tt * 128
                    pp = psk[tt % 2]
                    for k in range(16):
                        s.op("tensor", lambda e, k=k, t0=t0, pp=pp, wbb=wbb: e.matmul(
                            pp[:, :], x_ln[k][:, t0:t0 + 128],
                            wbb[:, k * 512:(k + 1) * 512],
                            start=(k == 0), stop=(k == 15)),
                             reads=[x_ln[k], wbb], writes=[pp])
                    s.op("vector", lambda e, pp=pp: e.tensor_copy(ktf[:, :], pp[:, :]),
                         reads=[pp], writes=[ktf])
                    for h4 in range(4):
                        h = ob * 4 + h4
                        scal = wsel[:, tt * 16 + h:tt * 16 + h + 1]
                        if ob == 0 and h4 == 0:
                            s.op("vector", lambda e, h4=h4, tt=tt, scal=scal, acc=acc:
                                 e.tensor_scalar(acc[:, tt * 128:(tt + 1) * 128],
                                                 ktf[:, h4 * 128:(h4 + 1) * 128],
                                                 scal, None, ALU.mult),
                                 reads=[ktf, wsel], writes=[acc])
                        else:
                            s.op("vector", lambda e, h4=h4, tt=tt, scal=scal, acc=acc:
                                 e.scalar_tensor_tensor(
                                     acc[:, tt * 128:(tt + 1) * 128],
                                     ktf[:, h4 * 128:(h4 + 1) * 128], scal,
                                     acc[:, tt * 128:(tt + 1) * 128],
                                     ALU.mult, ALU.add),
                                 reads=[ktf, wsel, acc], writes=[acc])
        for tt in range(16):
            t0 = tt * 128
            s.op("vector", lambda e, tt=tt: e.tensor_copy(
                kv257[:, 0:128], ksacc[:, tt * 128:(tt + 1) * 128]),
                 reads=[ksacc], writes=[kv257])
            s.op("vector", lambda e, tt=tt: e.tensor_copy(
                kv257[:, 128:256], vsacc[:, tt * 128:(tt + 1) * 128]),
                 reads=[vsacc], writes=[kv257])
            s.op("vector", lambda e: e.memset(kv257[:, 256:257], 1.0),
                 reads=[], writes=[kv257])
            s.dma("scalar", ksvs.ap()[t0:t0 + 128, :], kv257[:, :],
                  reads=[kv257], writes=[ksvs])
    return b.finish() if standalone else None


def math_sqrt128():
    return float(np.sqrt(128.0))



def build_l1f_attn(b=None, io=None):
    standalone = b is None
    if standalone:
        b = Build("l1f_attn")
    s = b.s
    if io is None:
        io = {"q_fm": b.din("q_fm", [D, T], BF16),
              "ksvsf": b.din("ksvsf", [S, 257], BF16),
              "xin": b.din("xin", [D, T], F32),
              "bandM": b.din("bandM", [128, 2048], F32),
              "x4": b.dout("x4", [D, T], F32)}
    q_fm_i, ksvsf, xin, bandM, x4 = (io["q_fm"], io["ksvsf"], io["xin"],
                                     io["bandM"], io["x4"])
    kvrow = io.get("kvrow", lambda kt: kt * 128)
    if standalone:
        b.load_consts()
    band = b.sb("band", [128, 2048], F32)
    s.dma("sync", band[:, :], bandM.ap(), reads=[bandM], writes=[band])
    ks_fm = b.sb("ksfm", [128, S], BF16)
    vs_sb = b.sb("vs", [128, 32 * 129], BF16)
    with b.scope():
        kseg = [b.sb(f"kseg{i}", [128, 128], BF16) for i in range(2)]
        pst = b.ps("pst", [128, 128], BF16)
        for kt in range(32):
            sg = kseg[kt % 2]
            s.dma("sync", sg[:, :], ksvsf.ap()[kvrow(kt):kvrow(kt) + 128, 0:128],
                  reads=[ksvsf], writes=[sg])
            s.op("tensor", lambda e, sg=sg: e.transpose(pst[:, :], sg[:, :], b.i128[:, :]),
                 reads=[sg, b.i128], writes=[pst])
            s.op("vector", lambda e, kt=kt: e.tensor_copy(
                ks_fm[:, kt * 128:(kt + 1) * 128], pst[:, :]),
                 reads=[pst], writes=[ks_fm])
            s.dma("scalar", vs_sb[:, kt * 129:(kt + 1) * 129],
                  ksvsf.ap()[kvrow(kt):kvrow(kt) + 128, 128:257],
                  reads=[ksvsf], writes=[vs_sb])
    with b.scope():
        q_sb = b.sb("qsb", [128, 16 * 512], BF16)
        probsT = [b.sb(f"pT{i}", [128, 512], BF16) for i in range(2)]
        sc_f = [b.sb(f"scf{i}", [128, 512], F32) for i in range(2)]
        rz = b.sb("rz", [1, 512], F32)
        zb_sb = b.sb("zbsb", [128, 512], F32)
        y_sb = b.sb("ysb", [128, 16 * 512], F32)
        xb = [b.sb(f"xb{i}", [128, 512], F32) for i in range(2)]
        x4t = b.sb("x4t", [128, 512], F32)
        ps_sc = [b.ps(f"pssc{i}", [128, 512]) for i in range(2)]
        ps_pv = b.ps("pspv", [128, 512])
        ps_z = b.ps("psz", [1, 512])
        ps_bc = b.ps("psbc", [128, 512])
        for qg in range(4):
            nkt = nkt_of(qg)
            qsl = slice(qg * 512, qg * 512 + 512)
            s.dma("sync", q_sb[:, :].rearrange("p (c t) -> p c t", t=512),
                  _rearr_cpT(q_fm_i.ap())[:, :, qsl], reads=[q_fm_i], writes=[q_sb])
            for h in range(16):
                def emit_pv2(kt, nkt=None):
                    pT = probsT[kt % 2]
                    s.op("tensor", lambda e, kt=kt, pT=pT, nkt=nkt: e.matmul(
                        ps_pv[:, :], vs_sb[:, kt * 129:kt * 129 + 128], pT[:, :],
                        start=(kt == 0), stop=(kt == nkt - 1)),
                         reads=[vs_sb, pT], writes=[ps_pv])
                    s.op("tensor", lambda e, kt=kt, pT=pT, nkt=nkt: e.matmul(
                        ps_z[:, :], vs_sb[:, kt * 129 + 128:kt * 129 + 129], pT[:, :],
                        start=(kt == 0), stop=(kt == nkt - 1)),
                         reads=[vs_sb, pT], writes=[ps_z])
                for kt in range(nkt):
                    pa = ps_sc[kt % 2]
                    sf = sc_f[kt % 2]
                    s.op("tensor", lambda e, pa=pa, kt=kt, h=h: e.matmul(
                        pa[:, :], ks_fm[:, kt * 128:(kt + 1) * 128],
                        q_sb[:, h * 512:(h + 1) * 512], start=True, stop=True),
                         reads=[ks_fm, q_sb], writes=[pa])
                    if kt >= 8 * qg:
                        s.op("vector", lambda e, pa=pa, sf=sf, qg=qg, kt=kt:
                             e.scalar_tensor_tensor(ps_view(sf), ps_view(pa), 1.0,
                                                    band_view(band, qg, kt),
                                                    ALU.mult, ALU.add),
                             reads=[pa, band], writes=[sf])
                        esrc = sf
                    else:
                        esrc = pa
                    pT = probsT[kt % 2]
                    s.op("scalar", lambda e, esrc=esrc, pT=pT: e.activation(
                        pT[:, :], esrc[:, :], AF.Exp),
                         reads=[esrc], writes=[pT])
                    if kt > 0:
                        emit_pv2(kt - 1, nkt=nkt)
                emit_pv2(nkt - 1, nkt=nkt)
                s.op("vector", lambda e: e.reciprocal(rz[0:1, :], ps_z[:, :]),
                     reads=[ps_z], writes=[rz])
                s.op("tensor", lambda e: e.matmul(ps_bc[:, :], b.onesr_f[:, :],
                                                  rz[0:1, :], start=True, stop=True),
                     reads=[b.onesr_f, rz], writes=[ps_bc])
                s.op("vector", lambda e: e.tensor_copy(zb_sb[:, :], ps_bc[:, :]),
                     reads=[ps_bc], writes=[zb_sb])
                s.op("vector", lambda e, h=h: e.tensor_mul(
                    y_sb[:, h * 512:(h + 1) * 512], ps_pv[:, :], zb_sb[:, :]),
                     reads=[ps_pv, zb_sb], writes=[y_sb])
            for ch in range(16):
                xbb = xb[ch % 2]
                s.dma("scalar", xbb[:, :], xin.ap()[ch * 128:(ch + 1) * 128, qsl],
                      reads=[xin], writes=[xbb])
                s.op("vector", lambda e, ch=ch, xbb=xbb: e.tensor_add(
                    x4t[:, :], y_sb[:, ch * 512:(ch + 1) * 512], xbb[:, :]),
                     reads=[y_sb, xbb], writes=[x4t])
                s.dma("scalar", x4.ap()[ch * 128:(ch + 1) * 128, qsl], x4t[:, :],
                      reads=[x4t], writes=[x4])
    return b.finish() if standalone else None


def host_rope_tables(c):
    half = c % 2
    pos = np.empty((128, 16), np.int64)
    for tt in range(16):
        pos[:, tt] = (2 * tt + half) * 128 + np.arange(128)
    inv = 1.0 / (10000.0 ** (np.arange(32) / 32.0))
    ang = pos[:, :, None].astype(np.float64) * inv[None, None, :]
    cosv = np.cos(ang); sinv = np.sin(ang)
    ct = np.empty((128, 16, 8, 64), np.float32)
    st = np.empty((128, 16, 8, 64), np.float32)
    ct[:, :, :, 0:32] = cosv[:, :, None, :]
    ct[:, :, :, 32:64] = cosv[:, :, None, :]
    st[:, :, :, 0:32] = -sinv[:, :, None, :]
    st[:, :, :, 32:64] = sinv[:, :, None, :]
    return (ct.reshape(128, 16 * 512).astype(BF16NP),
            st.reshape(128, 16 * 512).astype(BF16NP))


def host_inputs_l1(inputs, x2_arrs=None, resF=None, x3_arrs=None, resH1=None, stage="pre"):
    consts = mkconsts()
    maps = []
    if stage == "pre":
        for c in range(NCORE):
            ct, st = host_rope_tables(c)
            m = {"xin": x2_arrs[c], "wq": inputs["L1_attn_Wq"],
                 "wk": inputs["L1_attn_Wk"], "wv": inputs["L1_attn_Wv"],
                 "ctab": ct, "stab": st}
            m.update(consts)
            maps.append(m)
    elif stage == "attn":
        for (bb, half) in cores_bh():
            c = 2 * bb + half
            kvf = host_kv_global([resF[2 * bb]["kv1"], resF[2 * bb + 1]["kv1"]])
            m = {"q_tm": resF[c]["q_tm"], "kvf": kvf, "wo": inputs["L1_attn_Wo"],
                 "xin": x2_arrs[c], "bandM": np.ascontiguousarray(mk_band(half, "mask")[:, 0:2048])}
            m.update(consts)
            maps.append(m)
    elif stage == "fpre":
        wqkvT = np.ascontiguousarray(inputs["L1_ffn_W_qkv"].T)
        wgT_small = inputs["L1_ffn_W_gate"].T.reshape(16, 128, 16).transpose(1, 0, 2)
        wgT = np.zeros((128, 16, 128), np.float32)
        wgT[:, :, 0:16] = wgT_small
        wgT = np.ascontiguousarray(wgT.reshape(128, 16 * 128))
        for c in range(NCORE):
            m = {"xin": x3_arrs[c], "wqkvT": wqkvT, "wgT": wgT}
            m.update(consts)
            maps.append(m)
    elif stage == "fattn":
        for (bb, half) in cores_bh():
            c = 2 * bb + half
            ksvsf = host_kv_global([resH1[2 * bb]["ksvs"], resH1[2 * bb + 1]["ksvs"]])
            m = {"q_fm": resH1[c]["q_fm"], "ksvsf": ksvsf,
                 "xin": x3_arrs[c],
                 "bandM": np.ascontiguousarray(mk_band(half, "mask")[:, 0:2048])}
            m.update(consts)
            maps.append(m)
    return maps


def core_perm(c):
    half = c % 2
    return np.concatenate([np.arange((2 * j + half) * 128, (2 * j + half) * 128 + 128)
                           for j in range(16)])

def cores_bh():
    return [(c // 2, c % 2) for c in range(NCORE)]


def host_inputs_l0a_pre(inputs):
    x = inputs["L0_attn_x"]
    expT = np.ascontiguousarray(inputs["L0_attn_expand"].T)
    gam = np.ascontiguousarray(inputs["L0_attn_ln_gamma"].reshape(16, 128).T)
    bet = np.ascontiguousarray(inputs["L0_attn_ln_beta"].reshape(16, 128).T)
    consts = mkconsts()
    maps = []
    for (bb, half) in cores_bh():
        m = {"xin": np.ascontiguousarray(x[bb, core_perm(2 * bb + half), :].T),
             "expT": expT, "gam": gam, "bet": bet}
        m.update(consts)
        maps.append(m)
    return maps


def host_kv_global(kv_pair):
    ev, od = kv_pair
    W = ev.shape[1]
    out = np.empty((S, W), ev.dtype)
    out.reshape(32, 128, W)[0::2] = ev.reshape(16, 128, W)
    out.reshape(32, 128, W)[1::2] = od.reshape(16, 128, W)
    return out


def host_inputs_l0a_attn(inputs, resA):
    projT = np.ascontiguousarray(inputs["L0_attn_project"].T)
    x = inputs["L0_attn_x"]
    posm = np.full((1, 128), float(np.asarray(inputs["L0_attn_pos_mult"])), np.float32)
    consts = mkconsts()
    maps = []
    for (bb, half) in cores_bh():
        c = 2 * bb + half
        kvf = host_kv_global([resA[2 * bb]["kv"], resA[2 * bb + 1]["kv"]])
        Rb, Mb = mk_band(half, "alibi")
        m = {"qf": resA[c]["qf"], "kvf": kvf, "gl": resA[c]["gl"],
             "projT": projT,
             "xin": np.ascontiguousarray(x[bb, core_perm(c), :].T),
             "murho": resA[c]["murho"], "bandR": Rb, "bandM": Mb, "posm": posm}
        m.update(consts)
        maps.append(m)
    return maps



def _kvrow_ag(kt):
    return (kt % 2) * T + (kt // 2) * 128


def _cc_pair_gather(b, src, dst):
    b.s.op("gpsimd", lambda e: e.collective_compute(
        "AllGather", ALU.bypass,
        replica_groups=[[0, 1], [2, 3], [4, 5], [6, 7]],
        ins=[src.ap().opt()], outs=[dst.ap().opt()]),
        reads=[src], writes=[dst], kind="cc")


def build_fused():
    b = Build("fused")
    din, dint, dout = b.din, b.dint, b.dout
    xin = din("xin", [D, T], F32)
    expT = din("expT", [D, 8704], F32)
    gam = din("gam", [128, 16], F32)
    bet = din("bet", [128, 16], F32)
    projT = din("projT", [4096, D], F32)
    bandR = din("bandR", [128, BANDW], F32)
    bandM = din("bandM", [128, BANDW], F32)
    posm = din("posm", [1, 128], F32)
    gwT = din("gwT", [128, 16 * 128], F32)
    sgw = din("sgw", [128, 16], F32)
    wg = din("wg", [8 * D, I_MOE], F32)
    wu = din("wu", [8 * D, I_MOE], F32)
    swg = din("swg", [D, IS], F32)
    swu = din("swu", [D, IS], F32)
    wdf = din("wdf", [8 * I_MOE, D], F32)
    swdf = din("swdf", [IS, D], F32)
    wq = din("wq", [D, D], F32)
    wk = din("wk", [D, 512], F32)
    wv = din("wv", [D, 512], F32)
    ctab = din("ctab", [128, 16 * 512], BF16)
    stab = din("stab", [128, 16 * 512], BF16)
    wo = din("wo", [D, D], F32)
    bandM2 = din("bandM2", [128, 2048], F32)
    wqkvT = din("wqkvT", [D, 6144], F32)
    wgT2 = din("wgT2", [128, 16 * 128], F32)
    b.load_consts()
    kv = dint("i_kv", [T, 2304], BF16)
    kvg = dint("i_kvg", [S, 2304], BF16)
    qf = dint("i_qf", [256, T], BF16)
    gl = dint("i_gl", [D, T], BF16)
    murho = dint("i_murho", [2, T], F32)
    x1 = dint("i_x1", [D, T], F32)
    comb = dint("i_comb", [8, T], BF16)
    grow = dint("i_grow", [1, T], F32)
    hh = dint("i_hh", [8 * I_MOE, T], BF16)
    sh = dint("i_sh", [IS, T], BF16)
    wdb = dint("i_wdb", [8 * I_MOE, D], BF16)
    swdb = dint("i_swdb", [IS, D], BF16)
    x2 = dint("i_x2", [D, T], F32)
    q_tm = dint("i_qtm", [T, D], BF16)
    kv1 = dint("i_kv1", [T, 1024], BF16)
    kv1g = dint("i_kv1g", [S, 1024], BF16)
    x3 = dint("i_x3", [D, T], F32)
    q2 = dint("i_q2", [D, T], BF16)
    ksvs = dint("i_ksvs", [T, 257], BF16)
    ksvsg = dint("i_ksvsg", [S, 257], BF16)
    x4 = dout("x4", [D, T], F32)

    with b.scope():
        build_l0a_pre(b, io={"xin": xin, "expT": expT, "gam": gam, "bet": bet,
                             "kv": kv, "qf": qf, "gl": gl, "murho": murho})
    _cc_pair_gather(b, kv, kvg)
    with b.scope():
        build_l0a_attn(b, io={"qf": qf, "kvf": kvg, "gl": gl, "projT": projT,
                              "xin": xin, "murho": murho, "bandR": bandR,
                              "bandM": bandM, "posm": posm, "x1": x1,
                              "kvrow": _kvrow_ag})
    with b.scope():
        xln_sb = [b.sb(f"fxln{ch}", [128, T], BF16) for ch in range(16)]
        with b.scope():
            build_l0f_pre(b, io={"xin": x1, "gwT": gwT, "sgw": sgw,
                                 "comb": comb, "grow": grow, "xln_sb": xln_sb})
        with b.scope():
            build_l0f_hh(b, io={"wg": wg, "wu": wu, "comb": comb, "hh": hh,
                                "xln_sb": xln_sb})
        with b.scope():
            build_l0f_sh(b, io={"swg": swg, "swu": swu, "grow": grow,
                                "wdf": wdf, "swdf": swdf, "sh": sh, "wdb": wdb,
                                "swdb": swdb, "xln_sb": xln_sb})
    with b.scope():
        build_l0f_down(b, io={"hh": hh, "sh": sh, "wdb": wdb, "swdb": swdb,
                              "x1": x1, "x2": x2})
    with b.scope():
        build_l1a_pre(b, io={"xin": x2, "wq": wq, "wk": wk, "wv": wv,
                             "ctab": ctab, "stab": stab, "q_tm": q_tm, "kv1": kv1,
                             "after_kv": lambda: _cc_pair_gather(b, kv1, kv1g)})
    with b.scope():
        build_l1a_attn(b, io={"q_tm": q_tm, "kvf": kv1g, "wo": wo, "xin": x2,
                              "bandM": bandM2, "x3": x3, "kvrow": _kvrow_ag})
    with b.scope():
        build_l1f_pre(b, io={"xin": x3, "wqkvT": wqkvT, "wgT": wgT2,
                             "q_fm": q2, "ksvs": ksvs})
    _cc_pair_gather(b, ksvs, ksvsg)
    with b.scope():
        build_l1f_attn(b, io={"q_fm": q2, "ksvsf": ksvsg, "xin": x3,
                              "bandM": bandM2, "x4": x4, "kvrow": _kvrow_ag})
    return b.finish()


def host_inputs_fused(inputs):
    x = inputs["L0_attn_x"]
    consts = mkconsts()
    expT = np.ascontiguousarray(inputs["L0_attn_expand"].T)
    gam = np.ascontiguousarray(inputs["L0_attn_ln_gamma"].reshape(16, 128).T)
    bet = np.ascontiguousarray(inputs["L0_attn_ln_beta"].reshape(16, 128).T)
    projT = np.ascontiguousarray(inputs["L0_attn_project"].T)
    posm = np.full((1, 128), float(np.asarray(inputs["L0_attn_pos_mult"])), np.float32)
    gwT_small = inputs["L0_ffn_gate_w"].T.reshape(16, 128, 8).transpose(1, 0, 2)
    gwT = np.zeros((128, 16, 128), np.float32)
    gwT[:, :, 0:8] = gwT_small
    gwT = np.ascontiguousarray(gwT.reshape(128, 16 * 128))
    sgw = np.ascontiguousarray(inputs["L0_ffn_shared_gate_w"].reshape(16, 128).T)
    wg_ = np.ascontiguousarray(inputs["L0_ffn_w_gate"].reshape(8 * D, I_MOE))
    wu_ = np.ascontiguousarray(inputs["L0_ffn_w_up"].reshape(8 * D, I_MOE))
    wdf = np.ascontiguousarray(inputs["L0_ffn_w_down"].reshape(8 * I_MOE, D))
    wqkvT = np.ascontiguousarray(inputs["L1_ffn_W_qkv"].T)
    wgT2_small = inputs["L1_ffn_W_gate"].T.reshape(16, 128, 16).transpose(1, 0, 2)
    wgT2 = np.zeros((128, 16, 128), np.float32)
    wgT2[:, :, 0:16] = wgT2_small
    wgT2 = np.ascontiguousarray(wgT2.reshape(128, 16 * 128))
    maps = []
    for (bb, half) in cores_bh():
        c = 2 * bb + half
        Rb, Mb = mk_band(half, "alibi")
        ct, st = host_rope_tables(c)
        m = {"xin": np.ascontiguousarray(x[bb, core_perm(c), :].T),
             "expT": expT, "gam": gam, "bet": bet, "projT": projT,
             "bandR": Rb, "bandM": Mb, "posm": posm,
             "gwT": gwT, "sgw": sgw, "wg": wg_, "wu": wu_,
             "swg": inputs["L0_ffn_sw_gate"], "swu": inputs["L0_ffn_sw_up"],
             "wdf": wdf, "swdf": inputs["L0_ffn_sw_down"],
             "wq": inputs["L1_attn_Wq"], "wk": inputs["L1_attn_Wk"],
             "wv": inputs["L1_attn_Wv"], "ctab": ct, "stab": st,
             "wo": inputs["L1_attn_Wo"],
             "bandM2": np.ascontiguousarray(mk_band(half, "mask")[:, 0:2048]),
             "wqkvT": wqkvT, "wgT2": wgT2}
        m.update(consts)
        maps.append(m)
    return maps


_NC_CACHE = {}

def _get_nc(name, builder):
    if name not in _NC_CACHE:
        _NC_CACHE[name] = builder()
    return _NC_CACHE[name]


def kernel(**inputs):
    nc = _get_nc("fused", build_fused)
    res = run_bass_kernel_spmd(nc, host_inputs_fused(inputs),
                               core_ids=list(range(NCORE))).results
    out = np.empty((B, S, D), np.float32)
    for c in range(NCORE):
        out[c // 2, core_perm(c), :] = np.asarray(res[c]["x4"]).T
    return out


def kernel_unfused(**inputs):
    cores = list(range(NCORE))
    resA = run_bass_kernel_spmd(_get_nc("A", build_l0a_pre),
                                host_inputs_l0a_pre(inputs), core_ids=cores).results
    resB = run_bass_kernel_spmd(_get_nc("B", build_l0a_attn),
                                host_inputs_l0a_attn(inputs, resA), core_ids=cores).results
    x1 = [np.asarray(resB[c]["x1"]) for c in cores]
    resC = run_bass_kernel_spmd(_get_nc("C", build_l0f_pre),
                                host_inputs_l0f(inputs, x1, stage="pre"),
                                core_ids=cores).results
    resD = run_bass_kernel_spmd(_get_nc("D", build_l0f_hh),
                                host_inputs_l0f(inputs, x1, resC=resC, stage="hh"),
                                core_ids=cores).results
    resD2 = run_bass_kernel_spmd(_get_nc("D2", build_l0f_sh),
                                 host_inputs_l0f(inputs, x1, resC=resC, stage="sh"),
                                 core_ids=cores).results
    resE = run_bass_kernel_spmd(_get_nc("E", build_l0f_down),
                                host_inputs_l0f(inputs, x1, resD=resD, resD2=resD2,
                                                stage="down"),
                                core_ids=cores).results
    x2 = [np.asarray(resE[c]["x2"]) for c in cores]
    resF = run_bass_kernel_spmd(_get_nc("F", build_l1a_pre),
                                host_inputs_l1(inputs, x2_arrs=x2, stage="pre"),
                                core_ids=cores).results
    resG = run_bass_kernel_spmd(_get_nc("G", build_l1a_attn),
                                host_inputs_l1(inputs, x2_arrs=x2, resF=resF, stage="attn"),
                                core_ids=cores).results
    x3 = [np.asarray(resG[c]["x3"]) for c in cores]
    resH1 = run_bass_kernel_spmd(_get_nc("H1", build_l1f_pre),
                                 host_inputs_l1(inputs, x3_arrs=x3, stage="fpre"),
                                 core_ids=cores).results
    resH2 = run_bass_kernel_spmd(_get_nc("H2", build_l1f_attn),
                                 host_inputs_l1(inputs, x3_arrs=x3, resH1=resH1,
                                                stage="fattn"),
                                 core_ids=cores).results
    out = np.empty((B, S, D), np.float32)
    for c in cores:
        bb = c // 2
        out[bb, core_perm(c), :] = np.asarray(resH2[c]["x4"]).T
    return out
